# version 08 (final)
# baseline (speedup 1.0000x reference)
import math
_DBG_SPILL = False
import numpy as np
import ml_dtypes
from contextlib import ExitStack

import concourse.bass as bass
import concourse.mybir as mybir
from concourse.bass_utils import run_bass_kernel_spmd

F32 = mybir.dt.float32
BF16 = mybir.dt.bfloat16
AF = mybir.ActivationFunctionType
ALU = mybir.AluOpType
BB = ml_dtypes.bfloat16

NCORES = 8
T = 8192
TC = T // NCORES
D = 2048
NFC = D // 128
HD = 128
NHEADS_Q = 16
HPC = 2
MOE_I = 1408
NIC = MOE_I // 128
D_INNER = 4096
CONV_DIM = D_INNER + 256
NCC = CONV_DIM // 128
MCH = 128
NCHUNK = TC // MCH
QK_SCALE = 128.0 ** -0.25
NQT = T // 512


def _bf(a):
    return np.ascontiguousarray(np.asarray(a)).astype(BB)


def _f32(a):
    return np.ascontiguousarray(np.asarray(a), dtype=np.float32)



def prep_host(ins):
    x = _f32(ins["L0_attn_x"][0])
    x_fm = np.ascontiguousarray(x.T)

    w_aq = np.asarray(ins["L0_attn_w_aq"])
    w_ak = np.asarray(ins["L0_attn_w_ak"])
    w_av = np.asarray(ins["L0_attn_w_av"])
    w_ao = np.asarray(ins["L0_attn_w_ao"])
    wo_flat = _bf(w_ao.reshape(NHEADS_Q * HD, D))

    half = HD // 2
    pos = np.arange(T, dtype=np.float32)[:, None]
    freqs = (10000.0 ** (-np.arange(half, dtype=np.float32) / half))[None, :]
    ang = pos * freqs
    cos_t = _bf((np.cos(ang) * QK_SCALE).T)
    sin_t = _bf((np.sin(ang) * QK_SCALE).T)

    masks = np.zeros((4, 128, 512), dtype=np.float32)
    tkk = np.arange(128)[:, None]
    tqq = np.arange(512)[None, :]
    for dd in range(4):
        masks[dd] = np.where(tqq >= tkk + dd * 128, 0.0, -1e30)

    sel8 = np.zeros((8, 8 * 128), dtype=np.float32)
    for e in range(8):
        sel8[e, e * 128:(e + 1) * 128] = 1.0

    tril = np.where(np.arange(128)[None, :] >= np.arange(128)[:, None], 1.0, 0.0
                    ).astype(np.float32)

    shared = dict(
        wo=wo_flat, cos_t=cos_t, sin_t=sin_t, masks=masks, sel8=_bf(sel8),
        ident=np.eye(128, dtype=np.float32), ident_bf=_bf(np.eye(128)),
        tril=tril,
        rT=_bf(np.asarray(ins["L0_ffn_router_w"]).T),
        sgT=_bf(np.asarray(ins["L0_ffn_shared_gate"]).T),
        suT=_bf(np.asarray(ins["L0_ffn_shared_up"]).T),
        sdT=_bf(np.asarray(ins["L0_ffn_shared_down"]).T),
        wgT=_bf(np.transpose(np.asarray(ins["L0_ffn_w_gate"]), (0, 2, 1))),
        wuT=_bf(np.transpose(np.asarray(ins["L0_ffn_w_up"]), (0, 2, 1))),
        wdT=_bf(np.transpose(np.asarray(ins["L0_ffn_w_down"]), (0, 2, 1))),
    )

    w_inproj = np.asarray(ins["L1_attn_in_proj_kernel"])
    shared.update(
        w_xbc=_bf(w_inproj[:, D_INNER:D_INNER + CONV_DIM]),
        w_dt=_bf(w_inproj[:, D_INNER + CONV_DIM:]),
        w_z=_bf(w_inproj[:, :D_INNER]),
        w_out=_bf(np.asarray(ins["L1_attn_out_proj_kernel"])),
        conv_w=_f32(np.asarray(ins["L1_attn_conv1d_weight"]).reshape(NCC, 128, 4).transpose(1, 0, 2).reshape(128, NCC * 4)),
        conv_b=_f32(np.asarray(ins["L1_attn_conv1d_bias"]).reshape(NCC, 128).T),
        dt_bias=_f32(np.asarray(ins["L1_attn_dt_bias"]).reshape(64, 1)),
        A_log=_f32(np.asarray(ins["L1_attn_A_log"]).reshape(64, 1)),
        D_row=_f32(np.asarray(ins["L1_attn_D"]).reshape(1, 64)),
        nw_row=_f32(np.asarray(ins["L1_attn_norm_weight"]).reshape(1, D_INNER)),
        factors=_f32(np.asarray(ins["L1_ffn_factors"])),
        l1_bias=_f32(np.asarray(ins["L1_ffn_bias"]).reshape(NFC, 128).T),
    )

    in_maps = []
    for c in range(NCORES):
        h0 = HPC * c
        g = h0 // 4
        wq = np.stack([_bf(w_aq[(h0 + j) // 4, (h0 + j) % 4]) for j in range(HPC)])
        m_pref = np.zeros((128, 7), dtype=np.float32)
        m_pref[:, :c] = 1.0
        w_halo = np.zeros((128, 8), dtype=np.float32)
        if c > 0:
            w_halo[:, c - 1] = 1.0
        im = dict(shared)
        im.update(
            x_fm=np.ascontiguousarray(x_fm[:, c * TC:(c + 1) * TC]),
            wq=wq, wk=_bf(w_ak[g]), wv=_bf(w_av[g]),
            m_pref=m_pref, w_halo=w_halo,
            tok_off=np.array([[c * TC]], dtype=np.int32),
        )
        in_maps.append(im)
    return in_maps



class SemC:
    def __init__(self, es, nc, name):
        self.h = es.enter_context(nc.semaphore(name))
        self.n = 0

    def inc(self, inst, k=1):
        inst.then_inc(self.h, k)
        self.n += k
        return self.n


class B:
    def __init__(self):
        self.nc = bass.Bass("TRN2", target_bir_lowering=False)
        self.es = ExitStack()
        self._uid = 0
        self._names = set()

    def uid(self):
        self._uid += 1
        return self._uid

    def _uniq(self, name):
        if name in self._names:
            name = f"{name}_{self.uid()}"
        self._names.add(name)
        return name

    def sem(self, name):
        return SemC(self.es, self.nc, self._uniq("sem_" + name))

    def sbuf(self, name, shape, dtype, es=None):
        return (es or self.es).enter_context(self.nc.sbuf_tensor(self._uniq(name), list(shape), dtype))

    def psum(self, name, shape, dtype=F32, es=None):
        return (es or self.es).enter_context(self.nc.psum_tensor(self._uniq(name), list(shape), dtype))

    def din(self, name, shape, dtype):
        return self.nc.dram_tensor(name, list(shape), dtype, kind="ExternalInput").ap()

    def dout(self, name, shape, dtype):
        return self.nc.dram_tensor(name, list(shape), dtype, kind="ExternalOutput").ap()

    def dram(self, name, shape, dtype):
        return self.nc.dram_tensor(name, list(shape), dtype)


class Sched:
    ENGS = ("tensor", "vector", "scalar", "gpsimd", "sync")

    def __init__(self, b, tag):
        self.b = b
        self.sems = {e: b.sem(f"{tag}_{e}") for e in self.ENGS}
        self.sems["coll"] = b.sem(f"{tag}_coll")
        self.steps = {e: [] for e in self.ENGS}

    def up(self, eng):
        return (self.sems[eng].h, self.sems[eng].n)

    def add(self, eng, fn, waits=(), amt=1):
        lane = "gpsimd" if eng == "coll" else eng
        s = self.sems[eng]
        s.n += amt
        self.steps[lane].append((fn, list(waits), amt, s))
        return (s.h, s.n)

    def emit(self, block):
        for eng in self.ENGS:
            if not self.steps[eng]:
                continue
            steps = self.steps[eng]

            def body(e, steps=steps):
                for fn, waits, amt, s in steps:
                    for (h, v) in waits:
                        if v > 0:
                            e.wait_ge(h, v)
                    inst = fn(e)
                    inst.then_inc(s.h, amt)
            getattr(block, eng)(body)


def fm_dram(ap_2d):
    return ap_2d.rearrange("(c p) t -> p c t", p=128)


def fm_view(sb_ap, nfc):
    return sb_ap.rearrange("p (c t) -> p c t", c=nfc)


def fm_sbuf_to_dram(sb_ap, nfc):
    return sb_ap.rearrange("p (c t) -> p c t", c=nfc)



def emit_ln_fm(b, block, *, xs, xn, tokens, scratch, stats,
               ones_col, ones_row, eps_ap, ready, done):
    nc = b.nc
    u = b.uid()
    ntt = tokens // 512
    ps = ExitStack()
    st_ps = ps.enter_context(nc.psum_tensor(f"ln{u}_st", [1, tokens], F32))
    sq_ps = ps.enter_context(nc.psum_tensor(f"ln{u}_sq", [1, tokens], F32))
    mb_ps = ps.enter_context(nc.psum_tensor(f"ln{u}_mb", [128, 512], F32))
    rb_ps = ps.enter_context(nc.psum_tensor(f"ln{u}_rb", [128, 512], F32))
    sq = b.sem(f"ln{u}_sqs")
    mm = b.sem(f"ln{u}_mm")
    sts = b.sem(f"ln{u}_st")
    bc = b.sem(f"ln{u}_bc")
    ap_s = b.sem(f"ln{u}_ap")

    @block.scalar
    def _(scalar):
        for rh, rv in ready:
            scalar.wait_ge(rh, rv)
        for fc in range(NFC):
            if fc >= 2:
                scalar.wait_ge(mm.h, fc - 1)
            sq.inc(scalar.activation(scratch[:, (fc % 2) * tokens:(fc % 2 + 1) * tokens],
                                     xs[:, fc * tokens:(fc + 1) * tokens], AF.Square))

    @block.tensor
    def _(tensor):
        for rh, rv in ready:
            tensor.wait_ge(rh, rv)
        for fc in range(NFC):
            tensor.wait_ge(sq.h, fc + 1)
            for tt in range(ntt):
                sl = slice(tt * 512, (tt + 1) * 512)
                tensor.matmul(st_ps[0:1, sl], ones_col[:, :],
                              xs[:, fc * tokens + tt * 512: fc * tokens + (tt + 1) * 512],
                              start=(fc == 0), stop=(fc == NFC - 1))
                last = tensor.matmul(sq_ps[0:1, sl], ones_col[:, :],
                                     scratch[:, (fc % 2) * tokens + tt * 512:(fc % 2) * tokens + (tt + 1) * 512],
                                     start=(fc == 0), stop=(fc == NFC - 1))
            mm.inc(last)

    mean = stats[0:1, 0:tokens]
    rstd = stats[0:1, tokens:2 * tokens]
    tmp1 = stats[0:1, 2 * tokens:3 * tokens]

    @block.vector
    def _(vector):
        vector.wait_ge(mm.h, NFC)
        vector.tensor_scalar(mean, st_ps[0:1, 0:tokens], 1.0 / D, None, ALU.mult)
        vector.tensor_scalar(tmp1, sq_ps[0:1, 0:tokens], 1.0 / D, None, ALU.mult)
        vector.tensor_tensor(rstd, mean, mean, ALU.mult)
        sts.inc(vector.tensor_tensor(tmp1, tmp1, rstd, ALU.subtract))

    @block.scalar
    def _(scalar):
        scalar.wait_ge(sts.h, 1)
        sts.inc(scalar.activation(tmp1, tmp1, AF.Sqrt, bias=eps_ap))

    @block.vector
    def _(vector):
        vector.wait_ge(sts.h, 2)
        sts.inc(vector.reciprocal(rstd, tmp1))

    @block.tensor
    def _(tensor):
        tensor.wait_ge(sts.h, 3)
        for tt in range(ntt):
            sl = slice(tt * 512, (tt + 1) * 512)
            if tt > 0:
                tensor.wait_ge(ap_s.h, tt)
            tensor.matmul(mb_ps[:, :], ones_row[:, :], mean[:, sl], start=True, stop=True)
            bc.inc(tensor.matmul(rb_ps[:, :], ones_row[:, :], rstd[:, sl], start=True, stop=True))

    @block.vector
    def _(vector):
        for tt in range(ntt):
            vector.wait_ge(bc.h, tt + 1)
            for fc in range(NFC):
                xsl = slice(fc * tokens + tt * 512, fc * tokens + (tt + 1) * 512)
                vector.tensor_tensor(scratch[:, 0:512], xs[:, xsl], mb_ps[:, :], ALU.subtract)
                last = vector.tensor_tensor(xn[:, xsl], scratch[:, 0:512], rb_ps[:, :], ALU.mult)
            ap_s.inc(last)
        done.inc(vector.tensor_copy(stats[0:1, 0:1], stats[0:1, 0:1]))
    ps.close()



def build_attn():
    b = B()
    nc = b.nc
    x_in = b.din("x_fm", [D, TC], F32)
    wq_in = b.din("wq", [HPC, D, 128], BF16)
    wk_in = b.din("wk", [D, 128], BF16)
    wv_in = b.din("wv", [D, 128], BF16)
    wo_in = b.din("wo", [NHEADS_Q * HD, D], BF16)
    cos_in = b.din("cos_t", [64, T], BF16)
    sin_in = b.din("sin_t", [64, T], BF16)
    masks_in = b.din("masks", [4, 128, 512], F32)
    ident_in = b.din("ident", [128, 128], F32)
    identb_in = b.din("ident_bf", [128, 128], BF16)

    xn_out = b.dout("xn_out", [D, TC], BF16)
    attn_out = b.dout("attn_out", [D, TC], F32)

    ag_in = b.dram("xn_ag_in", [D, TC], BF16)
    ag_out = b.dram("xn_ag_out", [NCORES * D, TC], BF16)
    oag_in = b.dram("o_ag_in", [HPC * HD, T], BF16)
    oag_out = b.dram("o_ag_out", [NHEADS_Q * HD, T], BF16)

    es = b.es
    with es:
        block = es.enter_context(nc.Block())
        ld = b.sem("ld")
        st = b.sem("st")
        cc = b.sem("cc")

        ones_col = b.sbuf("ones_col", [128, 1], F32)
        ones_row = b.sbuf("ones_row", [1, 128], F32)
        ones_bf = b.sbuf("ones_bf", [128, 128], BF16)
        cst = b.sbuf("cst", [128, 4], F32)
        ident_f = b.sbuf("ident_f", [128, 128], F32)
        ident_b = b.sbuf("ident_b", [128, 128], BF16)
        cset = b.sem("cset")

        @block.vector
        def _(vector):
            vector.memset(ones_col[:, :], 1.0)
            vector.memset(ones_row[:, :], 1.0)
            vector.memset(cst[:, 0:1], 1e-5)
            cset.inc(vector.memset(ones_bf[:, :], 1.0))

        sc1 = ExitStack()
        xs = b.sbuf("xs", [128, NFC * TC], F32, es=sc1)
        xn = b.sbuf("xn", [128, NFC * TC], BF16, es=sc1)
        scratch = b.sbuf("lnscr", [128, 2 * TC], F32, es=sc1)
        stats = b.sbuf("lnstats", [1, 3 * TC], F32, es=sc1)

        @block.sync
        def _(sync):
            ld.inc(sync.dma_start(out=fm_view(xs[:, :], NFC), in_=fm_dram(x_in)), 16)
            ld.inc(sync.dma_start(out=ident_f[:, :], in_=ident_in), 16)
            ld.inc(sync.dma_start(out=ident_b[:, :], in_=identb_in), 16)

        ln_done = b.sem("ln_done")
        emit_ln_fm(b, block, xs=xs, xn=xn, tokens=TC, scratch=scratch, stats=stats,
                   ones_col=ones_col, ones_row=ones_row, eps_ap=cst[0:1, 0:1],
                   ready=[(ld.h, 48), (cset.h, 1)], done=ln_done)

        @block.gpsimd
        def _(g):
            g.wait_ge(ln_done.h, 1)
            st.inc(g.dma_start(out=fm_dram(xn_out), in_=fm_sbuf_to_dram(xn[:, :], NFC)), 16)
            st.inc(g.dma_start(out=fm_dram(ag_in.ap()), in_=fm_sbuf_to_dram(xn[:, :], NFC)), 16)
            g.wait_ge(st.h, 32)
            cc.inc(g.collective_compute(
                "AllGather", ALU.bypass, replica_groups=[list(range(NCORES))],
                ins=[ag_in.ap().opt()], outs=[ag_out.ap().opt()]))

        sc1.close()

        q_s = b.sbuf("q_s", [128, HPC * T], BF16)
        k_s = b.sbuf("k_s", [128, T], BF16)
        v_tm = b.sbuf("v_tm", [128, T], BF16)
        sc2b = ExitStack()
        wq_s = b.sbuf("wq_s", [128, HPC * NFC * 128], BF16, es=sc2b)
        wk_s = b.sbuf("wk_s", [128, NFC * 128], BF16, es=sc2b)
        wv_s = b.sbuf("wv_s", [128, NFC * 128], BF16, es=sc2b)
        cos_s = b.sbuf("cos_s", [64, T], BF16, es=sc2b)
        sin_s = b.sbuf("sin_s", [64, T], BF16, es=sc2b)

        @block.sync
        def _(sync):
            sync.wait_ge(st.h, 32)
            ld.inc(sync.dma_start(out=wq_s[:, :].rearrange("p (j c d) -> p j c d", j=HPC, c=NFC), in_=wq_in.rearrange("j (c p) d -> p j c d", p=128)), 16)
            ld.inc(sync.dma_start(out=fm_view(wk_s[:, :], NFC), in_=fm_dram(wk_in)), 16)
            ld.inc(sync.dma_start(out=fm_view(wv_s[:, :], NFC), in_=fm_dram(wv_in)), 16)
            ld.inc(sync.dma_start(out=cos_s[:, :], in_=cos_in), 16)
            ld.inc(sync.dma_start(out=sin_s[:, :], in_=sin_in), 16)
        w_loaded = ld.n

        xt = b.sbuf("xt", [128, 2 * NFC * 512], BF16, es=sc2b)
        rsc = b.sbuf("rsc", [64, 2 * 512], F32, es=sc2b)
        vfm = b.sbuf("vfm", [128, 2 * 512], BF16, es=sc2b)

        q_ps0 = b.psum("q_ps0", [128, 512], F32, es=sc2b)
        q_ps1 = b.psum("q_ps1", [128, 512], F32, es=sc2b)
        k_ps = b.psum("k_ps", [128, 512], F32, es=sc2b)
        v_ps = b.psum("v_ps", [128, 512], F32, es=sc2b)
        t_ps = b.psum("t_ps", [128, 128], BF16, es=sc2b)

        ldq = b.sem("ldq")
        mmk = b.sem("mmk")
        rp = b.sem("rp")
        tvt = b.sem("tvt")
        tve = b.sem("tve")

        @block.sync
        def _(sync):
            sync.wait_ge(cc.h, 1)
            for tt in range(NQT):
                buf = tt % 2
                if tt >= 2:
                    sync.wait_ge(rp.h, 4 * (tt - 1))
                src = ag_out.ap()[D * (tt // 2):D * (tt // 2 + 1),
                                  (tt % 2) * 512:(tt % 2) * 512 + 512]
                ldq.inc(sync.dma_start(out=fm_view(xt[:, buf * NFC * 512:(buf + 1) * NFC * 512], NFC),
                                       in_=fm_dram(src)), 16)

        @block.tensor
        def _(tensor):
            tensor.wait_ge(ld.h, w_loaded)
            for tt in range(NQT):
                buf = tt % 2
                xsl = lambda fc: xt[:, (buf * NFC + fc) * 512:(buf * NFC + fc + 1) * 512]
                tensor.wait_ge(ldq.h, 16 * (tt + 1))
                if tt >= 1:
                    tensor.wait_ge(rp.h, 4 * tt)
                for jps, wsl in ((q_ps0, 0), (q_ps1, 1)):
                    for fc in range(NFC):
                        mm = tensor.matmul(jps[:, :],
                                           wq_s[:, (wsl * NFC + fc) * 128:(wsl * NFC + fc + 1) * 128],
                                           xsl(fc), start=(fc == 0), stop=(fc == NFC - 1))
                    mmk.inc(mm)
                for fc in range(NFC):
                    mm = tensor.matmul(k_ps[:, :], wk_s[:, fc * 128:(fc + 1) * 128],
                                       xsl(fc), start=(fc == 0), stop=(fc == NFC - 1))
                mmk.inc(mm)
                for fc in range(NFC):
                    mm = tensor.matmul(v_ps[:, :], wv_s[:, fc * 128:(fc + 1) * 128],
                                       xsl(fc), start=(fc == 0), stop=(fc == NFC - 1))
                mmk.inc(mm)
                tts = [tt - 1] if tt >= 1 else []
                if tt == NQT - 1:
                    tts.append(tt)
                for tp in tts:
                    tensor.wait_ge(rp.h, 4 * tp + 4)
                    for i in range(4):
                        idx = tp * 4 + i
                        if idx >= 1:
                            tensor.wait_ge(tve.h, idx)
                        tvt.inc(tensor.transpose(t_ps[:, :],
                                                 vfm[:, (tp % 2) * 512 + i * 128:(tp % 2) * 512 + (i + 1) * 128],
                                                 ident_b[:, :]))

        def rope_emit(vector, ps, dst, tt):
            c_sl = cos_s[:, tt * 512:(tt + 1) * 512]
            s_sl = sin_s[:, tt * 512:(tt + 1) * 512]
            e, o = ps[0:64, :], ps[64:128, :]
            t1 = rsc[:, 0 * 512:1 * 512]
            t2 = rsc[:, 1 * 512:2 * 512]
            vector.tensor_tensor(t1, e, c_sl, ALU.mult)
            vector.tensor_tensor(t2, o, s_sl, ALU.mult)
            vector.tensor_tensor(dst[0:64, :], t1, t2, ALU.subtract)
            vector.tensor_tensor(t1, e, s_sl, ALU.mult)
            vector.tensor_tensor(t2, o, c_sl, ALU.mult)
            return vector.tensor_tensor(dst[64:128, :], t1, t2, ALU.add)

        @block.vector
        def _(vector):
            for tt in range(NQT):
                buf = tt % 2
                base = 4 * tt
                vector.wait_ge(mmk.h, base + 1)
                rp.inc(rope_emit(vector, q_ps0, q_s[:, 0 * T + tt * 512:0 * T + tt * 512 + 512], tt))
                vector.wait_ge(mmk.h, base + 2)
                rp.inc(rope_emit(vector, q_ps1, q_s[:, 1 * T + tt * 512:1 * T + tt * 512 + 512], tt))
                vector.wait_ge(mmk.h, base + 3)
                rp.inc(rope_emit(vector, k_ps, k_s[:, tt * 512:(tt + 1) * 512], tt))
                vector.wait_ge(mmk.h, base + 4)
                if tt >= 2:
                    vector.wait_ge(tvt.h, 4 * (tt - 1))
                rp.inc(vector.tensor_copy(vfm[:, buf * 512:(buf + 1) * 512], v_ps[:, :]))
                tps = [tt - 1] if tt >= 1 else []
                if tt == NQT - 1:
                    tps.append(tt)
                for tp in tps:
                    for i in range(4):
                        idx = tp * 4 + i
                        vector.wait_ge(tvt.h, idx + 1)
                        tve.inc(vector.tensor_copy(v_tm[:, idx * 128:(idx + 1) * 128], t_ps[:, :]))

        sc2b.close()

        p_s = b.sbuf("p_s", [128, 2 * 512], BF16)
        mask_s = b.sbuf("mask_s", [128, 4 * 512], F32)
        o_stage = b.sbuf("o_stage", [128, 2 * 512], BF16)
        nrm_s = b.sbuf("nrm_s", [128, 512], F32)

        @block.sync
        def _(sync):
            sync.wait_ge(tve.h, 4 * NQT)
            ld.inc(sync.dma_start(out=mask_s[:, :].rearrange("p (d t) -> p d t", d=4), in_=masks_in.rearrange("d p t -> p d t")), 16)
        mask_ld = ld.n

        s_ps = [b.psum("s_ps0", [128, 512], F32), b.psum("s_ps1", [128, 512], F32)]
        o_ps = b.psum("o_ps", [128, 512], F32)
        n_ps = b.psum("n_ps", [128, 512], F32)

        sS = b.sem("sS")
        sM = b.sem("sM")
        sP = b.sem("sP")
        sC = b.sem("sC")
        sE = b.sem("sE")
        sO = b.sem("sO")

        jobs = []
        for j in range(HPC):
            for qt in range(NQT):
                jobs.append((j, qt, 4 * qt + 4, 4 * qt))
        iters = []
        for jb, (j, qt, nkc, diag0) in enumerate(jobs):
            for kc in range(nkc):
                iters.append(dict(jb=jb, j=j, qt=qt, kc=kc, first=(kc == 0),
                                  last=(kc == nkc - 1), diag=(kc >= diag0),
                                  delta=kc - diag0))
        nd_cum = np.cumsum([1 if it["diag"] else 0 for it in iters]).tolist()
        qkv_rp = 4 * NQT
        qkv_tve = 4 * NQT

        @block.tensor
        def _(tensor):
            tensor.wait_ge(rp.h, qkv_rp)
            tensor.wait_ge(tve.h, qkv_tve)
            tensor.wait_ge(ld.h, mask_ld)
            for i, it in enumerate(iters):
                q_sl = q_s[:, it["j"] * T + it["qt"] * 512: it["j"] * T + it["qt"] * 512 + 512]
                if i >= 2:
                    tensor.wait_ge(sP.h, i - 1)
                sS.inc(tensor.matmul(s_ps[i % 2][:, :],
                                     k_s[:, it["kc"] * 128:(it["kc"] + 1) * 128],
                                     q_sl, start=True, stop=True))
                if i >= 1:
                    pit = iters[i - 1]
                    if pit["first"]:
                        tensor.wait_ge(sE.h, pit["jb"])
                    tensor.wait_ge(sP.h, i)
                    tensor.matmul(o_ps[:, :], v_tm[:, pit["kc"] * 128:(pit["kc"] + 1) * 128],
                                  p_s[:, ((i - 1) % 2) * 512:((i - 1) % 2 + 1) * 512],
                                  start=pit["first"], stop=pit["last"])
                    sC.inc(tensor.matmul(n_ps[:, :], ones_bf[:, :],
                                         p_s[:, ((i - 1) % 2) * 512:((i - 1) % 2 + 1) * 512],
                                         start=pit["first"], stop=pit["last"]))
            i = len(iters)
            pit = iters[i - 1]
            if pit["first"]:
                tensor.wait_ge(sE.h, pit["jb"])
            tensor.wait_ge(sP.h, i)
            tensor.matmul(o_ps[:, :], v_tm[:, pit["kc"] * 128:(pit["kc"] + 1) * 128],
                          p_s[:, ((i - 1) % 2) * 512:((i - 1) % 2 + 1) * 512],
                          start=pit["first"], stop=pit["last"])
            sC.inc(tensor.matmul(n_ps[:, :], ones_bf[:, :],
                                 p_s[:, ((i - 1) % 2) * 512:((i - 1) % 2 + 1) * 512],
                                 start=pit["first"], stop=pit["last"]))

        @block.vector
        def _(vector):
            i = 0
            for jb, (j, qt, nkc, diag0) in enumerate(jobs):
                for kc in range(nkc):
                    it = iters[i]
                    if it["diag"]:
                        vector.wait_ge(sS.h, i + 1)
                        sM.inc(vector.tensor_tensor(
                            s_ps[i % 2][:, :], s_ps[i % 2][:, :],
                            mask_s[:, it["delta"] * 512:(it["delta"] + 1) * 512], ALU.add))
                    i += 1
                vector.wait_ge(sC.h, i)
                if jb >= 2:
                    vector.wait_ge(sO.h, 16 * (jb - 1))
                vector.reciprocal(nrm_s[:, :], n_ps[:, :])
                sE.inc(vector.tensor_tensor(o_stage[:, (jb % 2) * 512:(jb % 2 + 1) * 512],
                                            o_ps[:, :], nrm_s[:, :], ALU.mult))

        @block.scalar
        def _(scalar):
            for i, it in enumerate(iters):
                if it["diag"]:
                    scalar.wait_ge(sM.h, nd_cum[i])
                else:
                    scalar.wait_ge(sS.h, i + 1)
                if i >= 2:
                    scalar.wait_ge(sC.h, i - 1)
                sP.inc(scalar.activation(p_s[:, (i % 2) * 512:(i % 2 + 1) * 512],
                                         s_ps[i % 2][:, :], AF.Exp))

        @block.gpsimd
        def _(g):
            for jb, (j, qt, nkc, diag0) in enumerate(jobs):
                g.wait_ge(sE.h, jb + 1)
                sO.inc(g.dma_start(out=oag_in.ap()[j * 128:(j + 1) * 128, qt * 512:(qt + 1) * 512],
                                   in_=o_stage[:, (jb % 2) * 512:(jb % 2 + 1) * 512]), 16)
            g.wait_ge(sO.h, 16 * len(jobs))
            cc.inc(g.collective_compute(
                "AllGather", ALU.bypass, replica_groups=[list(range(NCORES))],
                ins=[oag_in.ap().opt()], outs=[oag_out.ap().opt()]))

        tok_in = b.din("tok_off", [1, 1], mybir.dt.int32)
        oa_s = b.sbuf("oa_s", [128, 2 * NFC * 512], BF16)
        wo_s = b.sbuf("wo_s", [128, 2 * NFC * 128], BF16)
        r_stage = b.sbuf("r_stage", [128, 2 * 512], F32)
        stok = b.sbuf("stok", [1, 1], mybir.dt.int32)

        r_ps = [b.psum("r_ps0", [128, 512], F32), b.psum("r_ps1", [128, 512], F32)]
        lo = b.sem("lo")
        lw = b.sem("lw")
        mo = b.sem("mo")
        re = b.sem("re")
        ro = b.sem("ro")

        @block.sync
        def _(sync):
            ld.inc(sync.dma_start(out=stok[:, :], in_=tok_in), 16)
            for oc in range(NFC):
                if oc >= 2:
                    sync.wait_ge(mo.h, 2 * (oc - 1))
                lw.inc(sync.dma_start(
                    out=fm_view(wo_s[:, (oc % 2) * NFC * 128:(oc % 2 + 1) * NFC * 128], NFC),
                    in_=fm_dram(wo_in[:, oc * 128:(oc + 1) * 128])), 16)
        tok_ld = ld.n

        @block.gpsimd
        def _(g):
            g.wait_ge(cc.h, 2)
            g.wait_ge(ld.h, tok_ld)
            with g.register("r_tok") as r:
                g.reg_load(r, stok[0:1, 0:1])
                for tt2 in range(2):
                    ofs = g.snap(r, min_val=0, max_val=T - 512)
                    src = oag_out.ap().rearrange("(h p) t -> p h t", p=128)[:, :, bass.ds(ofs, 512)]
                    dst = oa_s[:, tt2 * NFC * 512:(tt2 + 1) * NFC * 512].rearrange("p (h t) -> p h t", h=NFC)
                    lo.inc(g.dma_start(out=dst, in_=src), 16)
                    if tt2 == 0:
                        g.reg_add(r, r, 512)

        @block.tensor
        def _(tensor):
            tensor.wait_ge(lo.h, 32)
            for oc in range(NFC):
                tensor.wait_ge(lw.h, 16 * (oc + 1))
                for tt2 in range(2):
                    if oc >= 1:
                        tensor.wait_ge(re.h, 2 * (oc - 1) + tt2 + 1)
                    for hd in range(NFC):
                        mm = tensor.matmul(
                            r_ps[tt2][:, :],
                            wo_s[:, ((oc % 2) * NFC + hd) * 128:((oc % 2) * NFC + hd + 1) * 128],
                            oa_s[:, (tt2 * NFC + hd) * 512:(tt2 * NFC + hd + 1) * 512],
                            start=(hd == 0), stop=(hd == NFC - 1))
                    mo.inc(mm)

        @block.vector
        def _(vector):
            for oc in range(NFC):
                for tt2 in range(2):
                    vector.wait_ge(mo.h, 2 * oc + tt2 + 1)
                    if oc >= 1:
                        vector.wait_ge(ro.h, 16 * (2 * (oc - 1) + tt2 + 1))
                    re.inc(vector.tensor_copy(r_stage[:, tt2 * 512:(tt2 + 1) * 512],
                                              r_ps[tt2][:, :]))

        @block.gpsimd
        def _(g):
            for oc in range(NFC):
                for tt2 in range(2):
                    g.wait_ge(re.h, 2 * oc + tt2 + 1)
                    ro.inc(g.dma_start(
                        out=attn_out[oc * 128:(oc + 1) * 128, tt2 * 512:(tt2 + 1) * 512],
                        in_=r_stage[:, tt2 * 512:(tt2 + 1) * 512]), 16)
            g.wait_ge(ro.h, 16 * 2 * NFC)
            g.wait_ge(st.h, st.n)

    return b



def build_moe():
    b = B()
    nc = b.nc
    x_in = b.din("x_fm", [D, TC], F32)
    attn_in = b.din("attn_in", [D, TC], F32)
    xn_in = b.din("xn_in", [D, TC], BF16)
    rT_in = b.din("rT", [D, 8], BF16)
    sgT_in = b.din("sgT", [D, MOE_I], BF16)
    suT_in = b.din("suT", [D, MOE_I], BF16)
    sdT_in = b.din("sdT", [MOE_I, D], BF16)
    wgT_in = b.din("wgT", [8, D, MOE_I], BF16)
    wuT_in = b.din("wuT", [8, D, MOE_I], BF16)
    wdT_in = b.din("wdT", [8, MOE_I, D], BF16)
    sel8_in = b.din("sel8", [8, 8 * 128], BF16)
    ident_in = b.din("ident", [128, 128], F32)

    x1_out = b.dout("x1_out", [D, TC], F32)
    xn1_out = b.dout("xn1_out", [D, TC], BF16)

    es = b.es
    with es:
        block = es.enter_context(nc.Block())
        ld = b.sem("ld")
        gl = b.sem("gl")
        st = b.sem("st")

        ones_col = b.sbuf("ones_col", [128, 1], F32)
        ones_row = b.sbuf("ones_row", [1, 128], F32)
        cst = b.sbuf("cst", [128, 4], F32)
        ident_f = b.sbuf("ident_f", [128, 128], F32)
        cset = b.sem("cset")

        @block.vector
        def _(vector):
            vector.memset(ones_col[:, :], 1.0)
            vector.memset(ones_row[:, :], 1.0)
            vector.memset(cst[:, 0:1], 1e-5)
            cset.inc(vector.memset(cst[:, 1:2], 0.0))

        x1a = b.sbuf("x1a", [128, NFC * TC], F32)
        xn = b.sbuf("xn", [128, NFC * TC], BF16)

        @block.sync
        def _(sync):
            ld.inc(sync.dma_start(out=fm_view(xn[:, :], NFC), in_=fm_dram(xn_in)), 16)
            ld.inc(sync.dma_start(out=ident_f[:, :], in_=ident_in), 16)
        xn_ld = ld.n

        m1 = b.sem("m1")

        @block.gpsimd
        def _(g):
            gl.inc(g.dma_start(out=fm_view(x1a[:, :], NFC), in_=fm_dram(x_in)), 16)
            g.wait_ge(gl.h, 16)
            gl.inc(g.dma_start(out=fm_view(x1a[:, :], NFC), in_=fm_dram(attn_in),
                               accum_op=ALU.add), 16)

        @block.vector
        def _(vector):
            vector.wait_ge(gl.h, 32)
            vector.wait_ge(ld.h, xn_ld)
            for fc in range(NFC):
                last = vector.tensor_tensor(x1a[:, fc * TC:(fc + 1) * TC],
                                            x1a[:, fc * TC:(fc + 1) * TC],
                                            xn[:, fc * TC:(fc + 1) * TC], ALU.add)
            m1.inc(last)

        rT_s = b.sbuf("rT_s", [128, NFC * 8], BF16)
        sel_s = b.sbuf("sel_s", [8, 8 * 128], BF16)
        comb_tm = b.sbuf("comb_tm", [128, 8 * 8], F32)
        comb_fm = b.sbuf("comb_fm", [8, TC], BF16)
        rsc = b.sbuf("moe_rsc", [128, 8 * 8], F32)
        rcol = b.sbuf("rcol", [128, 8], F32)

        @block.sync
        def _(sync):
            ld.inc(sync.dma_start(out=rT_s[:, :].rearrange("p (c e) -> p c e", c=NFC),
                                  in_=rT_in.rearrange("(c p) e -> p c e", p=128)), 16)
            ld.inc(sync.dma_start(out=sel_s[:, :], in_=sel8_in), 16)
        r_ld = ld.n

        sc_r = ExitStack()
        r_ps = b.psum("r_ps", [128, 8], F32, es=sc_r)
        t_ps = b.psum("t_ps", [8, 128], F32, es=sc_r)

        rm = b.sem("rm")
        rsv = b.sem("rsv")
        rse = b.sem("rse")
        rt = b.sem("rt")
        rf = b.sem("rf")

        @block.tensor
        def _(tensor):
            tensor.wait_ge(ld.h, r_ld)
            for tc in range(8):
                if tc >= 1:
                    tensor.wait_ge(rsv.h, 2 * tc - 1)
                for fc in range(NFC):
                    mm = tensor.matmul(r_ps[:, :], xn[:, fc * TC + tc * 128:fc * TC + (tc + 1) * 128],
                                       rT_s[:, fc * 8:(fc + 1) * 8],
                                       start=(fc == 0), stop=(fc == NFC - 1))
                rm.inc(mm)

        vch = b.sem("vch")

        @block.vector
        def _(vector):
            def ch(inst):
                vch.inc(inst)
                vector.wait_ge(vch.h, vch.n)
            for tc in range(8):
                w = rsc[:, 0:8]
                ebuf = rsc[:, 8:16]
                tbuf = rsc[:, 16:24]
                vector.wait_ge(rm.h, tc + 1)
                ch(vector.reduce_max(rcol[:, 0:1], r_ps[:, :], axis=mybir.AxisListType.X))
                rsv.inc(vector.tensor_scalar(w, r_ps[:, :], rcol[:, 0:1], None, ALU.subtract))
                vector.wait_ge(rse.h, tc + 1)
                ch(vector.reciprocal(rcol[:, 1:2], rcol[:, 2:3]))
                ch(vector.tensor_scalar(w, ebuf, rcol[:, 1:2], None, ALU.mult))
                ch(vector.reduce_max(rcol[:, 3:4], w, axis=mybir.AxisListType.X))
                ch(vector.tensor_scalar(tbuf, w, rcol[:, 3:4], None, ALU.is_ge))
                ch(vector.scalar_tensor_tensor(tbuf, tbuf, -1e9, w, ALU.mult, ALU.add))
                ch(vector.reduce_max(rcol[:, 5:6], tbuf, axis=mybir.AxisListType.X))
                ch(vector.tensor_scalar(tbuf, w, rcol[:, 5:6], None, ALU.is_ge))
                rsv.inc(vector.tensor_tensor(comb_tm[:, tc * 8:(tc + 1) * 8], w, tbuf, ALU.mult))

        @block.scalar
        def _(scalar):
            for tc in range(8):
                scalar.wait_ge(rsv.h, 2 * tc + 1)
                if tc >= 1:
                    scalar.wait_ge(rsv.h, 2 * tc)
                rse.inc(scalar.activation(rsc[:, 8:16], rsc[:, 0:8], AF.Exp,
                                          accum_out=rcol[:, 2:3]))

        @block.tensor
        def _(tensor):
            for tc in range(8):
                tensor.wait_ge(rsv.h, 2 * (tc + 1))
                if tc >= 1:
                    tensor.wait_ge(rf.h, tc)
                rt.inc(tensor.transpose(t_ps[:, :], comb_tm[:, tc * 8:(tc + 1) * 8], ident_f[:, :]))

        @block.vector
        def _(vector):
            for tc in range(8):
                vector.wait_ge(rt.h, tc + 1)
                rf.inc(vector.tensor_copy(comb_fm[:, tc * 128:(tc + 1) * 128], t_ps[:, :]))

        sc_exp = ExitStack()
        he_buf = b.sbuf("he_buf", [128, NIC * TC], BF16, es=sc_exp)
        wg_buf = b.sbuf("wg_buf", [128, 2 * NFC * 128], BF16, es=sc_exp)
        wu_buf = b.sbuf("wu_buf", [128, 2 * NFC * 128], BF16, es=sc_exp)
        wd_buf = b.sbuf("wd_buf", [128, 2 * NIC * 128], BF16, es=sc_exp)
        sig_buf = b.sbuf("sig_buf", [128, 2 * 512], F32, es=sc_exp)
        tsc_buf = b.sbuf("tsc_buf", [128, 512], F32, es=sc_exp)
        comb_bc = b.sbuf("comb_bc", [128, TC], F32, es=sc_exp)

        sc_r.close()
        bc_ps = b.psum("bc_ps", [128, 1024], F32, es=sc_exp)
        g_ps = [b.psum("g_ps0", [128, 512], F32, es=sc_exp), b.psum("g_ps1", [128, 512], F32, es=sc_exp)]
        u_ps = [b.psum("u_ps0", [128, 512], F32, es=sc_exp), b.psum("u_ps1", [128, 512], F32, es=sc_exp)]
        d_ps = [b.psum("d_ps0", [128, 512], F32, es=sc_exp), b.psum("d_ps1", [128, 512], F32, es=sc_exp)]

        lwg = b.sem("lwg")
        lwd = b.sem("lwd")
        mmg = b.sem("mmg")
        sg = b.sem("sg")
        he = b.sem("he")
        cbc = b.sem("cbc")
        mmd = b.sem("mmd")
        dev_ = b.sem("dev")

        NEXP = 9

        def gu_src(e, kind, ic):
            if e == 8:
                base = sgT_in if kind == "g" else suT_in
                sl = base[:, ic * 128:(ic + 1) * 128]
            else:
                base = wgT_in if kind == "g" else wuT_in
                sl = base[e, :, ic * 128:(ic + 1) * 128]
            return sl.rearrange("(c p) d -> p c d", p=128)

        def wd_src(e, oc):
            if e == 8:
                sl = sdT_in[:, oc * 128:(oc + 1) * 128]
            else:
                sl = wdT_in[e, :, oc * 128:(oc + 1) * 128]
            return sl.rearrange("(c p) d -> p c d", p=128)

        @block.sync
        def _(sync):
            k = 0
            for e in range(NEXP):
                for ic in range(NIC):
                    if k >= 2:
                        sync.wait_ge(mmg.h, 4 * (k - 1))
                    buf = k % 2
                    lwg.inc(sync.dma_start(
                        out=wg_buf[:, buf * NFC * 128:(buf + 1) * NFC * 128].rearrange("p (c d) -> p c d", c=NFC),
                        in_=gu_src(e, "g", ic)), 16)
                    lwg.inc(sync.dma_start(
                        out=wu_buf[:, buf * NFC * 128:(buf + 1) * NFC * 128].rearrange("p (c d) -> p c d", c=NFC),
                        in_=gu_src(e, "u", ic)), 16)
                    k += 1

        @block.gpsimd
        def _(g):
            kd = 0
            for e in range(NEXP):
                for oc in range(NFC):
                    if kd >= 2:
                        g.wait_ge(mmd.h, 2 * (kd - 1))
                    buf = kd % 2
                    lwd.inc(g.dma_start(
                        out=wd_buf[:, buf * NIC * 128:(buf + 1) * NIC * 128].rearrange("p (c d) -> p c d", c=NIC),
                        in_=wd_src(e, oc)), 16)
                    kd += 1

        @block.tensor
        def _(tensor):
            k = 0
            kt = 0
            kd = 0
            kdt = 0
            for e in range(NEXP):
                if e < 8:
                    tensor.wait_ge(rf.h, 8)
                    if e >= 1:
                        tensor.wait_ge(he.h, 2 * NIC * e)
                    for th in range(2):
                        cbc.inc(tensor.matmul(bc_ps[:, th * 512:(th + 1) * 512],
                                              sel_s[:, e * 128:(e + 1) * 128],
                                              comb_fm[:, th * 512:(th + 1) * 512],
                                              start=True, stop=True))
                for ic in range(NIC):
                    tensor.wait_ge(lwg.h, 32 * (k + 1))
                    buf = k % 2
                    for th in range(2):
                        pb = kt % 2
                        if kt >= 2:
                            tensor.wait_ge(he.h, kt - 1)
                        for fc in range(NFC):
                            mm = tensor.matmul(g_ps[pb][:, :],
                                               wg_buf[:, (buf * NFC + fc) * 128:(buf * NFC + fc + 1) * 128],
                                               xn[:, fc * TC + th * 512:fc * TC + th * 512 + 512],
                                               start=(fc == 0), stop=(fc == NFC - 1))
                        mmg.inc(mm)
                        for fc in range(NFC):
                            mm = tensor.matmul(u_ps[pb][:, :],
                                               wu_buf[:, (buf * NFC + fc) * 128:(buf * NFC + fc + 1) * 128],
                                               xn[:, fc * TC + th * 512:fc * TC + th * 512 + 512],
                                               start=(fc == 0), stop=(fc == NFC - 1))
                        mmg.inc(mm)
                        kt += 1
                    k += 1
                tensor.wait_ge(he.h, 2 * NIC * (e + 1))
                for oc in range(NFC):
                    tensor.wait_ge(lwd.h, 16 * (kd + 1))
                    bufd = kd % 2
                    for th in range(2):
                        pb = kdt % 2
                        if kdt >= 2:
                            tensor.wait_ge(dev_.h, kdt - 1)
                        for ic in range(NIC):
                            mm = tensor.matmul(d_ps[pb][:, :],
                                               wd_buf[:, (bufd * NIC + ic) * 128:(bufd * NIC + ic + 1) * 128],
                                               he_buf[:, ic * TC + th * 512:ic * TC + th * 512 + 512],
                                               start=(ic == 0), stop=(ic == NIC - 1))
                        mmd.inc(mm)
                        kdt += 1
                    kd += 1

        @block.scalar
        def _(scalar):
            kt = 0
            for e in range(NEXP):
                for ic in range(NIC):
                    for th in range(2):
                        scalar.wait_ge(mmg.h, 2 * kt + 1)
                        if kt >= 2:
                            scalar.wait_ge(he.h, kt - 1)
                        sg.inc(scalar.activation(sig_buf[:, (kt % 2) * 512:(kt % 2 + 1) * 512],
                                                 g_ps[kt % 2][:, :], AF.Sigmoid))
                        kt += 1

        @block.vector
        def _(vector):
            kt = 0
            kdt = 0
            for e in range(NEXP):
                if e < 8:
                    for th in range(2):
                        vector.wait_ge(cbc.h, 2 * e + th + 1)
                        vector.tensor_copy(comb_bc[:, th * 512:(th + 1) * 512],
                                           bc_ps[:, th * 512:(th + 1) * 512])
                for ic in range(NIC):
                    for th in range(2):
                        pb = kt % 2
                        vector.wait_ge(sg.h, kt + 1)
                        vector.wait_ge(mmg.h, 2 * kt + 2)
                        vector.tensor_tensor(tsc_buf[:, :], sig_buf[:, pb * 512:(pb + 1) * 512],
                                             u_ps[pb][:, :], ALU.mult)
                        dst = he_buf[:, ic * TC + th * 512:ic * TC + th * 512 + 512]
                        if e < 8:
                            he.inc(vector.tensor_tensor(dst, tsc_buf[:, :],
                                                        comb_bc[:, th * 512:(th + 1) * 512], ALU.mult))
                        else:
                            he.inc(vector.tensor_copy(dst, tsc_buf[:, :]))
                        kt += 1
                for oc in range(NFC):
                    for th in range(2):
                        pb = kdt % 2
                        vector.wait_ge(mmd.h, kdt + 1)
                        vector.wait_ge(m1.h, 1)
                        dev_.inc(vector.tensor_tensor(
                            x1a[:, oc * TC + th * 512:oc * TC + th * 512 + 512],
                            x1a[:, oc * TC + th * 512:oc * TC + th * 512 + 512],
                            d_ps[pb][:, :], ALU.add))
                        kdt += 1

        sc_exp.close()
        sc_ln = ExitStack()
        scratch1 = b.sbuf("lnscr1", [128, 2 * TC], F32, es=sc_ln)
        stats1 = b.sbuf("lnstats1", [1, 3 * TC], F32, es=sc_ln)
        xn1 = b.sbuf("xn1", [128, NFC * TC], BF16, es=sc_ln)
        ln1_done = b.sem("ln1_done")
        emit_ln_fm(b, block, xs=x1a, xn=xn1, tokens=TC, scratch=scratch1, stats=stats1,
                   ones_col=ones_col, ones_row=ones_row, eps_ap=cst[0:1, 0:1],
                   ready=[(dev_.h, 2 * NFC * NEXP), (cset.h, 1)], done=ln1_done)

        @block.gpsimd
        def _(g):
            g.wait_ge(ln1_done.h, 1)
            st.inc(g.dma_start(out=fm_dram(x1_out), in_=fm_sbuf_to_dram(x1a[:, :], NFC)), 16)
            st.inc(g.dma_start(out=fm_dram(xn1_out), in_=fm_sbuf_to_dram(xn1[:, :], NFC)), 16)
            g.wait_ge(st.h, 32)
        sc_ln.close()
    return b



XBW = 1027

def build_mamba():
    b = B()
    nc = b.nc
    xn1_in = b.din("xn1_in", [D, TC], BF16)
    wxbc_in = b.din("w_xbc", [D, CONV_DIM], BF16)
    wdt_in = b.din("w_dt", [D, 64], BF16)
    wz_in = b.din("w_z", [D, D_INNER], BF16)
    wout_in = b.din("w_out", [D_INNER, D], BF16)
    convw_in = b.din("conv_w", [128, NCC * 4], F32)
    convb_in = b.din("conv_b", [128, NCC], F32)
    dtb_in = b.din("dt_bias", [64, 1], F32)
    alog_in = b.din("A_log", [64, 1], F32)
    drow_in = b.din("D_row", [1, 64], F32)
    nw_in = b.din("nw_row", [1, D_INNER], F32)
    ident_in = b.din("ident", [128, 128], F32)
    identb_in = b.din("ident_bf", [128, 128], BF16)
    tril_in = b.din("tril", [128, 128], F32)
    mpref_in = b.din("m_pref", [128, 7], F32)
    whalo_in = b.din("w_halo", [128, 8], F32)

    mamba_out = b.dout("mamba_out", [D, TC], F32)

    halo_in_d = b.dram("halo_in", [128, NCC * 3], BF16)
    halo_out_d = b.dram("halo_out", [NCORES * 128, NCC * 3], BF16)
    carry_in_d = b.dram("carry_in", [128, D_INNER + 64], F32)
    carry_out_d = b.dram("carry_out", [NCORES * 128, D_INNER + 64], F32)
    y_scr_d = b.dram("y_scr", [128, NCHUNK * D_INNER], BF16)

    es = b.es
    with es:
        block = es.enter_context(nc.Block())
        ld = b.sem("ld")
        gl = b.sem("gl")
        st = b.sem("st")
        cc = b.sem("cc")

        ones_col = b.sbuf("ones_col", [128, 1], F32)
        ones_row = b.sbuf("ones_row", [1, 128], F32)
        ones64 = b.sbuf("ones64", [64, 128], F32)
        i64 = b.sbuf("i64", [64, 64], F32)
        cst = b.sbuf("cst", [128, 8], F32)
        ident_f = b.sbuf("ident_f", [128, 128], F32)
        ident_b = b.sbuf("ident_b", [128, 128], BF16)
        tril_s = b.sbuf("tril_s", [128, 128], F32)
        mpref_s = b.sbuf("mpref_s", [128, 7], F32)
        whalo_s = b.sbuf("whalo_s", [128, 8], F32)
        dtb_s = b.sbuf("dtb_s", [64, 1], F32)
        alog_s = b.sbuf("alog_s", [64, 1], F32)
        drow_s = b.sbuf("drow_s", [1, 64], F32)
        cset = b.sem("cset")

        @block.vector
        def _(vector):
            vector.memset(ones_col[:, :], 1.0)
            vector.memset(ones_row[:, :], 1.0)
            vector.memset(ones64[:, :], 1.0)
            vector.memset(cst[:, 0:1], 1e-5)
            vector.memset(cst[:, 1:2], 0.0)
            vector.memset(cst[:, 2:3], -1.0)
            cset.inc(vector.memset(cst[:, 3:4], 1.0))

        @block.sync
        def _(sync):
            ld.inc(sync.dma_start(out=ident_f[:, :], in_=ident_in), 16)
            ld.inc(sync.dma_start(out=ident_b[:, :], in_=identb_in), 16)
            ld.inc(sync.dma_start(out=tril_s[:, :], in_=tril_in), 16)
            ld.inc(sync.dma_start(out=mpref_s[:, :], in_=mpref_in), 16)
            ld.inc(sync.dma_start(out=whalo_s[:, :], in_=whalo_in), 16)
            ld.inc(sync.dma_start(out=dtb_s[:, :], in_=dtb_in), 16)
            ld.inc(sync.dma_start(out=alog_s[:, :], in_=alog_in), 16)
            ld.inc(sync.dma_start(out=drow_s[:, :], in_=drow_in), 16)
            ld.inc(sync.dma_start(out=i64[:, :], in_=ident_in[0:64, 0:64]), 16)
        const_ld = ld.n

        xn1 = b.sbuf("xn1", [128, NFC * TC], BF16)
        sdect_all = b.sbuf("sdect_all", [128, NCHUNK * 64], F32)
        dbc_all = b.sbuf("dbc_all", [128, NCHUNK * 64], F32)
        c_keep = b.sbuf("c_keep", [128, TC], BF16)
        sc_body = ExitStack()
        xbc = b.sbuf("xbc", [128, NCC * XBW], BF16, es=sc_body)
        convw_s = b.sbuf("convw_s", [128, NCC * 4], F32, es=sc_body)
        convb_s = b.sbuf("convb_s", [128, NCC], F32, es=sc_body)
        dt_raw = b.sbuf("dt_raw", [64, TC], F32, es=sc_body)
        dt_s = b.sbuf("dt_s", [64, TC], F32, es=sc_body)
        dsc = b.sbuf("dsc", [64, TC], F32, es=sc_body)
        a_cs = b.sbuf("a_cs", [64, TC], F32, es=sc_body)
        dec_s = b.sbuf("dec_s", [64, TC], F32, es=sc_body)
        aexp_s = b.sbuf("aexp_s", [64, 2], F32, es=sc_body)
        acst_all = b.sbuf("acst_all", [128, NCHUNK * 64], F32, es=sc_body)
        dtt_all = b.sbuf("dtt_all", [128, NCHUNK * 64], F32, es=sc_body)
        dect_all = b.sbuf("dect_all", [128, NCHUNK * 64], F32, es=sc_body)
        prod_d = b.sbuf("prod_d", [128, 64], F32, es=sc_body)
        rowsc = b.sbuf("rowsc", [1, 64], F32, es=sc_body)
        s_run = b.sbuf("s_run", [128, D_INNER], F32, es=sc_body)
        zsc = b.sbuf("zsc", [64, 128], F32, es=sc_body)
        d_b = b.sbuf("d_b", [128, 64], F32, es=sc_body)

        @block.sync
        def _(sync):
            ld.inc(sync.dma_start(out=fm_view(xn1[:, :], NFC), in_=fm_dram(xn1_in)), 16)
            ld.inc(sync.dma_start(out=convw_s[:, :], in_=convw_in), 16)
            ld.inc(sync.dma_start(out=convb_s[:, :], in_=convb_in), 16)
        xn1_ld = ld.n

        sc_pa = ExitStack()
        wx_buf = b.sbuf("wx_buf", [128, 2 * NFC * 128], BF16, es=sc_pa)
        sc_ip = ExitStack()
        ip_ps = [b.psum("ip_ps0", [128, 512], F32, es=sc_ip), b.psum("ip_ps1", [128, 512], F32, es=sc_ip)]
        lwx = b.sem("lwx")
        mip = b.sem("mip")
        eip = b.sem("eip")

        @block.sync
        def _(sync):
            for oc in range(NCC + 1):
                if oc >= 2:
                    sync.wait_ge(mip.h, 2 * (oc - 1))
                src = wxbc_in[:, oc * 128:(oc + 1) * 128] if oc < NCC else wdt_in
                w = 128 if oc < NCC else 64
                lwx.inc(sync.dma_start(
                    out=wx_buf[:, (oc % 2) * NFC * 128:(oc % 2) * NFC * 128 + NFC * w].rearrange("p (c d) -> p c d", c=NFC),
                    in_=src.rearrange("(c p) d -> p c d", p=128)), 16)

        @block.tensor
        def _(tensor):
            tensor.wait_ge(ld.h, xn1_ld)
            for oc in range(NCC + 1):
                tensor.wait_ge(lwx.h, 16 * (oc + 1))
                w = 128 if oc < NCC else 64
                for th in range(2):
                    k = 2 * oc + th
                    if k >= 2:
                        tensor.wait_ge(eip.h, k - 1)
                    out_ap = ip_ps[k % 2][:, :] if w == 128 else ip_ps[k % 2][0:64, :]
                    for fc in range(NFC):
                        mm = tensor.matmul(out_ap,
                                           wx_buf[:, (oc % 2) * NFC * 128 + fc * w:(oc % 2) * NFC * 128 + fc * w + w],
                                           xn1[:, fc * TC + th * 512:fc * TC + th * 512 + 512],
                                           start=(fc == 0), stop=(fc == NFC - 1))
                    mip.inc(mm)

        @block.vector
        def _(vector):
            for oc in range(NCC + 1):
                for th in range(2):
                    k = 2 * oc + th
                    vector.wait_ge(mip.h, k + 1)
                    if oc < NCC:
                        dst = xbc[:, oc * XBW + 3 + th * 512:oc * XBW + 3 + th * 512 + 512]
                        eip.inc(vector.tensor_copy(dst, ip_ps[k % 2][:, :]))
                    else:
                        eip.inc(vector.tensor_copy(dt_raw[:, th * 512:(th + 1) * 512],
                                                   ip_ps[k % 2][0:64, :]))
        ip_done = eip.n

        halo_sb = b.sbuf("halo_sb", [128, 8 * NCC * 3], BF16, es=sc_pa)
        halo_acc = b.sbuf("halo_acc", [128, NCC * 3], F32, es=sc_pa)
        hcmb = b.sem("hcmb")

        @block.gpsimd
        def _(g):
            g.wait_ge(eip.h, ip_done)
            gl.inc(g.dma_start(
                out=halo_in_d.ap().rearrange("p (c t) -> p c t", c=NCC),
                in_=xbc[:, :].rearrange("p (c t) -> p c t", c=NCC)[:, :, 1024:1027]), 16)
            g.wait_ge(gl.h, gl.n)
            cc.inc(g.collective_compute(
                "AllGather", ALU.bypass, replica_groups=[list(range(NCORES))],
                ins=[halo_in_d.ap().opt()], outs=[halo_out_d.ap().opt()]))
            g.wait_ge(cc.h, 1)
            gl.inc(g.dma_start(
                out=halo_sb[:, :].rearrange("p (j t) -> p j t", j=8),
                in_=halo_out_d.ap().rearrange("(j p) t -> p j t", p=128)), 16)
        halo_gl = gl.n

        @block.vector
        def _(vector):
            vector.wait_ge(gl.h, halo_gl)
            vector.wait_ge(cset.h, 1)
            vector.memset(halo_acc[:, :], 0.0)
            for j in range(8):
                hs = halo_sb[:, j * NCC * 3:(j + 1) * NCC * 3]
                inst = vector.scalar_tensor_tensor(halo_acc[:, :], hs, whalo_s[:, j:j + 1],
                                                   halo_acc[:, :], ALU.mult, ALU.add)
                hcmb.inc(inst)
                vector.wait_ge(hcmb.h, hcmb.n)
            hcmb.inc(vector.tensor_copy(
                xbc[:, :].rearrange("p (c t) -> p c t", c=NCC)[:, :, 0:3],
                halo_acc[:, :].rearrange("p (c t) -> p c t", c=NCC)))
        halo_done = hcmb.n

        cacc = b.sbuf("cacc", [128, 2 * TC], F32, es=sc_pa)
        cvt = b.sem("cvt")
        cvs = b.sem("cvs")

        @block.vector
        def _(vector):
            vector.wait_ge(hcmb.h, halo_done)
            for cci in range(NCC):
                buf = cci % 2
                acc = cacc[:, buf * TC:(buf + 1) * TC]
                base = cci * XBW
                if cci >= 2:
                    vector.wait_ge(cvs.h, cci - 1)
                vector.tensor_scalar(acc, xbc[:, base + 0:base + TC],
                                     convw_s[:, cci * 4 + 0:cci * 4 + 1],
                                     convb_s[:, cci:cci + 1], ALU.mult, ALU.add)
                for kk in range(1, 3):
                    vector.scalar_tensor_tensor(acc, xbc[:, base + kk:base + kk + TC],
                                                convw_s[:, cci * 4 + kk:cci * 4 + kk + 1],
                                                acc, ALU.mult, ALU.add)
                cvt.inc(vector.scalar_tensor_tensor(acc, xbc[:, base + 3:base + 3 + TC],
                                                    convw_s[:, cci * 4 + 3:cci * 4 + 4],
                                                    acc, ALU.mult, ALU.add))

        @block.scalar
        def _(scalar):
            for cci in range(NCC):
                buf = cci % 2
                scalar.wait_ge(cvt.h, cci + 1)
                cvs.inc(scalar.activation(xbc[:, cci * XBW + 3:cci * XBW + 3 + TC],
                                          cacc[:, buf * TC:(buf + 1) * TC], AF.Silu))
        conv_done = cvs.n

        S = Sched(b, "mb")
        AX = mybir.AxisListType.X

        @block.vector
        def _(vector):
            vector.memset(zsc[:, :], 0.0)

        ec = cst[0:64, :]

        S.add("vector", lambda e: e.tensor_scalar(dsc[:, :], dt_raw[:, :], dtb_s[:, :], None, ALU.add),
              waits=[(eip.h, ip_done), (cset.h, 1)])
        S.add("scalar", lambda e: e.activation(dec_s[:, :], dsc[:, :], AF.Abs), waits=[S.up("vector")])
        S.add("scalar", lambda e: e.activation(dec_s[:, :], dec_s[:, :], AF.Exp, scale=ec[:, 2:3]))
        S.add("scalar", lambda e: e.activation(dec_s[:, :], dec_s[:, :], AF.Ln, bias=ec[:, 3:4]))
        S.add("vector", lambda e: e.tensor_scalar(dt_s[:, :], dsc[:, :], 0.0, None, ALU.max))
        S.add("vector", lambda e: e.tensor_tensor(dt_s[:, :], dt_s[:, :], dec_s[:, :], ALU.add),
              waits=[S.up("scalar")])
        S.add("scalar", lambda e: e.activation(aexp_s[:, 0:1], alog_s[:, :], AF.Exp))
        S.add("vector", lambda e: e.tensor_scalar(dsc[:, :], dt_s[:, :], aexp_s[:, 0:1], ec[:, 2:3], ALU.mult, ALU.mult),
              waits=[S.up("scalar")])
        for ci in range(NCHUNK):
            sl = slice(ci * MCH, (ci + 1) * MCH)
            S.add("vector", lambda e, sl=sl: e.tensor_tensor_scan(a_cs[:, sl], dsc[:, sl], zsc[:, :], 0.0, ALU.add, ALU.add))
        for ci in range(NCHUNK):
            sl = slice(ci * MCH, (ci + 1) * MCH)
            lc = a_cs[:, ci * MCH + MCH - 1:ci * MCH + MCH]
            S.add("vector", lambda e, sl=sl, lc=lc: e.tensor_scalar(dec_s[:, sl], a_cs[:, sl], lc, ec[:, 2:3], ALU.subtract, ALU.mult))
        dec_ts_done = S.up("vector")
        for ci in range(NCHUNK):
            sl = slice(ci * MCH, (ci + 1) * MCH)
            S.add("scalar", lambda e, sl=sl: e.activation(dec_s[:, sl], dec_s[:, sl], AF.Exp),
                  waits=[dec_ts_done])

        sc_ip.close()
        sc_pa.close()
        tp_f = b.psum("tp_f", [128, 128], F32)
        tp_b = b.psum("tp_b", [128, 128], BF16)
        bcA = b.psum("bcA", [128, 128], F32)
        bcB = b.psum("bcB", [128, 128], F32)
        sh4 = b.psum("sh4", [128, 2048], F32)

        S.add("tensor", lambda e: e.matmul(tp_f[:, 0:64], ones_row[:, :], drow_s[:, :], start=True, stop=True),
              waits=[(ld.h, const_ld), (cset.h, 1)])
        S.add("vector", lambda e: e.tensor_copy(d_b[:, :], tp_f[:, 0:64]), waits=[S.up("tensor")])
        S.add("vector", lambda e: e.memset(prod_d[:, :], 1.0))

        id64 = ident_f[0:64, 0:64]
        for ci in range(NCHUNK):
            sl = slice(ci * MCH, (ci + 1) * MCH)
            s64 = slice(ci * 64, (ci + 1) * 64)
            S.add("tensor", lambda e, sl=sl: e.transpose(tp_f[:, 0:64], a_cs[:, sl], id64),
                  waits=[S.up("vector"), S.up("scalar")])
            S.add("vector", lambda e, s64=s64: e.tensor_copy(acst_all[:, s64], tp_f[:, 0:64]),
                  waits=[S.up("tensor")])
            S.add("scalar", lambda e, s64=s64: e.activation(sdect_all[:, s64], acst_all[:, s64], AF.Exp),
                  waits=[S.up("vector")])
            S.add("tensor", lambda e, sl=sl: e.transpose(tp_f[:, 0:64], dt_s[:, sl], id64),
                  waits=[S.up("vector")])
            S.add("vector", lambda e, s64=s64: e.tensor_copy(dtt_all[:, s64], tp_f[:, 0:64]),
                  waits=[S.up("tensor")])
            S.add("tensor", lambda e, sl=sl: e.transpose(tp_f[:, 0:64], dec_s[:, sl], id64),
                  waits=[S.up("vector")])
            S.add("vector", lambda e, s64=s64: e.tensor_copy(dect_all[:, s64], tp_f[:, 0:64]),
                  waits=[S.up("tensor")])
            S.add("scalar", lambda e, ci=ci: e.activation(aexp_s[:, 1:2],
                  a_cs[:, ci * MCH + MCH - 1:ci * MCH + MCH], AF.Exp), waits=[S.up("vector")])
            S.add("tensor", lambda e: e.transpose(tp_f[0:1, 0:64], aexp_s[:, 1:2], id64),
                  waits=[S.up("scalar"), S.up("vector")])
            S.add("vector", lambda e: e.tensor_copy(rowsc[:, :], tp_f[0:1, 0:64]),
                  waits=[S.up("tensor")])
            S.add("tensor", lambda e: e.matmul(tp_f[:, 0:64], ones_row[:, :], rowsc[:, :], start=True, stop=True),
                  waits=[S.up("vector")])
            S.add("vector", lambda e, s64=s64: e.tensor_copy(dbc_all[:, s64], tp_f[:, 0:64]),
                  waits=[S.up("tensor")])
            S.add("vector", lambda e, s64=s64: e.tensor_tensor(prod_d[:, :], prod_d[:, :], dbc_all[:, s64], ALU.mult),
                  waits=[S.up("vector")])

        x_tm = b.sbuf("x_tm", [128, 2048], BF16, es=sc_body)
        xdt_h = b.sbuf("xdt_h", [128, 2048], BF16, es=sc_body)
        xdtd_h = b.sbuf("xdtd_h", [128, 2048], BF16, es=sc_body)
        m_all = b.sbuf("m_all", [128, 32 * 128], BF16, es=sc_body)
        msk_s = b.sbuf("msk_s", [64, 2 * 128], F32, es=sc_body)
        seg_s = b.sbuf("seg_s", [128, 2 * 128], F32, es=sc_body)
        exps_s = b.sbuf("exps_s", [128, 2 * 128], F32, es=sc_body)
        g_tm = b.sbuf("g_tm", [128, 128], F32, es=sc_body)
        b_tm = b.sbuf("b_tm", [128, 128], BF16, es=sc_body)
        y_sc = b.sbuf("y_sc", [128, 2048], F32, es=sc_body)
        y_sc2 = b.sbuf("y_sc2", [128, 2048], F32, es=sc_body)
        s_bf = b.sbuf("s_bf", [128, 2048], BF16, es=sc_body)
        y_stage = b.sbuf("y_stage", [128, 2 * 2048], BF16, es=sc_body)

        def exp_ap(t, col0, nh):
            pstep = t[:, :].ap[0][0]
            return bass.AP(t, col0, [[pstep, 128], [1, nh], [0, 64]])

        S.add("vector", lambda e: e.memset(s_run[:, :], 0.0))

        conv_w_ = (cvs.h, conv_done)
        for ci in range(NCHUNK):
            s64c = ci * 64
            btile = xbc[:, 32 * XBW + 3 + ci * MCH:32 * XBW + 3 + ci * MCH + MCH]
            ctile = xbc[:, 33 * XBW + 3 + ci * MCH:33 * XBW + 3 + ci * MCH + MCH]
            S.add("tensor", lambda e, btile=btile: e.transpose(tp_b[:, :], btile, ident_b[:, :]),
                  waits=[conv_w_, S.up("vector")])
            S.add("vector", lambda e: e.tensor_copy(b_tm[:, :], tp_b[:, :]), waits=[S.up("tensor")])
            S.add("tensor", lambda e, btile=btile, ctile=ctile: e.matmul(bcA[:, :], btile, ctile, start=True, stop=True),
                  waits=[S.up("vector")])
            S.add("vector", lambda e: e.tensor_tensor(g_tm[:, :], bcA[:, :], tril_s[:, :], ALU.mult),
                  waits=[S.up("tensor")])
            for hf in range(2):
                h0 = 32 * hf
                for fc in range(NFC):
                    cci = hf * NFC + fc
                    xt_ = xbc[:, cci * XBW + 3 + ci * MCH:cci * XBW + 3 + ci * MCH + MCH]
                    S.add("tensor", lambda e, xt_=xt_: e.transpose(tp_b[:, :], xt_, ident_b[:, :]),
                          waits=[S.up("vector")])
                    S.add("vector", lambda e, fc=fc: e.tensor_copy(x_tm[:, fc * 128:(fc + 1) * 128], tp_b[:, :]),
                          waits=[S.up("tensor")])
                S.add("vector", lambda e, s64c=s64c, h0=h0: e.tensor_tensor(
                    xdt_h[:, :], x_tm[:, :], exp_ap(dtt_all, s64c + h0, 32), ALU.mult))
                S.add("vector", lambda e, s64c=s64c, h0=h0: e.tensor_tensor(
                    xdtd_h[:, :], xdt_h[:, :], exp_ap(dect_all, s64c + h0, 32), ALU.mult))
                if ci >= 1:
                    S.add("vector", lambda e, hf=hf: e.tensor_copy(s_bf[:, :], s_run[:, hf * 2048:(hf + 1) * 2048]))
                    def yoff(e, ctile=ctile):
                        for q in range(4):
                            mm = e.matmul(sh4[:, q * 512:(q + 1) * 512], ctile, s_bf[:, q * 512:(q + 1) * 512],
                                          start=True, stop=True)
                        return mm
                    S.add("tensor", yoff, waits=[S.up("vector")])
                    S.add("vector", lambda e, s64c=s64c, h0=h0: e.tensor_tensor(
                        y_sc[:, :], sh4[:, :], exp_ap(sdect_all, s64c + h0, 32), ALU.mult),
                        waits=[S.up("tensor")])
                else:
                    S.add("vector", lambda e: e.memset(y_sc[:, :], 0.0))
                for hp in range(32):
                    h = h0 + hp
                    pb = hp % 2
                    S.add("vector", lambda e, pb=pb, ci=ci, h=h: e.tensor_scalar(
                        msk_s[:, pb * 128:(pb + 1) * 128], a_cs[:, ci * MCH:(ci + 1) * MCH],
                        i64[:, h:h + 1], None, ALU.mult),
                        waits=[S.up("tensor")])
                    S.add("tensor", lambda e, pb=pb: e.matmul(
                        bcB[:, :] if pb else bcA[:, :], ones64[:, :],
                        msk_s[:, pb * 128:(pb + 1) * 128], start=True, stop=True),
                        waits=[S.up("vector")])
                    S.add("vector", lambda e, pb=pb, col=s64c + h: e.tensor_scalar(
                        seg_s[:, pb * 128:(pb + 1) * 128], bcB[:, :] if pb else bcA[:, :],
                        acst_all[:, col:col + 1], cst[:, 1:2], ALU.subtract, ALU.min),
                        waits=[S.up("tensor"), S.up("scalar")])
                    S.add("scalar", lambda e, pb=pb: e.activation(
                        exps_s[:, pb * 128:(pb + 1) * 128], seg_s[:, pb * 128:(pb + 1) * 128], AF.Exp),
                        waits=[S.up("vector")])
                    S.add("vector", lambda e, pb=pb, hp=hp: e.tensor_tensor(
                        m_all[:, hp * 128:(hp + 1) * 128], exps_s[:, pb * 128:(pb + 1) * 128],
                        g_tm[:, :], ALU.mult),
                        waits=[S.up("scalar")])
                def ydiag(e):
                    for hp in range(32):
                        mm = e.matmul(sh4[:, hp * 64:(hp + 1) * 64], m_all[:, hp * 128:(hp + 1) * 128],
                                      xdt_h[:, hp * 64:(hp + 1) * 64], start=True, stop=True)
                    return mm
                S.add("tensor", ydiag, waits=[S.up("vector")])
                S.add("vector", lambda e, h0=h0: e.tensor_tensor(
                    y_sc2[:, :], x_tm[:, :], exp_ap(d_b, h0, 32), ALU.mult))
                S.add("vector", lambda e: e.tensor_tensor(y_sc[:, :], y_sc[:, :], y_sc2[:, :], ALU.add))
                kst = (2 * ci + hf) % 2
                S.add("vector", lambda e, kst=kst: e.tensor_tensor(
                    y_stage[:, kst * 2048:(kst + 1) * 2048],
                    y_sc[:, :], sh4[:, :], ALU.add),
                    waits=[S.up("tensor"), S.up("gpsimd")])
                S.add("gpsimd", lambda e, ci=ci, hf=hf, kst=kst: e.dma_start(
                    out=y_scr_d.ap()[:, ci * D_INNER + hf * 2048:ci * D_INNER + (hf + 1) * 2048],
                    in_=y_stage[:, kst * 2048:(kst + 1) * 2048]),
                    waits=[S.up("vector")], amt=16)
                def stmm(e):
                    for q in range(4):
                        mm = e.matmul(sh4[:, q * 512:(q + 1) * 512], b_tm[:, :],
                                      xdtd_h[:, q * 512:(q + 1) * 512], start=True, stop=True)
                    return mm
                S.add("tensor", stmm, waits=[S.up("vector")])
                S.add("vector", lambda e, hf=hf, col=s64c + h0: e.tensor_tensor(
                    y_sc2[:, :], s_run[:, hf * 2048:(hf + 1) * 2048],
                    exp_ap(dbc_all, col, 32), ALU.mult),
                    waits=[S.up("tensor")])
                S.add("vector", lambda e, hf=hf: e.tensor_tensor(
                    s_run[:, hf * 2048:(hf + 1) * 2048], y_sc2[:, :], sh4[:, :], ALU.add))

        S.add("vector", lambda e: e.tensor_copy(
            c_keep[:, :], bass.AP(xbc, 33 * XBW + 3, [[xbc[:, :].ap[0][0], 128], [1, TC]])))

        S.add("gpsimd", lambda e: e.dma_start(out=carry_in_d.ap()[:, 0:D_INNER], in_=s_run[:, :]),
              waits=[S.up("vector")], amt=16)
        S.add("gpsimd", lambda e: e.dma_start(out=carry_in_d.ap()[:, D_INNER:D_INNER + 64], in_=prod_d[:, :]),
              amt=16)
        S.add("coll", lambda e: e.collective_compute(
            "AllGather", ALU.bypass, replica_groups=[list(range(NCORES))],
            ins=[carry_in_d.ap().opt()], outs=[carry_out_d.ap().opt()]),
            waits=[S.up("gpsimd")])
        S.emit(block)
        pb_g = (S.sems["gpsimd"].h, S.sems["gpsimd"].n)
        pb_c = (S.sems["coll"].h, S.sems["coll"].n)
        pb_v = (S.sems["vector"].h, S.sems["vector"].n)
        sc_body.close()

        P = Sched(b, "pc")
        sc_pc = ExitStack()
        y_sb = b.sbuf("y_sb", [128, NCHUNK * D_INNER], BF16, es=sc_pc)
        nw_b = b.sbuf("nw_b", [128, D_INNER], F32, es=sc_pc)
        ysc_pc = b.sbuf("ysc_pc", [128, 2048], F32, es=sc_pc)
        sumsq = b.sbuf("sumsq", [128, 8], F32, es=sc_pc)
        rstd_a = b.sbuf("rstd_a", [128, 8], F32, es=sc_pc)
        dsc2 = b.sbuf("dsc2", [128, 64], F32, es=sc_pc)
        zs_s = b.sbuf("zs_s", [128, 2 * 512], F32, es=sc_pc)
        sq_scr = b.sbuf("sq_scr", [128, 512], BF16, es=sc_pc)
        sqtmp = b.sbuf("sqtmp", [128, 2], F32, es=sc_pc)
        ostage = b.sbuf("ostage", [128, 2 * 512], F32, es=sc_pc)
        nwst = b.sbuf("nwst", [1, 512], F32, es=sc_pc)

        if _DBG_SPILL:
            P.add("gpsimd", lambda e: e.dma_start(out=dbg_y, in_=y_scr_d.ap()),
                  waits=[pb_g, pb_c], amt=16)
        P.add("gpsimd", lambda e: e.dma_start(out=y_sb[:, :], in_=y_scr_d.ap()),
              waits=[pb_g, pb_c], amt=16)
        for zg in range(8):
            P.add("gpsimd", lambda e, zg=zg: e.dma_start(out=nwst[:, :], in_=nw_in[0:1, zg * 512:(zg + 1) * 512]),
                  waits=[P.up("tensor")], amt=16)
            P.add("tensor", lambda e, zg=zg: e.matmul(sh4[:, (zg % 4) * 512:(zg % 4) * 512 + 512],
                                                      ones_row[:, :], nwst[:, :],
                                                      start=True, stop=True),
                  waits=[pb_g, pb_v, P.up("vector"), P.up("gpsimd")])
            P.add("vector", lambda e, zg=zg: e.tensor_copy(nw_b[:, zg * 512:(zg + 1) * 512],
                                                           sh4[:, (zg % 4) * 512:(zg % 4) * 512 + 512]),
                  waits=[P.up("tensor")])
        P.add("vector", lambda e: e.memset(sumsq[:, :], 0.0))
        P.add("vector", lambda e: e.memset(t_buf[:, :], 0.0))

        sc_mid = ExitStack()
        t_buf = b.sbuf("t_buf", [128, D_INNER], F32, es=sc_mid)
        t_bf = b.sbuf("t_bf", [128, D_INNER], BF16, es=sc_mid)
        sc_slab = ExitStack()
        slab = b.sbuf("slab", [128, D_INNER + 64], F32, es=sc_slab)
        for j in range(7):
            P.add("gpsimd", lambda e, j=j: e.dma_start(
                out=slab[:, :], in_=carry_out_d.ap()[j * 128:(j + 1) * 128, :]),
                waits=[P.up("vector")], amt=16)
            P.add("vector", lambda e: e.tensor_scalar(dsc2[:, :], slab[:, D_INNER:D_INNER + 64], 1.0, None, ALU.subtract),
                  waits=[P.up("gpsimd")])
            P.add("vector", lambda e, j=j: e.tensor_scalar(dsc2[:, :], dsc2[:, :], mpref_s[:, j:j + 1],
                                                           cst[:, 3:4], ALU.mult, ALU.add),
                  waits=[P.up("vector")])
            P.add("vector", lambda e: e.tensor_tensor(t_buf[:, :], t_buf[:, :], exp_ap(dsc2, 0, 64), ALU.mult),
                  waits=[P.up("vector")])
            P.add("vector", lambda e, j=j: e.scalar_tensor_tensor(t_buf[:, :], slab[:, 0:D_INNER],
                                                                  mpref_s[:, j:j + 1], t_buf[:, :],
                                                                  ALU.mult, ALU.add))
        sc_slab.close()

        for ci in range(NCHUNK):
            P.add("vector", lambda e: e.tensor_copy(t_bf[:, :], t_buf[:, :]),
                  waits=[P.up("tensor")])
            for hf in range(2):
                def ycar(e, ci=ci, hf=hf):
                    ct = c_keep[:, ci * MCH:(ci + 1) * MCH]
                    for q in range(4):
                        mm = e.matmul(sh4[:, q * 512:(q + 1) * 512], ct,
                                      t_bf[:, hf * 2048 + q * 512:hf * 2048 + (q + 1) * 512],
                                      start=True, stop=True)
                    return mm
                P.add("tensor", ycar, waits=[P.up("vector")])
                P.add("vector", lambda e, ci=ci, hf=hf: e.tensor_tensor(
                    ysc_pc[:, :], sh4[:, :], exp_ap(sdect_all, ci * 64 + 32 * hf, 32), ALU.mult),
                    waits=[P.up("tensor")])
                ysl = slice(ci * D_INNER + hf * 2048, ci * D_INNER + (hf + 1) * 2048)
                P.add("vector", lambda e, ysl=ysl: e.tensor_tensor(
                    y_sb[:, ysl], y_sb[:, ysl], ysc_pc[:, :], ALU.add))
            P.add("vector", lambda e, ci=ci: e.tensor_tensor(
                t_buf[:, :], t_buf[:, :], exp_ap(dbc_all, ci * 64, 64), ALU.mult))

        sc_mid.close()
        sc_wz = ExitStack()
        wz_buf = b.sbuf("wz_buf", [128, 2 * NFC * 512], BF16, es=sc_wz)
        kz = 0
        for zg in range(8):
            P.add("gpsimd", lambda e, zg=zg: e.dma_start(
                out=wz_buf[:, (zg % 2) * NFC * 512:(zg % 2 + 1) * NFC * 512].rearrange("p (c d) -> p c d", c=NFC),
                in_=wz_in[:, zg * 512:(zg + 1) * 512].rearrange("(c p) d -> p c d", p=128)),
                waits=[P.up("tensor"), P.up("vector")], amt=16)
            wz_tok = P.up("gpsimd")
            for ch in range(NCHUNK):
                def zmm(e, zg=zg, ch=ch, kz=kz):
                    for fc in range(NFC):
                        mm = e.matmul(sh4[:, (kz % 2) * 512:(kz % 2) * 512 + 512],
                                      xn1[:, fc * TC + ch * 128:fc * TC + (ch + 1) * 128],
                                      wz_buf[:, ((zg % 2) * NFC + fc) * 512:((zg % 2) * NFC + fc + 1) * 512],
                                      start=(fc == 0), stop=(fc == NFC - 1))
                    return mm
                P.add("tensor", zmm, waits=[wz_tok, P.up("scalar"), P.up("vector")])
                P.add("scalar", lambda e, kz=kz: e.activation(
                    zs_s[:, (kz % 2) * 512:(kz % 2) * 512 + 512],
                    sh4[:, (kz % 2) * 512:(kz % 2) * 512 + 512], AF.Silu),
                    waits=[P.up("tensor")])
                ysl = slice(ch * D_INNER + zg * 512, ch * D_INNER + (zg + 1) * 512)
                P.add("vector", lambda e, ysl=ysl, kz=kz: e.tensor_tensor(
                    y_sb[:, ysl], y_sb[:, ysl], zs_s[:, (kz % 2) * 512:(kz % 2) * 512 + 512], ALU.mult),
                    waits=[P.up("scalar")])
                P.add("scalar", lambda e, ysl=ysl: e.activation(
                    sq_scr[:, :], y_sb[:, ysl], AF.Square, accum_out=sqtmp[:, 0:1]),
                    waits=[P.up("vector")])
                P.add("vector", lambda e, ch=ch: e.tensor_tensor(
                    sumsq[:, ch:ch + 1], sumsq[:, ch:ch + 1], sqtmp[:, 0:1], ALU.add),
                    waits=[P.up("scalar"), P.up("vector")])
                kz += 1
        sc_wz.close()

        for ch in range(NCHUNK):
            P.add("vector", lambda e, ch=ch: e.tensor_scalar(rstd_a[:, ch:ch + 1], sumsq[:, ch:ch + 1],
                                                             1.0 / D_INNER, None, ALU.mult),
                  waits=[P.up("vector")])
            P.add("scalar", lambda e, ch=ch: e.activation(rstd_a[:, ch:ch + 1], rstd_a[:, ch:ch + 1],
                                                          AF.Sqrt, bias=cst[:, 0:1]),
                  waits=[P.up("vector")])
            P.add("vector", lambda e, ch=ch: e.reciprocal(rstd_a[:, ch:ch + 1], rstd_a[:, ch:ch + 1]),
                  waits=[P.up("scalar")])
            ysl = slice(ch * D_INNER, (ch + 1) * D_INNER)
            P.add("vector", lambda e, ysl=ysl, ch=ch: e.tensor_scalar(
                y_sb[:, ysl], y_sb[:, ysl], rstd_a[:, ch:ch + 1], None, ALU.mult),
                waits=[P.up("vector")])
            P.add("vector", lambda e, ysl=ysl: e.tensor_tensor(
                y_sb[:, ysl], y_sb[:, ysl], nw_b[:, :], ALU.mult))

        sc_fm = ExitStack()
        y_fm = b.sbuf("y_fm", [128, 32 * 512], BF16, es=sc_fm)
        wo_buf = b.sbuf("wo_buf", [128, 2 * 32 * 128], BF16, es=sc_fm)
        ko = 0
        for hf2 in range(2):
            for ch in range(4 * hf2, 4 * hf2 + 4):
                for dc in range(32):
                    ysl = y_sb[:, ch * D_INNER + dc * 128:ch * D_INNER + (dc + 1) * 128]
                    P.add("tensor", lambda e, ysl=ysl: e.transpose(tp_b[:, :], ysl, ident_b[:, :]),
                          waits=[P.up("vector")])
                    P.add("vector", lambda e, dc=dc, ch=ch, hf2=hf2: e.tensor_copy(
                        y_fm[:, dc * 512 + (ch - 4 * hf2) * 128:dc * 512 + (ch - 4 * hf2) * 128 + 128],
                        tp_b[:, :]),
                        waits=[P.up("tensor")])
            for oc in range(NFC):
                P.add("gpsimd", lambda e, oc=oc, ko=ko: e.dma_start(
                    out=wo_buf[:, (ko % 2) * 32 * 128:(ko % 2 + 1) * 32 * 128].rearrange("p (c d) -> p c d", c=32),
                    in_=wout_in[:, oc * 128:(oc + 1) * 128].rearrange("(c p) d -> p c d", p=128)),
                    waits=[P.up("tensor")], amt=16)
                wo_tok = P.up("gpsimd")
                def opmm(e, ko=ko):
                    for dc in range(32):
                        mm = e.matmul(sh4[:, (ko % 2) * 512:(ko % 2) * 512 + 512],
                                      wo_buf[:, ((ko % 2) * 32 + dc) * 128:((ko % 2) * 32 + dc + 1) * 128],
                                      y_fm[:, dc * 512:(dc + 1) * 512],
                                      start=(dc == 0), stop=(dc == 31))
                    return mm
                P.add("tensor", opmm, waits=[wo_tok, P.up("vector")])
                P.add("vector", lambda e, ko=ko: e.tensor_copy(ostage[:, (ko % 2) * 512:(ko % 2) * 512 + 512],
                                                               sh4[:, (ko % 2) * 512:(ko % 2) * 512 + 512]),
                      waits=[P.up("tensor"), P.up("gpsimd")])
                P.add("gpsimd", lambda e, oc=oc, hf2=hf2, ko=ko: e.dma_start(
                    out=mamba_out[oc * 128:(oc + 1) * 128, hf2 * 512:(hf2 + 1) * 512],
                    in_=ostage[:, (ko % 2) * 512:(ko % 2) * 512 + 512]),
                    waits=[P.up("vector")], amt=16)
                ko += 1
        sc_fm.close()
        P.emit(block)

        @block.gpsimd
        def _(g):
            g.wait_ge(P.sems["gpsimd"].h, P.sems["gpsimd"].n)
        sc_pc.close()
    return b



def build_final():
    b = B()
    nc = b.nc
    x1_in = b.din("x1_in", [D, TC], F32)
    mamba_in = b.din("mamba_in", [D, TC], F32)
    xn1_in = b.din("xn1_in", [D, TC], BF16)
    fac_in = b.din("factors", [8, D, D], F32)
    bias_in = b.din("l1_bias", [128, NFC], F32)
    out = b.dout("out", [D, TC], F32)

    es = b.es
    with es:
        block = es.enter_context(nc.Block())
        ld = b.sem("ld")
        gl = b.sem("gl")
        acd = b.sem("acd")
        mmf = b.sem("mmf")
        evf = b.sem("evf")
        stf = b.sem("stf")
        cvt = b.sem("cvt")

        x1a = b.sbuf("x1a", [128, NFC * TC], F32)
        xn1 = b.sbuf("xn1", [128, NFC * TC], BF16)
        bias_s = b.sbuf("bias_s", [128, NFC], F32)
        wf_buf = b.sbuf("wf_buf", [128, 2 * NFC * 128], F32)
        wsum_bf = b.sbuf("wsum_bf", [128, 2 * NFC * 128], BF16)
        ostage = b.sbuf("ostage", [128, 2 * 512], F32)
        f_ps = [b.psum("f_ps0", [128, 512], F32), b.psum("f_ps1", [128, 512], F32)]

        @block.sync
        def _(sync):
            ld.inc(sync.dma_start(out=fm_view(xn1[:, :], NFC), in_=fm_dram(xn1_in)), 16)
            ld.inc(sync.dma_start(out=bias_s[:, :], in_=bias_in), 16)
        xn1_ld = ld.n

        m1 = b.sem("m1")

        @block.gpsimd
        def _(g):
            gl.inc(g.dma_start(out=fm_view(x1a[:, :], NFC), in_=fm_dram(x1_in)), 16)
            g.wait_ge(gl.h, gl.n)
            gl.inc(g.dma_start(out=fm_view(x1a[:, :], NFC), in_=fm_dram(mamba_in),
                               accum_op=ALU.add), 16)
            g.wait_ge(gl.h, gl.n)
            for oc in range(NFC):
                buf = oc % 2
                if oc >= 2:
                    g.wait_ge(cvt.h, oc - 1)
                dst = fm_view(wf_buf[:, buf * NFC * 128:(buf + 1) * NFC * 128], NFC)
                for r in range(8):
                    acd.inc(g.dma_start(out=dst,
                                        in_=fac_in[r, :, oc * 128:(oc + 1) * 128].rearrange("(c p) d -> p c d", p=128),
                                        accum_op=(ALU.add if r > 0 else ALU.bypass)), 16)
                    g.wait_ge(acd.h, acd.n)

        @block.vector
        def _(vector):
            def evac(oc, tt2):
                k = 2 * oc + tt2
                vector.wait_ge(mmf.h, k + 1)
                vector.wait_ge(gl.h, 32)
                if k >= 2:
                    vector.wait_ge(stf.h, 16 * (k - 1))
                evf.inc(vector.scalar_tensor_tensor(
                    ostage[:, (k % 2) * 512:(k % 2 + 1) * 512],
                    f_ps[k % 2][:, :], bias_s[:, oc:oc + 1],
                    x1a[:, oc * TC + tt2 * 512:oc * TC + tt2 * 512 + 512],
                    ALU.add, ALU.add))
            for oc in range(NFC):
                buf = oc % 2
                vector.wait_ge(acd.h, 16 * 8 * (oc + 1))
                if oc >= 2:
                    vector.wait_ge(mmf.h, 2 * (oc - 1))
                cvt.inc(vector.tensor_scalar(wsum_bf[:, buf * NFC * 128:(buf + 1) * NFC * 128],
                                             wf_buf[:, buf * NFC * 128:(buf + 1) * NFC * 128],
                                             1.0 / math.sqrt(8.0), None, ALU.mult))
                if oc >= 1:
                    evac(oc - 1, 0)
                    evac(oc - 1, 1)
            evac(NFC - 1, 0)
            evac(NFC - 1, 1)

        @block.tensor
        def _(tensor):
            tensor.wait_ge(ld.h, xn1_ld)
            for oc in range(NFC):
                buf = oc % 2
                tensor.wait_ge(cvt.h, oc + 1)
                for tt2 in range(2):
                    k = 2 * oc + tt2
                    if k >= 2:
                        tensor.wait_ge(evf.h, k - 1)
                    for fc in range(NFC):
                        mm = tensor.matmul(f_ps[k % 2][:, :],
                                           wsum_bf[:, (buf * NFC + fc) * 128:(buf * NFC + fc + 1) * 128],
                                           xn1[:, fc * TC + tt2 * 512:fc * TC + tt2 * 512 + 512],
                                           start=(fc == 0), stop=(fc == NFC - 1))
                    mmf.inc(mm)

        @block.sync
        def _(sync):
            for oc in range(NFC):
                for tt2 in range(2):
                    k = 2 * oc + tt2
                    sync.wait_ge(evf.h, k + 1)
                    stf.inc(sync.dma_start(out=out[oc * 128:(oc + 1) * 128, tt2 * 512:(tt2 + 1) * 512],
                                           in_=ostage[:, (k % 2) * 512:(k % 2 + 1) * 512]), 16)
            sync.wait_ge(stf.h, 16 * 2 * NFC)
    return b




def build_fused():
    b = B()
    nc = b.nc
    x1_spill = b.dram("x1_spill", [D, TC], F32)
    dbg_spill = b.dout("spill_dbg", [D, TC], F32) if _DBG_SPILL else None
    dbg_y = b.dout("y_dbg", [128, NCHUNK * D_INNER], BF16) if _DBG_SPILL else None
    dbg_y2 = b.dout("y2_dbg", [128, NCHUNK * D_INNER], BF16) if _DBG_SPILL else None
    attn_scr = b.dram("attn_scr", [D, TC], F32)
    out_ap = b.dout("out", [D, TC], F32)

    es = b.es
    with es:
        block = es.enter_context(nc.Block())
        xn = b.sbuf("xn_shared", [128, NFC * TC], BF16)
        sc_l0 = ExitStack()
        sc_at = ExitStack()
        spl = b.sem("spl")
        x_in = b.din("x_fm", [D, TC], F32)
        wq_in = b.din("wq", [HPC, D, 128], BF16)
        wk_in = b.din("wk", [D, 128], BF16)
        wv_in = b.din("wv", [D, 128], BF16)
        wo_in = b.din("wo", [NHEADS_Q * HD, D], BF16)
        cos_in = b.din("cos_t", [64, T], BF16)
        sin_in = b.din("sin_t", [64, T], BF16)
        masks_in = b.din("masks", [4, 128, 512], F32)
        ident_in = b.din("ident", [128, 128], F32)
        identb_in = b.din("ident_bf", [128, 128], BF16)


        ag_in = b.dram("xn_ag_in", [D, TC], BF16)
        ag_out = b.dram("xn_ag_out", [NCORES * D, TC], BF16)
        oag_in = b.dram("o_ag_in", [HPC * HD, T], BF16)
        oag_out = b.dram("o_ag_out", [NHEADS_Q * HD, T], BF16)

        ld = b.sem("ld")
        st = b.sem("st")
        cc = b.sem("cc")
        gx1 = b.sem("gx1")

        ones_col = b.sbuf("ones_col", [128, 1], F32)
        ones_row = b.sbuf("ones_row", [1, 128], F32)
        ones_bf = b.sbuf("ones_bf", [128, 128], BF16)
        cst = b.sbuf("cst", [128, 4], F32)
        ident_f = b.sbuf("ident_f", [128, 128], F32)
        ident_b = b.sbuf("ident_b", [128, 128], BF16)
        cset = b.sem("cset")

        @block.vector
        def _(vector):
            vector.memset(ones_col[:, :], 1.0)
            vector.memset(ones_row[:, :], 1.0)
            vector.memset(cst[:, 0:1], 1e-5)
            cset.inc(vector.memset(ones_bf[:, :], 1.0))

        sc1 = ExitStack()
        xs = b.sbuf("xs", [128, NFC * TC], F32, es=sc1)
        scratch = b.sbuf("lnscr", [128, 2 * TC], F32, es=sc1)
        stats = b.sbuf("lnstats", [1, 3 * TC], F32, es=sc1)

        @block.sync
        def _(sync):
            ld.inc(sync.dma_start(out=fm_view(xs[:, :], NFC), in_=fm_dram(x_in)), 16)
            ld.inc(sync.dma_start(out=ident_f[:, :], in_=ident_in), 16)
            ld.inc(sync.dma_start(out=ident_b[:, :], in_=identb_in), 16)

        ln_done = b.sem("ln_done")
        emit_ln_fm(b, block, xs=xs, xn=xn, tokens=TC, scratch=scratch, stats=stats,
                   ones_col=ones_col, ones_row=ones_row, eps_ap=cst[0:1, 0:1],
                   ready=[(ld.h, 48), (cset.h, 1)], done=ln_done)

        @block.gpsimd
        def _(g):
            g.wait_ge(ln_done.h, 1)
            st.inc(g.dma_start(out=fm_dram(ag_in.ap()), in_=fm_sbuf_to_dram(xn[:, :], NFC)), 16)
            g.wait_ge(st.h, 16)
            cc.inc(g.collective_compute(
                "AllGather", ALU.bypass, replica_groups=[list(range(NCORES))],
                ins=[ag_in.ap().opt()], outs=[ag_out.ap().opt()]))

        sc1.close()

        q_s = b.sbuf("q_s", [128, HPC * T], BF16, es=sc_at)
        k_s = b.sbuf("k_s", [128, T], BF16, es=sc_at)
        v_tm = b.sbuf("v_tm", [128, T], BF16, es=sc_at)
        sc2b = ExitStack()
        wq_s = b.sbuf("wq_s", [128, HPC * NFC * 128], BF16, es=sc2b)
        wk_s = b.sbuf("wk_s", [128, NFC * 128], BF16, es=sc2b)
        wv_s = b.sbuf("wv_s", [128, NFC * 128], BF16, es=sc2b)
        cos_s = b.sbuf("cos_s", [64, T], BF16, es=sc2b)
        sin_s = b.sbuf("sin_s", [64, T], BF16, es=sc2b)

        @block.sync
        def _(sync):
            sync.wait_ge(st.h, 16)
            ld.inc(sync.dma_start(out=wq_s[:, :].rearrange("p (j c d) -> p j c d", j=HPC, c=NFC), in_=wq_in.rearrange("j (c p) d -> p j c d", p=128)), 16)
            ld.inc(sync.dma_start(out=fm_view(wk_s[:, :], NFC), in_=fm_dram(wk_in)), 16)
            ld.inc(sync.dma_start(out=fm_view(wv_s[:, :], NFC), in_=fm_dram(wv_in)), 16)
            ld.inc(sync.dma_start(out=cos_s[:, :], in_=cos_in), 16)
            ld.inc(sync.dma_start(out=sin_s[:, :], in_=sin_in), 16)
        w_loaded = ld.n

        xt = b.sbuf("xt", [128, 2 * NFC * 512], BF16, es=sc2b)
        rsc = b.sbuf("rsc", [64, 2 * 512], F32, es=sc2b)
        vfm = b.sbuf("vfm", [128, 2 * 512], BF16, es=sc2b)

        q_ps0 = b.psum("q_ps0", [128, 512], F32, es=sc2b)
        q_ps1 = b.psum("q_ps1", [128, 512], F32, es=sc2b)
        k_ps = b.psum("k_ps", [128, 512], F32, es=sc2b)
        v_ps = b.psum("v_ps", [128, 512], F32, es=sc2b)
        t_ps = b.psum("t_ps", [128, 128], BF16, es=sc2b)

        ldq = b.sem("ldq")
        mmk = b.sem("mmk")
        rp = b.sem("rp")
        tvt = b.sem("tvt")
        tve = b.sem("tve")

        @block.sync
        def _(sync):
            sync.wait_ge(cc.h, 1)
            for tt in range(NQT):
                buf = tt % 2
                if tt >= 2:
                    sync.wait_ge(rp.h, 4 * (tt - 1))
                src = ag_out.ap()[D * (tt // 2):D * (tt // 2 + 1),
                                  (tt % 2) * 512:(tt % 2) * 512 + 512]
                ldq.inc(sync.dma_start(out=fm_view(xt[:, buf * NFC * 512:(buf + 1) * NFC * 512], NFC),
                                       in_=fm_dram(src)), 16)

        @block.tensor
        def _(tensor):
            tensor.wait_ge(ld.h, w_loaded)
            for tt in range(NQT):
                buf = tt % 2
                xsl = lambda fc: xt[:, (buf * NFC + fc) * 512:(buf * NFC + fc + 1) * 512]
                tensor.wait_ge(ldq.h, 16 * (tt + 1))
                if tt >= 1:
                    tensor.wait_ge(rp.h, 4 * tt)
                for jps, wsl in ((q_ps0, 0), (q_ps1, 1)):
                    for fc in range(NFC):
                        mm = tensor.matmul(jps[:, :],
                                           wq_s[:, (wsl * NFC + fc) * 128:(wsl * NFC + fc + 1) * 128],
                                           xsl(fc), start=(fc == 0), stop=(fc == NFC - 1))
                    mmk.inc(mm)
                for fc in range(NFC):
                    mm = tensor.matmul(k_ps[:, :], wk_s[:, fc * 128:(fc + 1) * 128],
                                       xsl(fc), start=(fc == 0), stop=(fc == NFC - 1))
                mmk.inc(mm)
                for fc in range(NFC):
                    mm = tensor.matmul(v_ps[:, :], wv_s[:, fc * 128:(fc + 1) * 128],
                                       xsl(fc), start=(fc == 0), stop=(fc == NFC - 1))
                mmk.inc(mm)
                tts = [tt - 1] if tt >= 1 else []
                if tt == NQT - 1:
                    tts.append(tt)
                for tp in tts:
                    tensor.wait_ge(rp.h, 4 * tp + 4)
                    for i in range(4):
                        idx = tp * 4 + i
                        if idx >= 1:
                            tensor.wait_ge(tve.h, idx)
                        tvt.inc(tensor.transpose(t_ps[:, :],
                                                 vfm[:, (tp % 2) * 512 + i * 128:(tp % 2) * 512 + (i + 1) * 128],
                                                 ident_b[:, :]))

        def rope_emit(vector, ps, dst, tt):
            c_sl = cos_s[:, tt * 512:(tt + 1) * 512]
            s_sl = sin_s[:, tt * 512:(tt + 1) * 512]
            e, o = ps[0:64, :], ps[64:128, :]
            t1 = rsc[:, 0 * 512:1 * 512]
            t2 = rsc[:, 1 * 512:2 * 512]
            vector.tensor_tensor(t1, e, c_sl, ALU.mult)
            vector.tensor_tensor(t2, o, s_sl, ALU.mult)
            vector.tensor_tensor(dst[0:64, :], t1, t2, ALU.subtract)
            vector.tensor_tensor(t1, e, s_sl, ALU.mult)
            vector.tensor_tensor(t2, o, c_sl, ALU.mult)
            return vector.tensor_tensor(dst[64:128, :], t1, t2, ALU.add)

        @block.vector
        def _(vector):
            for tt in range(NQT):
                buf = tt % 2
                base = 4 * tt
                vector.wait_ge(mmk.h, base + 1)
                rp.inc(rope_emit(vector, q_ps0, q_s[:, 0 * T + tt * 512:0 * T + tt * 512 + 512], tt))
                vector.wait_ge(mmk.h, base + 2)
                rp.inc(rope_emit(vector, q_ps1, q_s[:, 1 * T + tt * 512:1 * T + tt * 512 + 512], tt))
                vector.wait_ge(mmk.h, base + 3)
                rp.inc(rope_emit(vector, k_ps, k_s[:, tt * 512:(tt + 1) * 512], tt))
                vector.wait_ge(mmk.h, base + 4)
                if tt >= 2:
                    vector.wait_ge(tvt.h, 4 * (tt - 1))
                rp.inc(vector.tensor_copy(vfm[:, buf * 512:(buf + 1) * 512], v_ps[:, :]))
                tps = [tt - 1] if tt >= 1 else []
                if tt == NQT - 1:
                    tps.append(tt)
                for tp in tps:
                    for i in range(4):
                        idx = tp * 4 + i
                        vector.wait_ge(tvt.h, idx + 1)
                        tve.inc(vector.tensor_copy(v_tm[:, idx * 128:(idx + 1) * 128], t_ps[:, :]))

        sc2b.close()

        p_s = b.sbuf("p_s", [128, 2 * 512], BF16, es=sc_at)
        mask_s = b.sbuf("mask_s", [128, 4 * 512], F32, es=sc_at)
        o_stage = b.sbuf("o_stage", [128, 2 * 512], BF16, es=sc_at)
        nrm_s = b.sbuf("nrm_s", [128, 512], F32, es=sc_at)

        @block.sync
        def _(sync):
            sync.wait_ge(tve.h, 4 * NQT)
            ld.inc(sync.dma_start(out=mask_s[:, :].rearrange("p (d t) -> p d t", d=4), in_=masks_in.rearrange("d p t -> p d t")), 16)
        mask_ld = ld.n

        s_ps = [b.psum("s_ps0", [128, 512], F32, es=sc_at), b.psum("s_ps1", [128, 512], F32, es=sc_at)]
        o_ps = b.psum("o_ps", [128, 512], F32, es=sc_at)
        n_ps = b.psum("n_ps", [128, 512], F32, es=sc_at)

        sS = b.sem("sS")
        sM = b.sem("sM")
        sP = b.sem("sP")
        sC = b.sem("sC")
        sE = b.sem("sE")
        sO = b.sem("sO")

        jobs = []
        for j in range(HPC):
            for qt in range(NQT):
                jobs.append((j, qt, 4 * qt + 4, 4 * qt))
        iters = []
        for jb, (j, qt, nkc, diag0) in enumerate(jobs):
            for kc in range(nkc):
                iters.append(dict(jb=jb, j=j, qt=qt, kc=kc, first=(kc == 0),
                                  last=(kc == nkc - 1), diag=(kc >= diag0),
                                  delta=kc - diag0))
        nd_cum = np.cumsum([1 if it["diag"] else 0 for it in iters]).tolist()
        qkv_rp = 4 * NQT
        qkv_tve = 4 * NQT

        @block.tensor
        def _(tensor):
            tensor.wait_ge(rp.h, qkv_rp)
            tensor.wait_ge(tve.h, qkv_tve)
            tensor.wait_ge(ld.h, mask_ld)
            for i, it in enumerate(iters):
                q_sl = q_s[:, it["j"] * T + it["qt"] * 512: it["j"] * T + it["qt"] * 512 + 512]
                if i >= 2:
                    tensor.wait_ge(sP.h, i - 1)
                sS.inc(tensor.matmul(s_ps[i % 2][:, :],
                                     k_s[:, it["kc"] * 128:(it["kc"] + 1) * 128],
                                     q_sl, start=True, stop=True))
                if i >= 1:
                    pit = iters[i - 1]
                    if pit["first"]:
                        tensor.wait_ge(sE.h, pit["jb"])
                    tensor.wait_ge(sP.h, i)
                    tensor.matmul(o_ps[:, :], v_tm[:, pit["kc"] * 128:(pit["kc"] + 1) * 128],
                                  p_s[:, ((i - 1) % 2) * 512:((i - 1) % 2 + 1) * 512],
                                  start=pit["first"], stop=pit["last"])
                    sC.inc(tensor.matmul(n_ps[:, :], ones_bf[:, :],
                                         p_s[:, ((i - 1) % 2) * 512:((i - 1) % 2 + 1) * 512],
                                         start=pit["first"], stop=pit["last"]))
            i = len(iters)
            pit = iters[i - 1]
            if pit["first"]:
                tensor.wait_ge(sE.h, pit["jb"])
            tensor.wait_ge(sP.h, i)
            tensor.matmul(o_ps[:, :], v_tm[:, pit["kc"] * 128:(pit["kc"] + 1) * 128],
                          p_s[:, ((i - 1) % 2) * 512:((i - 1) % 2 + 1) * 512],
                          start=pit["first"], stop=pit["last"])
            sC.inc(tensor.matmul(n_ps[:, :], ones_bf[:, :],
                                 p_s[:, ((i - 1) % 2) * 512:((i - 1) % 2 + 1) * 512],
                                 start=pit["first"], stop=pit["last"]))

        @block.vector
        def _(vector):
            i = 0
            for jb, (j, qt, nkc, diag0) in enumerate(jobs):
                for kc in range(nkc):
                    it = iters[i]
                    if it["diag"]:
                        vector.wait_ge(sS.h, i + 1)
                        sM.inc(vector.tensor_tensor(
                            s_ps[i % 2][:, :], s_ps[i % 2][:, :],
                            mask_s[:, it["delta"] * 512:(it["delta"] + 1) * 512], ALU.add))
                    i += 1
                vector.wait_ge(sC.h, i)
                if jb >= 2:
                    vector.wait_ge(sO.h, 16 * (jb - 1))
                vector.reciprocal(nrm_s[:, :], n_ps[:, :])
                sE.inc(vector.tensor_tensor(o_stage[:, (jb % 2) * 512:(jb % 2 + 1) * 512],
                                            o_ps[:, :], nrm_s[:, :], ALU.mult))

        @block.scalar
        def _(scalar):
            for i, it in enumerate(iters):
                if it["diag"]:
                    scalar.wait_ge(sM.h, nd_cum[i])
                else:
                    scalar.wait_ge(sS.h, i + 1)
                if i >= 2:
                    scalar.wait_ge(sC.h, i - 1)
                sP.inc(scalar.activation(p_s[:, (i % 2) * 512:(i % 2 + 1) * 512],
                                         s_ps[i % 2][:, :], AF.Exp))

        @block.gpsimd
        def _(g):
            for jb, (j, qt, nkc, diag0) in enumerate(jobs):
                g.wait_ge(sE.h, jb + 1)
                sO.inc(g.dma_start(out=oag_in.ap()[j * 128:(j + 1) * 128, qt * 512:(qt + 1) * 512],
                                   in_=o_stage[:, (jb % 2) * 512:(jb % 2 + 1) * 512]), 16)
            g.wait_ge(sO.h, 16 * len(jobs))
            cc.inc(g.collective_compute(
                "AllGather", ALU.bypass, replica_groups=[list(range(NCORES))],
                ins=[oag_in.ap().opt()], outs=[oag_out.ap().opt()]))

        tok_in = b.din("tok_off", [1, 1], mybir.dt.int32)
        oa_s = b.sbuf("oa_s", [128, 2 * NFC * 512], BF16, es=sc_at)
        wo_s = b.sbuf("wo_s", [128, 2 * NFC * 128], BF16, es=sc_at)
        r_stage = b.sbuf("r_stage", [128, 2 * 512], F32, es=sc_at)
        stok = b.sbuf("stok", [1, 1], mybir.dt.int32, es=sc_at)

        r_ps = [b.psum("r_ps0", [128, 512], F32, es=sc_at), b.psum("r_ps1", [128, 512], F32, es=sc_at)]
        lo = b.sem("lo")
        lw = b.sem("lw")
        mo = b.sem("mo")
        re = b.sem("re")
        ro = b.sem("ro")

        @block.sync
        def _(sync):
            ld.inc(sync.dma_start(out=stok[:, :], in_=tok_in), 16)
            for oc in range(NFC):
                if oc >= 2:
                    sync.wait_ge(mo.h, 2 * (oc - 1))
                lw.inc(sync.dma_start(
                    out=fm_view(wo_s[:, (oc % 2) * NFC * 128:(oc % 2 + 1) * NFC * 128], NFC),
                    in_=fm_dram(wo_in[:, oc * 128:(oc + 1) * 128])), 16)
        tok_ld = ld.n

        @block.gpsimd
        def _(g):
            g.wait_ge(cc.h, 2)
            g.wait_ge(ld.h, tok_ld)
            with g.register("r_tok") as r:
                g.reg_load(r, stok[0:1, 0:1])
                for tt2 in range(2):
                    ofs = g.snap(r, min_val=0, max_val=T - 512)
                    src = oag_out.ap().rearrange("(h p) t -> p h t", p=128)[:, :, bass.ds(ofs, 512)]
                    dst = oa_s[:, tt2 * NFC * 512:(tt2 + 1) * NFC * 512].rearrange("p (h t) -> p h t", h=NFC)
                    lo.inc(g.dma_start(out=dst, in_=src), 16)
                    if tt2 == 0:
                        g.reg_add(r, r, 512)

        @block.tensor
        def _(tensor):
            tensor.wait_ge(lo.h, 32)
            for oc in range(NFC):
                tensor.wait_ge(lw.h, 16 * (oc + 1))
                for tt2 in range(2):
                    if oc >= 1:
                        tensor.wait_ge(re.h, 2 * (oc - 1) + tt2 + 1)
                    for hd in range(NFC):
                        mm = tensor.matmul(
                            r_ps[tt2][:, :],
                            wo_s[:, ((oc % 2) * NFC + hd) * 128:((oc % 2) * NFC + hd + 1) * 128],
                            oa_s[:, (tt2 * NFC + hd) * 512:(tt2 * NFC + hd + 1) * 512],
                            start=(hd == 0), stop=(hd == NFC - 1))
                    mo.inc(mm)

        @block.vector
        def _(vector):
            for oc in range(NFC):
                for tt2 in range(2):
                    vector.wait_ge(mo.h, 2 * oc + tt2 + 1)
                    if oc >= 1:
                        vector.wait_ge(ro.h, 16 * (2 * (oc - 1) + tt2 + 1))
                    re.inc(vector.tensor_copy(r_stage[:, tt2 * 512:(tt2 + 1) * 512],
                                              r_ps[tt2][:, :]))

        @block.gpsimd
        def _(g):
            for oc in range(NFC):
                for tt2 in range(2):
                    g.wait_ge(re.h, 2 * oc + tt2 + 1)
                    ro.inc(g.dma_start(
                        out=attn_scr.ap()[oc * 128:(oc + 1) * 128, tt2 * 512:(tt2 + 1) * 512],
                        in_=r_stage[:, tt2 * 512:(tt2 + 1) * 512]), 16)

        attn_done_h, attn_done_v = ro.h, ro.n
        sc_at.close()
        x1a = b.sbuf("x1a", [128, NFC * TC], F32, es=sc_l0)
        rT_in = b.din("rT", [D, 8], BF16)
        sgT_in = b.din("sgT", [D, MOE_I], BF16)
        suT_in = b.din("suT", [D, MOE_I], BF16)
        sdT_in = b.din("sdT", [MOE_I, D], BF16)
        wgT_in = b.din("wgT", [8, D, MOE_I], BF16)
        wuT_in = b.din("wuT", [8, D, MOE_I], BF16)
        wdT_in = b.din("wdT", [8, MOE_I, D], BF16)
        sel8_in = b.din("sel8", [8, 8 * 128], BF16)
    

        ld = b.sem("ld")
        gl = b.sem("gl")
        st = b.sem("st")

        cset = b.sem("cset")

        @block.vector
        def _(vector):
            cset.inc(vector.memset(cst[:, 1:2], 0.0))

        m1 = b.sem("m1")

        @block.gpsimd
        def _(g):
            g.wait_ge(attn_done_h, attn_done_v)
            gl.inc(g.dma_start(out=fm_view(x1a[:, :], NFC), in_=fm_dram(x_in)), 16)
            g.wait_ge(gl.h, gl.n)
            gl.inc(g.dma_start(out=fm_view(x1a[:, :], NFC), in_=fm_dram(attn_scr.ap()),
                               accum_op=ALU.add), 16)
        moe_gl = gl.n

        @block.vector
        def _(vector):
            vector.wait_ge(gl.h, moe_gl)
            for fc in range(NFC):
                last = vector.tensor_tensor(x1a[:, fc * TC:(fc + 1) * TC],
                                            x1a[:, fc * TC:(fc + 1) * TC],
                                            xn[:, fc * TC:(fc + 1) * TC], ALU.add)
            m1.inc(last)

        rT_s = b.sbuf("rT_s", [128, NFC * 8], BF16, es=sc_l0)
        sel_s = b.sbuf("sel_s", [8, 8 * 128], BF16, es=sc_l0)
        comb_tm = b.sbuf("comb_tm", [128, 8 * 8], F32, es=sc_l0)
        comb_fm = b.sbuf("comb_fm", [8, TC], BF16, es=sc_l0)
        rsc = b.sbuf("moe_rsc", [128, 8 * 8], F32, es=sc_l0)
        rcol = b.sbuf("rcol", [128, 8], F32, es=sc_l0)

        @block.sync
        def _(sync):
            sync.wait_ge(attn_done_h, attn_done_v)
            ld.inc(sync.dma_start(out=rT_s[:, :].rearrange("p (c e) -> p c e", c=NFC),
                                  in_=rT_in.rearrange("(c p) e -> p c e", p=128)), 16)
            ld.inc(sync.dma_start(out=sel_s[:, :], in_=sel8_in), 16)
        r_ld = ld.n

        sc_r = ExitStack()
        r_ps = b.psum("r_ps", [128, 8], F32, es=sc_r)
        t_ps = b.psum("t_ps", [8, 128], F32, es=sc_r)

        rm = b.sem("rm")
        rsv = b.sem("rsv")
        rse = b.sem("rse")
        rt = b.sem("rt")
        rf = b.sem("rf")

        @block.tensor
        def _(tensor):
            tensor.wait_ge(ld.h, r_ld)
            for tc in range(8):
                if tc >= 1:
                    tensor.wait_ge(rsv.h, 2 * tc - 1)
                for fc in range(NFC):
                    mm = tensor.matmul(r_ps[:, :], xn[:, fc * TC + tc * 128:fc * TC + (tc + 1) * 128],
                                       rT_s[:, fc * 8:(fc + 1) * 8],
                                       start=(fc == 0), stop=(fc == NFC - 1))
                rm.inc(mm)

        vch = b.sem("vch")

        @block.vector
        def _(vector):
            def ch(inst):
                vch.inc(inst)
                vector.wait_ge(vch.h, vch.n)
            for tc in range(8):
                w = rsc[:, 0:8]
                ebuf = rsc[:, 8:16]
                tbuf = rsc[:, 16:24]
                vector.wait_ge(rm.h, tc + 1)
                ch(vector.reduce_max(rcol[:, 0:1], r_ps[:, :], axis=mybir.AxisListType.X))
                rsv.inc(vector.tensor_scalar(w, r_ps[:, :], rcol[:, 0:1], None, ALU.subtract))
                vector.wait_ge(rse.h, tc + 1)
                ch(vector.reciprocal(rcol[:, 1:2], rcol[:, 2:3]))
                ch(vector.tensor_scalar(w, ebuf, rcol[:, 1:2], None, ALU.mult))
                ch(vector.reduce_max(rcol[:, 3:4], w, axis=mybir.AxisListType.X))
                ch(vector.tensor_scalar(tbuf, w, rcol[:, 3:4], None, ALU.is_ge))
                ch(vector.scalar_tensor_tensor(tbuf, tbuf, -1e9, w, ALU.mult, ALU.add))
                ch(vector.reduce_max(rcol[:, 5:6], tbuf, axis=mybir.AxisListType.X))
                ch(vector.tensor_scalar(tbuf, w, rcol[:, 5:6], None, ALU.is_ge))
                rsv.inc(vector.tensor_tensor(comb_tm[:, tc * 8:(tc + 1) * 8], w, tbuf, ALU.mult))

        @block.scalar
        def _(scalar):
            for tc in range(8):
                scalar.wait_ge(rsv.h, 2 * tc + 1)
                if tc >= 1:
                    scalar.wait_ge(rsv.h, 2 * tc)
                rse.inc(scalar.activation(rsc[:, 8:16], rsc[:, 0:8], AF.Exp,
                                          accum_out=rcol[:, 2:3]))

        @block.tensor
        def _(tensor):
            for tc in range(8):
                tensor.wait_ge(rsv.h, 2 * (tc + 1))
                if tc >= 1:
                    tensor.wait_ge(rf.h, tc)
                rt.inc(tensor.transpose(t_ps[:, :], comb_tm[:, tc * 8:(tc + 1) * 8], ident_f[:, :]))

        @block.vector
        def _(vector):
            for tc in range(8):
                vector.wait_ge(rt.h, tc + 1)
                rf.inc(vector.tensor_copy(comb_fm[:, tc * 128:(tc + 1) * 128], t_ps[:, :]))

        sc_exp = ExitStack()
        he_buf = b.sbuf("he_buf", [128, NIC * TC], BF16, es=sc_exp)
        wg_buf = b.sbuf("wg_buf", [128, 2 * NFC * 128], BF16, es=sc_exp)
        wu_buf = b.sbuf("wu_buf", [128, 2 * NFC * 128], BF16, es=sc_exp)
        wd_buf = b.sbuf("wd_buf", [128, 2 * NIC * 128], BF16, es=sc_exp)
        sig_buf = b.sbuf("sig_buf", [128, 2 * 512], F32, es=sc_exp)
        tsc_buf = b.sbuf("tsc_buf", [128, 512], F32, es=sc_exp)
        comb_bc = b.sbuf("comb_bc", [128, TC], F32, es=sc_exp)

        sc_r.close()
        bc_ps = b.psum("bc_ps", [128, 1024], F32, es=sc_exp)
        g_ps = [b.psum("g_ps0", [128, 512], F32, es=sc_exp), b.psum("g_ps1", [128, 512], F32, es=sc_exp)]
        u_ps = [b.psum("u_ps0", [128, 512], F32, es=sc_exp), b.psum("u_ps1", [128, 512], F32, es=sc_exp)]
        d_ps = [b.psum("d_ps0", [128, 512], F32, es=sc_exp), b.psum("d_ps1", [128, 512], F32, es=sc_exp)]

        lwg = b.sem("lwg")
        lwd = b.sem("lwd")
        mmg = b.sem("mmg")
        sg = b.sem("sg")
        he = b.sem("he")
        cbc = b.sem("cbc")
        mmd = b.sem("mmd")
        dev_ = b.sem("dev")

        NEXP = 9

        def gu_src(e, kind, ic):
            if e == 8:
                base = sgT_in if kind == "g" else suT_in
                sl = base[:, ic * 128:(ic + 1) * 128]
            else:
                base = wgT_in if kind == "g" else wuT_in
                sl = base[e, :, ic * 128:(ic + 1) * 128]
            return sl.rearrange("(c p) d -> p c d", p=128)

        def wd_src(e, oc):
            if e == 8:
                sl = sdT_in[:, oc * 128:(oc + 1) * 128]
            else:
                sl = wdT_in[e, :, oc * 128:(oc + 1) * 128]
            return sl.rearrange("(c p) d -> p c d", p=128)

        @block.sync
        def _(sync):
            k = 0
            for e in range(NEXP):
                for ic in range(NIC):
                    if k >= 2:
                        sync.wait_ge(mmg.h, 4 * (k - 1))
                    buf = k % 2
                    lwg.inc(sync.dma_start(
                        out=wg_buf[:, buf * NFC * 128:(buf + 1) * NFC * 128].rearrange("p (c d) -> p c d", c=NFC),
                        in_=gu_src(e, "g", ic)), 16)
                    lwg.inc(sync.dma_start(
                        out=wu_buf[:, buf * NFC * 128:(buf + 1) * NFC * 128].rearrange("p (c d) -> p c d", c=NFC),
                        in_=gu_src(e, "u", ic)), 16)
                    k += 1

        @block.gpsimd
        def _(g):
            kd = 0
            for e in range(NEXP):
                for oc in range(NFC):
                    if kd >= 2:
                        g.wait_ge(mmd.h, 2 * (kd - 1))
                    buf = kd % 2
                    lwd.inc(g.dma_start(
                        out=wd_buf[:, buf * NIC * 128:(buf + 1) * NIC * 128].rearrange("p (c d) -> p c d", c=NIC),
                        in_=wd_src(e, oc)), 16)
                    kd += 1

        @block.tensor
        def _(tensor):
            k = 0
            kt = 0
            kd = 0
            kdt = 0
            for e in range(NEXP):
                if e < 8:
                    tensor.wait_ge(rf.h, 8)
                    if e >= 1:
                        tensor.wait_ge(he.h, 2 * NIC * e)
                    for th in range(2):
                        cbc.inc(tensor.matmul(bc_ps[:, th * 512:(th + 1) * 512],
                                              sel_s[:, e * 128:(e + 1) * 128],
                                              comb_fm[:, th * 512:(th + 1) * 512],
                                              start=True, stop=True))
                for ic in range(NIC):
                    tensor.wait_ge(lwg.h, 32 * (k + 1))
                    buf = k % 2
                    for th in range(2):
                        pb = kt % 2
                        if kt >= 2:
                            tensor.wait_ge(he.h, kt - 1)
                        for fc in range(NFC):
                            mm = tensor.matmul(g_ps[pb][:, :],
                                               wg_buf[:, (buf * NFC + fc) * 128:(buf * NFC + fc + 1) * 128],
                                               xn[:, fc * TC + th * 512:fc * TC + th * 512 + 512],
                                               start=(fc == 0), stop=(fc == NFC - 1))
                        mmg.inc(mm)
                        for fc in range(NFC):
                            mm = tensor.matmul(u_ps[pb][:, :],
                                               wu_buf[:, (buf * NFC + fc) * 128:(buf * NFC + fc + 1) * 128],
                                               xn[:, fc * TC + th * 512:fc * TC + th * 512 + 512],
                                               start=(fc == 0), stop=(fc == NFC - 1))
                        mmg.inc(mm)
                        kt += 1
                    k += 1
                tensor.wait_ge(he.h, 2 * NIC * (e + 1))
                for oc in range(NFC):
                    tensor.wait_ge(lwd.h, 16 * (kd + 1))
                    bufd = kd % 2
                    for th in range(2):
                        pb = kdt % 2
                        if kdt >= 2:
                            tensor.wait_ge(dev_.h, kdt - 1)
                        for ic in range(NIC):
                            mm = tensor.matmul(d_ps[pb][:, :],
                                               wd_buf[:, (bufd * NIC + ic) * 128:(bufd * NIC + ic + 1) * 128],
                                               he_buf[:, ic * TC + th * 512:ic * TC + th * 512 + 512],
                                               start=(ic == 0), stop=(ic == NIC - 1))
                        mmd.inc(mm)
                        kdt += 1
                    kd += 1

        @block.scalar
        def _(scalar):
            kt = 0
            for e in range(NEXP):
                for ic in range(NIC):
                    for th in range(2):
                        scalar.wait_ge(mmg.h, 2 * kt + 1)
                        if kt >= 2:
                            scalar.wait_ge(he.h, kt - 1)
                        sg.inc(scalar.activation(sig_buf[:, (kt % 2) * 512:(kt % 2 + 1) * 512],
                                                 g_ps[kt % 2][:, :], AF.Sigmoid))
                        kt += 1

        @block.vector
        def _(vector):
            kt = 0
            kdt = 0
            for e in range(NEXP):
                if e < 8:
                    for th in range(2):
                        vector.wait_ge(cbc.h, 2 * e + th + 1)
                        vector.tensor_copy(comb_bc[:, th * 512:(th + 1) * 512],
                                           bc_ps[:, th * 512:(th + 1) * 512])
                for ic in range(NIC):
                    for th in range(2):
                        pb = kt % 2
                        vector.wait_ge(sg.h, kt + 1)
                        vector.wait_ge(mmg.h, 2 * kt + 2)
                        vector.tensor_tensor(tsc_buf[:, :], sig_buf[:, pb * 512:(pb + 1) * 512],
                                             u_ps[pb][:, :], ALU.mult)
                        dst = he_buf[:, ic * TC + th * 512:ic * TC + th * 512 + 512]
                        if e < 8:
                            he.inc(vector.tensor_tensor(dst, tsc_buf[:, :],
                                                        comb_bc[:, th * 512:(th + 1) * 512], ALU.mult))
                        else:
                            he.inc(vector.tensor_copy(dst, tsc_buf[:, :]))
                        kt += 1
                for oc in range(NFC):
                    for th in range(2):
                        pb = kdt % 2
                        vector.wait_ge(mmd.h, kdt + 1)
                        vector.wait_ge(m1.h, 1)
                        dev_.inc(vector.tensor_tensor(
                            x1a[:, oc * TC + th * 512:oc * TC + th * 512 + 512],
                            x1a[:, oc * TC + th * 512:oc * TC + th * 512 + 512],
                            d_ps[pb][:, :], ALU.add))
                        kdt += 1

        sc_exp.close()
        sc_ln = ExitStack()
        scratch1 = b.sbuf("lnscr1", [128, 2 * TC], F32, es=sc_ln)
        stats1 = b.sbuf("lnstats1", [1, 3 * TC], F32, es=sc_ln)
        ln1_done = b.sem("ln1_done")
        emit_ln_fm(b, block, xs=x1a, xn=xn, tokens=TC, scratch=scratch1, stats=stats1,
                   ones_col=ones_col, ones_row=ones_row, eps_ap=cst[0:1, 0:1],
                   ready=[(dev_.h, 2 * NFC * NEXP), (cset.h, 1)], done=ln1_done)
        sc_ln.close()


        fac_in = b.din("factors", [8, D, D], F32)
        bias_in = b.din("l1_bias", [128, NFC], F32)

        ld = b.sem("ld")
        gl = b.sem("gl")
        acd = b.sem("acd")
        mmf = b.sem("mmf")
        evf = b.sem("evf")
        stf = b.sem("stf")
        cvt = b.sem("cvt")

        bias_s = b.sbuf("bias_s", [128, NFC], F32, es=sc_l0)
        wf_buf = b.sbuf("wf_buf", [128, 2 * NFC * 128], F32, es=sc_l0)
        wsum_bf = b.sbuf("wsum_bf", [128, 2 * NFC * 128], BF16, es=sc_l0)
        f_ps = [b.psum("f_ps0", [128, 512], F32, es=sc_l0), b.psum("f_ps1", [128, 512], F32, es=sc_l0)]

        @block.sync
        def _(sync):
            sync.wait_ge(ln1_done.h, 1)
            ld.inc(sync.dma_start(out=bias_s[:, :], in_=bias_in), 16)
        xn1_ld = ld.n

        @block.gpsimd
        def _(g):
            g.wait_ge(ln1_done.h, 1)
            for oc in range(NFC):
                buf = oc % 2
                if oc >= 2:
                    g.wait_ge(cvt.h, oc - 1)
                dst = fm_view(wf_buf[:, buf * NFC * 128:(buf + 1) * NFC * 128], NFC)
                for r in range(8):
                    acd.inc(g.dma_start(out=dst,
                                        in_=fac_in[r, :, oc * 128:(oc + 1) * 128].rearrange("(c p) d -> p c d", p=128),
                                        accum_op=(ALU.add if r > 0 else ALU.bypass)), 16)
                    g.wait_ge(acd.h, acd.n)

        @block.vector
        def _(vector):
            def evac(oc, tt2):
                k = 2 * oc + tt2
                vector.wait_ge(mmf.h, k + 1)
                sl = slice(oc * TC + tt2 * 512, oc * TC + tt2 * 512 + 512)
                evf.inc(vector.scalar_tensor_tensor(
                    x1a[:, sl], f_ps[k % 2][:, :], bias_s[:, oc:oc + 1],
                    x1a[:, sl], ALU.add, ALU.add))
            for oc in range(NFC):
                buf = oc % 2
                vector.wait_ge(acd.h, 16 * 8 * (oc + 1))
                if oc >= 2:
                    vector.wait_ge(mmf.h, 2 * (oc - 1))
                cvt.inc(vector.tensor_scalar(wsum_bf[:, buf * NFC * 128:(buf + 1) * NFC * 128],
                                             wf_buf[:, buf * NFC * 128:(buf + 1) * NFC * 128],
                                             1.0 / math.sqrt(8.0), None, ALU.mult))
                if oc >= 1:
                    evac(oc - 1, 0)
                    evac(oc - 1, 1)
            evac(NFC - 1, 0)
            evac(NFC - 1, 1)

        @block.tensor
        def _(tensor):
            tensor.wait_ge(ld.h, xn1_ld)
            for oc in range(NFC):
                buf = oc % 2
                tensor.wait_ge(cvt.h, oc + 1)
                for tt2 in range(2):
                    k = 2 * oc + tt2
                    if k >= 2:
                        tensor.wait_ge(evf.h, k - 1)
                    for fc in range(NFC):
                        mm = tensor.matmul(f_ps[k % 2][:, :],
                                           wsum_bf[:, (buf * NFC + fc) * 128:(buf * NFC + fc + 1) * 128],
                                           xn[:, fc * TC + tt2 * 512:fc * TC + tt2 * 512 + 512],
                                           start=(fc == 0), stop=(fc == NFC - 1))
                    mmf.inc(mm)

        @block.gpsimd
        def _(g):
            g.wait_ge(evf.h, 2 * NFC)
            spl.inc(g.dma_start(out=fm_dram(x1_spill.ap()), in_=fm_sbuf_to_dram(x1a[:, :], NFC)), 16)



        sc_l0.close()
        wxbc_in = b.din("w_xbc", [D, CONV_DIM], BF16)
        wdt_in = b.din("w_dt", [D, 64], BF16)
        wz_in = b.din("w_z", [D, D_INNER], BF16)
        wout_in = b.din("w_out", [D_INNER, D], BF16)
        convw_in = b.din("conv_w", [128, NCC * 4], F32)
        convb_in = b.din("conv_b", [128, NCC], F32)
        dtb_in = b.din("dt_bias", [64, 1], F32)
        alog_in = b.din("A_log", [64, 1], F32)
        drow_in = b.din("D_row", [1, 64], F32)
        nw_in = b.din("nw_row", [1, D_INNER], F32)
        tril_in = b.din("tril", [128, 128], F32)
        mpref_in = b.din("m_pref", [128, 7], F32)
        whalo_in = b.din("w_halo", [128, 8], F32)


        halo_in_d = b.dram("halo_in", [128, NCC * 3], BF16)
        halo_out_d = b.dram("halo_out", [NCORES * 128, NCC * 3], BF16)
        carry_in_d = b.dram("carry_in", [128, D_INNER + 64], F32)
        carry_out_d = b.dram("carry_out", [NCORES * 128, D_INNER + 64], F32)
        y_scr_d = b.dram("y_scr", [128, NCHUNK * D_INNER], BF16)

        ld = b.sem("ld")
        gl = b.sem("gl")
        st = b.sem("st")
        cc = b.sem("cc")

        ones64 = b.sbuf("ones64", [64, 128], F32)
        i64 = b.sbuf("i64", [64, 64], F32)
        tril_s = b.sbuf("tril_s", [128, 128], F32)
        mpref_s = b.sbuf("mpref_s", [128, 7], F32)
        whalo_s = b.sbuf("whalo_s", [128, 8], F32)
        dtb_s = b.sbuf("dtb_s", [64, 1], F32)
        alog_s = b.sbuf("alog_s", [64, 1], F32)
        drow_s = b.sbuf("drow_s", [1, 64], F32)
        cset = b.sem("cset")

        @block.vector
        def _(vector):
            vector.wait_ge(spl.h, 16)
            vector.memset(ones64[:, :], 1.0)
            vector.memset(cst[:, 2:3], -1.0)
            cset.inc(vector.memset(cst[:, 3:4], 1.0))

        @block.sync
        def _(sync):
            sync.wait_ge(spl.h, 16)
            ld.inc(sync.dma_start(out=tril_s[:, :], in_=tril_in), 16)
            ld.inc(sync.dma_start(out=mpref_s[:, :], in_=mpref_in), 16)
            ld.inc(sync.dma_start(out=whalo_s[:, :], in_=whalo_in), 16)
            ld.inc(sync.dma_start(out=dtb_s[:, :], in_=dtb_in), 16)
            ld.inc(sync.dma_start(out=alog_s[:, :], in_=alog_in), 16)
            ld.inc(sync.dma_start(out=drow_s[:, :], in_=drow_in), 16)
            ld.inc(sync.dma_start(out=i64[:, :], in_=ident_in[0:64, 0:64]), 16)
        const_ld = ld.n

        xn1 = xn
        sdect_all = b.sbuf("sdect_all", [128, NCHUNK * 64], F32)
        dbc_all = b.sbuf("dbc_all", [128, NCHUNK * 64], F32)
        c_keep = b.sbuf("c_keep", [128, TC], BF16)
        sc_body = ExitStack()
        xbc = b.sbuf("xbc", [128, NCC * XBW], BF16, es=sc_body)
        convw_s = b.sbuf("convw_s", [128, NCC * 4], F32, es=sc_body)
        convb_s = b.sbuf("convb_s", [128, NCC], F32, es=sc_body)
        dt_raw = b.sbuf("dt_raw", [64, TC], F32, es=sc_body)
        dt_s = b.sbuf("dt_s", [64, TC], F32, es=sc_body)
        dsc = b.sbuf("dsc", [64, TC], F32, es=sc_body)
        a_cs = b.sbuf("a_cs", [64, TC], F32, es=sc_body)
        dec_s = b.sbuf("dec_s", [64, TC], F32, es=sc_body)
        aexp_s = b.sbuf("aexp_s", [64, 2], F32, es=sc_body)
        acst_all = b.sbuf("acst_all", [128, NCHUNK * 64], F32, es=sc_body)
        dtt_all = b.sbuf("dtt_all", [128, NCHUNK * 64], F32, es=sc_body)
        dect_all = b.sbuf("dect_all", [128, NCHUNK * 64], F32, es=sc_body)
        prod_d = b.sbuf("prod_d", [128, 64], F32, es=sc_body)
        rowsc = b.sbuf("rowsc", [1, 64], F32, es=sc_body)
        s_run = b.sbuf("s_run", [128, D_INNER], F32, es=sc_body)
        zsc = b.sbuf("zsc", [64, 128], F32, es=sc_body)
        d_b = b.sbuf("d_b", [128, 64], F32, es=sc_body)

        @block.sync
        def _(sync):
            ld.inc(sync.dma_start(out=convw_s[:, :], in_=convw_in), 16)
            ld.inc(sync.dma_start(out=convb_s[:, :], in_=convb_in), 16)
        xn1_ld = ld.n

        sc_pa = ExitStack()
        wx_buf = b.sbuf("wx_buf", [128, 2 * NFC * 128], BF16, es=sc_pa)
        sc_ip = ExitStack()
        ip_ps = [b.psum("ip_ps0", [128, 512], F32, es=sc_ip), b.psum("ip_ps1", [128, 512], F32, es=sc_ip)]
        lwx = b.sem("lwx")
        mip = b.sem("mip")
        eip = b.sem("eip")

        @block.sync
        def _(sync):
            for oc in range(NCC + 1):
                if oc >= 2:
                    sync.wait_ge(mip.h, 2 * (oc - 1))
                src = wxbc_in[:, oc * 128:(oc + 1) * 128] if oc < NCC else wdt_in
                w = 128 if oc < NCC else 64
                lwx.inc(sync.dma_start(
                    out=wx_buf[:, (oc % 2) * NFC * 128:(oc % 2) * NFC * 128 + NFC * w].rearrange("p (c d) -> p c d", c=NFC),
                    in_=src.rearrange("(c p) d -> p c d", p=128)), 16)

        @block.tensor
        def _(tensor):
            tensor.wait_ge(ld.h, xn1_ld)
            for oc in range(NCC + 1):
                tensor.wait_ge(lwx.h, 16 * (oc + 1))
                w = 128 if oc < NCC else 64
                for th in range(2):
                    k = 2 * oc + th
                    if k >= 2:
                        tensor.wait_ge(eip.h, k - 1)
                    out_ap = ip_ps[k % 2][:, :] if w == 128 else ip_ps[k % 2][0:64, :]
                    for fc in range(NFC):
                        mm = tensor.matmul(out_ap,
                                           wx_buf[:, (oc % 2) * NFC * 128 + fc * w:(oc % 2) * NFC * 128 + fc * w + w],
                                           xn1[:, fc * TC + th * 512:fc * TC + th * 512 + 512],
                                           start=(fc == 0), stop=(fc == NFC - 1))
                    mip.inc(mm)

        @block.vector
        def _(vector):
            for oc in range(NCC + 1):
                for th in range(2):
                    k = 2 * oc + th
                    vector.wait_ge(mip.h, k + 1)
                    if oc < NCC:
                        dst = xbc[:, oc * XBW + 3 + th * 512:oc * XBW + 3 + th * 512 + 512]
                        eip.inc(vector.tensor_copy(dst, ip_ps[k % 2][:, :]))
                    else:
                        eip.inc(vector.tensor_copy(dt_raw[:, th * 512:(th + 1) * 512],
                                                   ip_ps[k % 2][0:64, :]))
        ip_done = eip.n

        halo_sb = b.sbuf("halo_sb", [128, 8 * NCC * 3], BF16, es=sc_pa)
        halo_acc = b.sbuf("halo_acc", [128, NCC * 3], F32, es=sc_pa)
        hcmb = b.sem("hcmb")

        @block.gpsimd
        def _(g):
            g.wait_ge(eip.h, ip_done)
            gl.inc(g.dma_start(
                out=halo_in_d.ap().rearrange("p (c t) -> p c t", c=NCC),
                in_=xbc[:, :].rearrange("p (c t) -> p c t", c=NCC)[:, :, 1024:1027]), 16)
            g.wait_ge(gl.h, gl.n)
            cc.inc(g.collective_compute(
                "AllGather", ALU.bypass, replica_groups=[list(range(NCORES))],
                ins=[halo_in_d.ap().opt()], outs=[halo_out_d.ap().opt()]))
            g.wait_ge(cc.h, 1)
            gl.inc(g.dma_start(
                out=halo_sb[:, :].rearrange("p (j t) -> p j t", j=8),
                in_=halo_out_d.ap().rearrange("(j p) t -> p j t", p=128)), 16)
        halo_gl = gl.n

        @block.vector
        def _(vector):
            vector.wait_ge(gl.h, halo_gl)
            vector.wait_ge(cset.h, 1)
            vector.memset(halo_acc[:, :], 0.0)
            for j in range(8):
                hs = halo_sb[:, j * NCC * 3:(j + 1) * NCC * 3]
                inst = vector.scalar_tensor_tensor(halo_acc[:, :], hs, whalo_s[:, j:j + 1],
                                                   halo_acc[:, :], ALU.mult, ALU.add)
                hcmb.inc(inst)
                vector.wait_ge(hcmb.h, hcmb.n)
            hcmb.inc(vector.tensor_copy(
                xbc[:, :].rearrange("p (c t) -> p c t", c=NCC)[:, :, 0:3],
                halo_acc[:, :].rearrange("p (c t) -> p c t", c=NCC)))
        halo_done = hcmb.n

        cacc = b.sbuf("cacc", [128, 2 * TC], F32, es=sc_pa)
        cvt = b.sem("cvt")
        cvs = b.sem("cvs")

        @block.vector
        def _(vector):
            vector.wait_ge(hcmb.h, halo_done)
            for cci in range(NCC):
                buf = cci % 2
                acc = cacc[:, buf * TC:(buf + 1) * TC]
                base = cci * XBW
                if cci >= 2:
                    vector.wait_ge(cvs.h, cci - 1)
                vector.tensor_scalar(acc, xbc[:, base + 0:base + TC],
                                     convw_s[:, cci * 4 + 0:cci * 4 + 1],
                                     convb_s[:, cci:cci + 1], ALU.mult, ALU.add)
                for kk in range(1, 3):
                    vector.scalar_tensor_tensor(acc, xbc[:, base + kk:base + kk + TC],
                                                convw_s[:, cci * 4 + kk:cci * 4 + kk + 1],
                                                acc, ALU.mult, ALU.add)
                cvt.inc(vector.scalar_tensor_tensor(acc, xbc[:, base + 3:base + 3 + TC],
                                                    convw_s[:, cci * 4 + 3:cci * 4 + 4],
                                                    acc, ALU.mult, ALU.add))

        @block.scalar
        def _(scalar):
            for cci in range(NCC):
                buf = cci % 2
                scalar.wait_ge(cvt.h, cci + 1)
                cvs.inc(scalar.activation(xbc[:, cci * XBW + 3:cci * XBW + 3 + TC],
                                          cacc[:, buf * TC:(buf + 1) * TC], AF.Silu))
        conv_done = cvs.n

        S = Sched(b, "mb")
        AX = mybir.AxisListType.X

        @block.vector
        def _(vector):
            vector.wait_ge(spl.h, 16)
            vector.memset(zsc[:, :], 0.0)

        ec = cst[0:64, :]

        S.add("vector", lambda e: e.tensor_scalar(dsc[:, :], dt_raw[:, :], dtb_s[:, :], None, ALU.add),
              waits=[(eip.h, ip_done), (cset.h, 1)])
        S.add("scalar", lambda e: e.activation(dec_s[:, :], dsc[:, :], AF.Abs), waits=[S.up("vector")])
        S.add("scalar", lambda e: e.activation(dec_s[:, :], dec_s[:, :], AF.Exp, scale=ec[:, 2:3]))
        S.add("scalar", lambda e: e.activation(dec_s[:, :], dec_s[:, :], AF.Ln, bias=ec[:, 3:4]))
        S.add("vector", lambda e: e.tensor_scalar(dt_s[:, :], dsc[:, :], 0.0, None, ALU.max))
        S.add("vector", lambda e: e.tensor_tensor(dt_s[:, :], dt_s[:, :], dec_s[:, :], ALU.add),
              waits=[S.up("scalar")])
        S.add("scalar", lambda e: e.activation(aexp_s[:, 0:1], alog_s[:, :], AF.Exp))
        S.add("vector", lambda e: e.tensor_scalar(dsc[:, :], dt_s[:, :], aexp_s[:, 0:1], ec[:, 2:3], ALU.mult, ALU.mult),
              waits=[S.up("scalar")])
        for ci in range(NCHUNK):
            sl = slice(ci * MCH, (ci + 1) * MCH)
            S.add("vector", lambda e, sl=sl: e.tensor_tensor_scan(a_cs[:, sl], dsc[:, sl], zsc[:, :], 0.0, ALU.add, ALU.add))
        for ci in range(NCHUNK):
            sl = slice(ci * MCH, (ci + 1) * MCH)
            lc = a_cs[:, ci * MCH + MCH - 1:ci * MCH + MCH]
            S.add("vector", lambda e, sl=sl, lc=lc: e.tensor_scalar(dec_s[:, sl], a_cs[:, sl], lc, ec[:, 2:3], ALU.subtract, ALU.mult))
        dec_ts_done = S.up("vector")
        for ci in range(NCHUNK):
            sl = slice(ci * MCH, (ci + 1) * MCH)
            S.add("scalar", lambda e, sl=sl: e.activation(dec_s[:, sl], dec_s[:, sl], AF.Exp),
                  waits=[dec_ts_done])

        sc_ip.close()
        sc_pa.close()
        tp_f = b.psum("tp_f", [128, 128], F32)
        tp_b2 = b.psum("tp_b2", [128, 256], BF16)
        bcA = b.psum("bcA", [128, 128], F32)
        sh4 = b.psum("sh4", [128, 2048], F32)

        S.add("tensor", lambda e: e.matmul(tp_f[:, 0:64], ones_row[:, :], drow_s[:, :], start=True, stop=True),
              waits=[(ld.h, const_ld), (cset.h, 1)])
        S.add("vector", lambda e: e.tensor_copy(d_b[:, :], tp_f[:, 0:64]), waits=[S.up("tensor")])
        S.add("vector", lambda e: e.memset(prod_d[:, :], 1.0))

        id64 = ident_f[0:64, 0:64]
        for ci in range(NCHUNK):
            sl = slice(ci * MCH, (ci + 1) * MCH)
            s64 = slice(ci * 64, (ci + 1) * 64)
            S.add("tensor", lambda e, sl=sl: e.transpose(tp_f[:, 0:64], a_cs[:, sl], id64),
                  waits=[S.up("vector"), S.up("scalar")])
            S.add("vector", lambda e, s64=s64: e.tensor_copy(acst_all[:, s64], tp_f[:, 0:64]),
                  waits=[S.up("tensor")])
            S.add("scalar", lambda e, s64=s64: e.activation(sdect_all[:, s64], acst_all[:, s64], AF.Exp),
                  waits=[S.up("vector")])
            S.add("tensor", lambda e, sl=sl: e.transpose(tp_f[:, 0:64], dt_s[:, sl], id64),
                  waits=[S.up("vector")])
            S.add("vector", lambda e, s64=s64: e.tensor_copy(dtt_all[:, s64], tp_f[:, 0:64]),
                  waits=[S.up("tensor")])
            S.add("tensor", lambda e, sl=sl: e.transpose(tp_f[:, 0:64], dec_s[:, sl], id64),
                  waits=[S.up("vector")])
            S.add("vector", lambda e, s64=s64: e.tensor_copy(dect_all[:, s64], tp_f[:, 0:64]),
                  waits=[S.up("tensor")])
            S.add("scalar", lambda e, ci=ci: e.activation(aexp_s[:, 1:2],
                  a_cs[:, ci * MCH + MCH - 1:ci * MCH + MCH], AF.Exp), waits=[S.up("vector")])
            S.add("tensor", lambda e: e.transpose(tp_f[0:1, 0:64], aexp_s[:, 1:2], id64),
                  waits=[S.up("scalar"), S.up("vector")])
            S.add("vector", lambda e: e.tensor_copy(rowsc[:, :], tp_f[0:1, 0:64]),
                  waits=[S.up("tensor")])
            S.add("tensor", lambda e: e.matmul(tp_f[:, 0:64], ones_row[:, :], rowsc[:, :], start=True, stop=True),
                  waits=[S.up("vector")])
            S.add("vector", lambda e, s64=s64: e.tensor_copy(dbc_all[:, s64], tp_f[:, 0:64]),
                  waits=[S.up("tensor")])
            S.add("vector", lambda e, s64=s64: e.tensor_tensor(prod_d[:, :], prod_d[:, :], dbc_all[:, s64], ALU.mult),
                  waits=[S.up("vector")])

        x_tm = b.sbuf("x_tm", [128, 2048], BF16, es=sc_body)
        xdt_h = b.sbuf("xdt_h", [128, 2048], BF16, es=sc_body)
        xdtd_h = b.sbuf("xdtd_h", [128, 2048], BF16, es=sc_body)
        m_all = b.sbuf("m_all", [128, 32 * 128], BF16, es=sc_body)
        wide = b.sbuf("wide", [128, 2048], F32, es=sc_body)
        g_tm = b.sbuf("g_tm", [128, 128], F32, es=sc_body)
        b_tm = b.sbuf("b_tm", [128, 128], BF16, es=sc_body)
        y_sc = b.sbuf("y_sc", [128, 2048], F32, es=sc_body)
        y_sc2 = b.sbuf("y_sc2", [128, 2048], F32, es=sc_body)
        s_bf = b.sbuf("s_bf", [128, 2048], BF16, es=sc_body)
        y_stage = b.sbuf("y_stage", [128, 2048], BF16, es=sc_body)

        def exp_ap(t, col0, nh):
            pstep = t[:, :].ap[0][0]
            return bass.AP(t, col0, [[pstep, 128], [1, nh], [0, 64]])

        S.add("vector", lambda e: e.memset(s_run[:, :], 0.0))

        conv_w_ = (cvs.h, conv_done)
        for ci in range(NCHUNK):
            s64c = ci * 64
            btile = xbc[:, 32 * XBW + 3 + ci * MCH:32 * XBW + 3 + ci * MCH + MCH]
            ctile = xbc[:, 33 * XBW + 3 + ci * MCH:33 * XBW + 3 + ci * MCH + MCH]
            S.add("tensor", lambda e, btile=btile: e.transpose(tp_b2[:, 0:128], btile, ident_b[:, :]),
                  waits=[conv_w_, S.up("vector")])
            S.add("vector", lambda e: e.tensor_copy(b_tm[:, :], tp_b2[:, 0:128]), waits=[S.up("tensor")])
            S.add("tensor", lambda e, btile=btile, ctile=ctile: e.matmul(bcA[:, :], btile, ctile, start=True, stop=True),
                  waits=[S.up("vector")])
            S.add("vector", lambda e: e.tensor_tensor(g_tm[:, :], bcA[:, :], tril_s[:, :], ALU.mult),
                  waits=[S.up("tensor")])
            for hf in range(2):
                h0 = 32 * hf

                for fc in range(NFC):
                    cci = hf * NFC + fc
                    xt_ = xbc[:, cci * XBW + 3 + ci * MCH:cci * XBW + 3 + ci * MCH + MCH]
                    S.add("tensor", lambda e, xt_=xt_: e.transpose(tp_b2[:, 0:128], xt_, ident_b[:, :]),
                          waits=[S.up("vector")])
                    S.add("vector", lambda e, fc=fc: e.tensor_copy(
                        x_tm[:, fc * 128:(fc + 1) * 128], tp_b2[:, 0:128]),
                        waits=[S.up("tensor")])
                S.add("vector", lambda e, s64c=s64c, h0=h0: e.tensor_tensor(
                    xdt_h[:, :], x_tm[:, :], exp_ap(dtt_all, s64c + h0, 32), ALU.mult))
                S.add("vector", lambda e, s64c=s64c, h0=h0: e.tensor_tensor(
                    xdtd_h[:, :], xdt_h[:, :], exp_ap(dect_all, s64c + h0, 32), ALU.mult))
                if ci >= 1:
                    S.add("vector", lambda e, hf=hf: e.tensor_copy(s_bf[:, :], s_run[:, hf * 2048:(hf + 1) * 2048]))
                    def yoff(e, ctile=ctile):
                        for q in range(4):
                            mm = e.matmul(sh4[:, q * 512:(q + 1) * 512], ctile, s_bf[:, q * 512:(q + 1) * 512],
                                          start=True, stop=True)
                        return mm
                    S.add("tensor", yoff, waits=[S.up("vector")])
                    S.add("vector", lambda e, s64c=s64c, h0=h0: e.tensor_tensor(
                        y_sc[:, :], sh4[:, :], exp_ap(sdect_all, s64c + h0, 32), ALU.mult),
                        waits=[S.up("tensor")])
                else:
                    S.add("vector", lambda e: e.memset(y_sc[:, :], 0.0))
                def pstep(t):
                    return t[:, :].ap[0][0]
                for sub in range(2):
                    for hl in range(16):
                        h = h0 + 16 * sub + hl
                        S.add("vector", lambda e, hl=hl, ci=ci, h=h: e.tensor_scalar(
                            wide[0:64, hl * 128:(hl + 1) * 128], a_cs[:, ci * MCH:(ci + 1) * MCH],
                            i64[:, h:h + 1], None, ALU.mult),
                            waits=([S.up("tensor"), S.up("scalar")] if hl == 0 else []))
                    def bcmm(e):
                        for q in range(4):
                            mm = e.matmul(sh4[:, q * 512:(q + 1) * 512], ones64[:, :],
                                          wide[0:64, q * 512:(q + 1) * 512],
                                          start=True, stop=True)
                        return mm
                    S.add("tensor", bcmm, waits=[S.up("vector")])
                    for q in range(4):
                        S.add("vector", lambda e, q=q, col0=s64c + h0 + 16 * sub + 4 * q: e.tensor_tensor(
                            wide[:, q * 512:(q + 1) * 512], sh4[:, q * 512:(q + 1) * 512],
                            bass.AP(acst_all, col0, [[pstep(acst_all), 128], [1, 4], [0, MCH]]),
                            ALU.subtract),
                            waits=([S.up("tensor")] if q == 0 else []))
                    S.add("vector", lambda e: e.tensor_scalar(
                        wide[:, :], wide[:, :], 0.0, None, ALU.min))
                    S.add("scalar", lambda e: e.activation(y_sc2[:, :], wide[:, :], AF.Exp),
                        waits=[S.up("vector")])
                    for hl in range(16):
                        S.add("vector", lambda e, hp=sub * 16 + hl, hl=hl: e.tensor_tensor(
                            m_all[:, hp * 128:(hp + 1) * 128], y_sc2[:, hl * 128:(hl + 1) * 128],
                            g_tm[:, :], ALU.mult),
                            waits=([S.up("scalar")] if hl == 0 else []))
                def ydiag(e):
                    for hp in range(32):
                        mm = e.matmul(sh4[:, hp * 64:(hp + 1) * 64], m_all[:, hp * 128:(hp + 1) * 128],
                                      xdt_h[:, hp * 64:(hp + 1) * 64], start=True, stop=True)
                    return mm
                S.add("tensor", ydiag, waits=[S.up("vector")])
                S.add("vector", lambda e, h0=h0: e.tensor_tensor(
                    y_sc2[:, :], x_tm[:, :], exp_ap(d_b, h0, 32), ALU.mult))
                S.add("vector", lambda e: e.tensor_tensor(y_sc[:, :], y_sc[:, :], y_sc2[:, :], ALU.add))
                S.add("vector", lambda e: e.tensor_tensor(
                    y_stage[:, :], y_sc[:, :], sh4[:, :], ALU.add),
                    waits=[S.up("tensor"), S.up("gpsimd")])
                S.add("gpsimd", lambda e, ci=ci, hf=hf: e.dma_start(
                    out=y_scr_d.ap()[:, ci * D_INNER + hf * 2048:ci * D_INNER + (hf + 1) * 2048],
                    in_=y_stage[:, :]),
                    waits=[S.up("vector")], amt=16)
                def stmm(e):
                    for q in range(4):
                        mm = e.matmul(sh4[:, q * 512:(q + 1) * 512], b_tm[:, :],
                                      xdtd_h[:, q * 512:(q + 1) * 512], start=True, stop=True)
                    return mm
                S.add("tensor", stmm, waits=[S.up("vector")])
                S.add("vector", lambda e, hf=hf, col=s64c + h0: e.tensor_tensor(
                    y_sc2[:, :], s_run[:, hf * 2048:(hf + 1) * 2048],
                    exp_ap(dbc_all, col, 32), ALU.mult),
                    waits=[S.up("tensor")])
                S.add("vector", lambda e, hf=hf: e.tensor_tensor(
                    s_run[:, hf * 2048:(hf + 1) * 2048], y_sc2[:, :], sh4[:, :], ALU.add))

        S.add("vector", lambda e: e.tensor_copy(
            c_keep[:, :], bass.AP(xbc, 33 * XBW + 3, [[xbc[:, :].ap[0][0], 128], [1, TC]])))

        S.add("gpsimd", lambda e: e.dma_start(out=carry_in_d.ap()[:, 0:D_INNER], in_=s_run[:, :]),
              waits=[S.up("vector")], amt=16)
        S.add("gpsimd", lambda e: e.dma_start(out=carry_in_d.ap()[:, D_INNER:D_INNER + 64], in_=prod_d[:, :]),
              amt=16)
        S.add("coll", lambda e: e.collective_compute(
            "AllGather", ALU.bypass, replica_groups=[list(range(NCORES))],
            ins=[carry_in_d.ap().opt()], outs=[carry_out_d.ap().opt()]),
            waits=[S.up("gpsimd")])
        S.emit(block)
        pb_g = (S.sems["gpsimd"].h, S.sems["gpsimd"].n)
        pb_c = (S.sems["coll"].h, S.sems["coll"].n)
        pb_v = (S.sems["vector"].h, S.sems["vector"].n)
        sc_body.close()

        P = Sched(b, "pc")
        sc_pc = ExitStack()
        y_sb = b.sbuf("y_sb", [128, NCHUNK * D_INNER], BF16, es=sc_pc)
        nw_b = b.sbuf("nw_b", [128, D_INNER], F32, es=sc_pc)
        ysc_pc = b.sbuf("ysc_pc", [128, 2048], F32, es=sc_pc)
        sumsq = b.sbuf("sumsq", [128, 8], F32, es=sc_pc)
        rstd_a = b.sbuf("rstd_a", [128, 8], F32, es=sc_pc)
        dsc2 = b.sbuf("dsc2", [128, 64], F32, es=sc_pc)
        zs_s = b.sbuf("zs_s", [128, 2 * 512], F32, es=sc_pc)
        sq_scr = b.sbuf("sq_scr", [128, 512], BF16, es=sc_pc)
        sqtmp = b.sbuf("sqtmp", [128, 2], F32, es=sc_pc)
        ostage = b.sbuf("ostage", [128, 2 * 512], F32, es=sc_pc)
        spsl = b.sbuf("spsl", [128, 2 * 512], F32, es=sc_pc)
        nwst = b.sbuf("nwst", [1, 512], F32, es=sc_pc)

        if _DBG_SPILL:
            P.add("gpsimd", lambda e: e.dma_start(out=dbg_y, in_=y_scr_d.ap()),
                  waits=[pb_g, pb_c], amt=16)
        P.add("gpsimd", lambda e: e.dma_start(out=y_sb[:, :], in_=y_scr_d.ap()),
              waits=[pb_g, pb_c], amt=16)
        for zg in range(8):
            P.add("gpsimd", lambda e, zg=zg: e.dma_start(out=nwst[:, :], in_=nw_in[0:1, zg * 512:(zg + 1) * 512]),
                  waits=[P.up("tensor")], amt=16)
            P.add("tensor", lambda e, zg=zg: e.matmul(sh4[:, (zg % 4) * 512:(zg % 4) * 512 + 512],
                                                      ones_row[:, :], nwst[:, :],
                                                      start=True, stop=True),
                  waits=[pb_g, pb_v, P.up("vector"), P.up("gpsimd")])
            P.add("vector", lambda e, zg=zg: e.tensor_copy(nw_b[:, zg * 512:(zg + 1) * 512],
                                                           sh4[:, (zg % 4) * 512:(zg % 4) * 512 + 512]),
                  waits=[P.up("tensor")])
        P.add("vector", lambda e: e.memset(sumsq[:, :], 0.0))
        P.add("vector", lambda e: e.memset(t_buf[:, :], 0.0))

        sc_mid = ExitStack()
        t_buf = b.sbuf("t_buf", [128, D_INNER], F32, es=sc_mid)
        t_bf = b.sbuf("t_bf", [128, D_INNER], BF16, es=sc_mid)
        sc_slab = ExitStack()
        slab = b.sbuf("slab", [128, D_INNER + 64], F32, es=sc_slab)
        for j in range(7):
            P.add("gpsimd", lambda e, j=j: e.dma_start(
                out=slab[:, :], in_=carry_out_d.ap()[j * 128:(j + 1) * 128, :]),
                waits=[P.up("vector")], amt=16)
            P.add("vector", lambda e: e.tensor_scalar(dsc2[:, :], slab[:, D_INNER:D_INNER + 64], 1.0, None, ALU.subtract),
                  waits=[P.up("gpsimd")])
            P.add("vector", lambda e, j=j: e.tensor_scalar(dsc2[:, :], dsc2[:, :], mpref_s[:, j:j + 1],
                                                           cst[:, 3:4], ALU.mult, ALU.add),
                  waits=[P.up("vector")])
            P.add("vector", lambda e: e.tensor_tensor(t_buf[:, :], t_buf[:, :], exp_ap(dsc2, 0, 64), ALU.mult),
                  waits=[P.up("vector")])
            P.add("vector", lambda e, j=j: e.scalar_tensor_tensor(t_buf[:, :], slab[:, 0:D_INNER],
                                                                  mpref_s[:, j:j + 1], t_buf[:, :],
                                                                  ALU.mult, ALU.add))
        sc_slab.close()

        for ci in range(NCHUNK):
            P.add("vector", lambda e: e.tensor_copy(t_bf[:, :], t_buf[:, :]),
                  waits=[P.up("tensor"), P.up("gpsimd"), P.up("scalar")])
            for hf in range(2):
                def ycar(e, ci=ci, hf=hf):
                    ct = c_keep[:, ci * MCH:(ci + 1) * MCH]
                    for q in range(4):
                        mm = e.matmul(sh4[:, q * 512:(q + 1) * 512], ct,
                                      t_bf[:, hf * 2048 + q * 512:hf * 2048 + (q + 1) * 512],
                                      start=True, stop=True)
                    return mm
                P.add("tensor", ycar, waits=[P.up("vector"), P.up("gpsimd"), P.up("scalar")])
                P.add("vector", lambda e, ci=ci, hf=hf: e.tensor_tensor(
                    ysc_pc[:, :], sh4[:, :], exp_ap(sdect_all, ci * 64 + 32 * hf, 32), ALU.mult),
                    waits=[P.up("tensor")])
                ysl = slice(ci * D_INNER + hf * 2048, ci * D_INNER + (hf + 1) * 2048)
                P.add("vector", lambda e, ysl=ysl: e.tensor_tensor(
                    y_sb[:, ysl], y_sb[:, ysl], ysc_pc[:, :], ALU.add))
            P.add("vector", lambda e, ci=ci: e.tensor_tensor(
                t_buf[:, :], t_buf[:, :], exp_ap(dbc_all, ci * 64, 64), ALU.mult))

        sc_mid.close()
        sc_wz = ExitStack()
        wz_buf = b.sbuf("wz_buf", [128, 2 * NFC * 512], BF16, es=sc_wz)
        kz = 0
        for zg in range(8):
            P.add("gpsimd", lambda e, zg=zg: e.dma_start(
                out=wz_buf[:, (zg % 2) * NFC * 512:(zg % 2 + 1) * NFC * 512].rearrange("p (c d) -> p c d", c=NFC),
                in_=wz_in[:, zg * 512:(zg + 1) * 512].rearrange("(c p) d -> p c d", p=128)),
                waits=[P.up("tensor"), P.up("vector")], amt=16)
            wz_tok = P.up("gpsimd")
            for ch in range(NCHUNK):
                def zmm(e, zg=zg, ch=ch, kz=kz):
                    for fc in range(NFC):
                        mm = e.matmul(sh4[:, (kz % 2) * 512:(kz % 2) * 512 + 512],
                                      xn1[:, fc * TC + ch * 128:fc * TC + (ch + 1) * 128],
                                      wz_buf[:, ((zg % 2) * NFC + fc) * 512:((zg % 2) * NFC + fc + 1) * 512],
                                      start=(fc == 0), stop=(fc == NFC - 1))
                    return mm
                P.add("tensor", zmm, waits=[wz_tok, P.up("scalar"), P.up("vector"), P.up("gpsimd")])
                P.add("scalar", lambda e, kz=kz: e.activation(
                    zs_s[:, (kz % 2) * 512:(kz % 2) * 512 + 512],
                    sh4[:, (kz % 2) * 512:(kz % 2) * 512 + 512], AF.Silu),
                    waits=[P.up("tensor"), P.up("vector")])
                ysl = slice(ch * D_INNER + zg * 512, ch * D_INNER + (zg + 1) * 512)
                P.add("vector", lambda e, ysl=ysl, kz=kz: e.tensor_tensor(
                    y_sb[:, ysl], y_sb[:, ysl], zs_s[:, (kz % 2) * 512:(kz % 2) * 512 + 512], ALU.mult),
                    waits=[P.up("scalar")])
                P.add("scalar", lambda e, ysl=ysl: e.activation(
                    sq_scr[:, :], y_sb[:, ysl], AF.Square, accum_out=sqtmp[:, 0:1]),
                    waits=[P.up("vector")])
                P.add("vector", lambda e, ch=ch: e.tensor_tensor(
                    sumsq[:, ch:ch + 1], sumsq[:, ch:ch + 1], sqtmp[:, 0:1], ALU.add),
                    waits=[P.up("scalar"), P.up("vector")])
                kz += 1
        sc_wz.close()

        for ch in range(NCHUNK):
            P.add("vector", lambda e, ch=ch: e.tensor_scalar(rstd_a[:, ch:ch + 1], sumsq[:, ch:ch + 1],
                                                             1.0 / D_INNER, None, ALU.mult),
                  waits=[P.up("vector")])
            P.add("scalar", lambda e, ch=ch: e.activation(rstd_a[:, ch:ch + 1], rstd_a[:, ch:ch + 1],
                                                          AF.Sqrt, bias=cst[:, 0:1]),
                  waits=[P.up("vector")])
            P.add("vector", lambda e, ch=ch: e.reciprocal(rstd_a[:, ch:ch + 1], rstd_a[:, ch:ch + 1]),
                  waits=[P.up("scalar")])
            ysl = slice(ch * D_INNER, (ch + 1) * D_INNER)
            P.add("vector", lambda e, ysl=ysl, ch=ch: e.tensor_scalar(
                y_sb[:, ysl], y_sb[:, ysl], rstd_a[:, ch:ch + 1], None, ALU.mult),
                waits=[P.up("vector"), P.up("scalar")])
            P.add("vector", lambda e, ysl=ysl: e.tensor_tensor(
                y_sb[:, ysl], y_sb[:, ysl], nw_b[:, :], ALU.mult))

        if _DBG_SPILL:
            P.add("gpsimd", lambda e: e.dma_start(out=dbg_y2, in_=y_sb[:, :]),
                  waits=[P.up("vector"), P.up("scalar")], amt=16)
        sc_fm = ExitStack()
        y_fm = b.sbuf("y_fm", [128, 32 * 512], BF16, es=sc_fm)
        wo_buf = b.sbuf("wo_buf", [128, 2 * 32 * 128], BF16, es=sc_fm)
        ko = 0
        for hf2 in range(2):
            for ch in range(4 * hf2, 4 * hf2 + 4):
                for dc in range(32):
                    ysl = y_sb[:, ch * D_INNER + dc * 128:ch * D_INNER + (dc + 1) * 128]
                    P.add("tensor", lambda e, ysl=ysl: e.transpose(tp_b2[:, 0:128], ysl, ident_b[:, :]),
                          waits=[P.up("vector")])
                    P.add("vector", lambda e, dc=dc, ch=ch, hf2=hf2: e.tensor_copy(
                        y_fm[:, dc * 512 + (ch - 4 * hf2) * 128:dc * 512 + (ch - 4 * hf2) * 128 + 128],
                        tp_b2[:, 0:128]),
                        waits=[P.up("tensor")])
            for oc in range(NFC):
                P.add("gpsimd", lambda e, oc=oc, ko=ko: e.dma_start(
                    out=wo_buf[:, (ko % 2) * 32 * 128:(ko % 2 + 1) * 32 * 128].rearrange("p (c d) -> p c d", c=32),
                    in_=wout_in[:, oc * 128:(oc + 1) * 128].rearrange("(c p) d -> p c d", p=128)),
                    waits=[P.up("tensor")], amt=16)
                wo_tok = P.up("gpsimd")
                def opmm(e, ko=ko):
                    for dc in range(32):
                        mm = e.matmul(sh4[:, (ko % 2) * 512:(ko % 2) * 512 + 512],
                                      wo_buf[:, ((ko % 2) * 32 + dc) * 128:((ko % 2) * 32 + dc + 1) * 128],
                                      y_fm[:, dc * 512:(dc + 1) * 512],
                                      start=(dc == 0), stop=(dc == 31))
                    return mm
                P.add("tensor", opmm, waits=[wo_tok, P.up("vector"), P.up("scalar"), P.up("gpsimd")])
                P.add("gpsimd", lambda e, oc=oc, hf2=hf2, ko=ko: e.dma_start(
                    out=spsl[:, (ko % 2) * 512:(ko % 2 + 1) * 512],
                    in_=x1_spill.ap()[oc * 128:(oc + 1) * 128, hf2 * 512:(hf2 + 1) * 512]),
                    waits=[P.up("vector"), (spl.h, 16)], amt=16)
                P.add("vector", lambda e, ko=ko: e.tensor_tensor(
                    ostage[:, (ko % 2) * 512:(ko % 2) * 512 + 512],
                    spsl[:, (ko % 2) * 512:(ko % 2) * 512 + 512],
                    sh4[:, (ko % 2) * 512:(ko % 2) * 512 + 512], ALU.add),
                    waits=[P.up("tensor"), P.up("gpsimd"), P.up("scalar")])
                P.add("gpsimd", lambda e, oc=oc, hf2=hf2, ko=ko: e.dma_start(
                    out=out_ap[oc * 128:(oc + 1) * 128, hf2 * 512:(hf2 + 1) * 512],
                    in_=ostage[:, (ko % 2) * 512:(ko % 2) * 512 + 512]),
                    waits=[P.up("vector")], amt=16)
                ko += 1
        sc_fm.close()
        P.emit(block)

        @block.gpsimd
        def _(g):
            g.wait_ge(P.sems["gpsimd"].h, P.sems["gpsimd"].n)
        sc_pc.close()


    return b


_BUILD_CACHE = {}


def _get(name, fn):
    if name not in _BUILD_CACHE:
        _BUILD_CACHE[name] = fn()
    return _BUILD_CACHE[name]


def kernel(**ins):
    maps = prep_host(ins)
    keys = ("x_fm", "wq", "wk", "wv", "wo", "cos_t", "sin_t", "masks", "ident", "ident_bf",
            "tok_off", "rT", "sgT", "suT", "sdT", "wgT", "wuT", "wdT", "sel8",
            "w_xbc", "w_dt", "w_z", "w_out", "conv_w", "conv_b", "dt_bias", "A_log",
            "D_row", "nw_row", "tril", "m_pref", "w_halo", "factors", "l1_bias")
    b = _get("fused", build_fused)
    res = run_bass_kernel_spmd(b.nc, [{k: m[k] for k in keys} for m in maps],
                               core_ids=list(range(NCORES)))
    out = np.empty((1, T, D), dtype=np.float32)
    for c in range(NCORES):
        out[0, c * TC:(c + 1) * TC, :] = res.results[c]["out"].T
    return out


# version 09
# speedup vs baseline: 1.1110x; 1.1110x over previous
import math
_DBG_SPILL = False
import numpy as np
import ml_dtypes
from contextlib import ExitStack

import concourse.bass as bass
import concourse.mybir as mybir
from concourse.bass_utils import run_bass_kernel_spmd

F32 = mybir.dt.float32
BF16 = mybir.dt.bfloat16
AF = mybir.ActivationFunctionType
ALU = mybir.AluOpType
BB = ml_dtypes.bfloat16

NCORES = 8
T = 8192
TC = T // NCORES
D = 2048
NFC = D // 128
HD = 128
NHEADS_Q = 16
HPC = 2
MOE_I = 1408
NIC = MOE_I // 128
D_INNER = 4096
CONV_DIM = D_INNER + 256
NCC = CONV_DIM // 128
MCH = 128
NCHUNK = TC // MCH
QK_SCALE = 128.0 ** -0.25
NQT = T // 512


def _bf(a):
    return np.ascontiguousarray(np.asarray(a)).astype(BB)


def _f32(a):
    return np.ascontiguousarray(np.asarray(a), dtype=np.float32)



def prep_host(ins):
    x = _f32(ins["L0_attn_x"][0])
    x_fm = np.ascontiguousarray(x.T)

    w_aq = np.asarray(ins["L0_attn_w_aq"])
    w_ak = np.asarray(ins["L0_attn_w_ak"])
    w_av = np.asarray(ins["L0_attn_w_av"])
    w_ao = np.asarray(ins["L0_attn_w_ao"])
    wo_flat = _bf(w_ao.reshape(NHEADS_Q * HD, D))

    half = HD // 2
    pos = np.arange(T, dtype=np.float32)[:, None]
    freqs = (10000.0 ** (-np.arange(half, dtype=np.float32) / half))[None, :]
    ang = pos * freqs
    cos_t = _bf((np.cos(ang) * QK_SCALE).T)
    sin_t = _bf((np.sin(ang) * QK_SCALE).T)

    masks = np.zeros((4, 128, 512), dtype=np.float32)
    tkk = np.arange(128)[:, None]
    tqq = np.arange(512)[None, :]
    for dd in range(4):
        masks[dd] = np.where(tqq >= tkk + dd * 128, 0.0, -1e30)

    sel8 = np.zeros((8, 8 * 128), dtype=np.float32)
    for e in range(8):
        sel8[e, e * 128:(e + 1) * 128] = 1.0

    tril = np.where(np.arange(128)[None, :] >= np.arange(128)[:, None], 1.0, 0.0
                    ).astype(np.float32)

    shared = dict(
        wo=wo_flat, cos_t=cos_t, sin_t=sin_t, masks=masks, sel8=_bf(sel8),
        ident=np.eye(128, dtype=np.float32), ident_bf=_bf(np.eye(128)),
        tril=tril,
        rT=_bf(np.asarray(ins["L0_ffn_router_w"]).T),
        sgT=_bf(np.asarray(ins["L0_ffn_shared_gate"]).T),
        suT=_bf(np.asarray(ins["L0_ffn_shared_up"]).T),
        sdT=_bf(np.asarray(ins["L0_ffn_shared_down"]).T),
        wgT=_bf(np.transpose(np.asarray(ins["L0_ffn_w_gate"]), (0, 2, 1))),
        wuT=_bf(np.transpose(np.asarray(ins["L0_ffn_w_up"]), (0, 2, 1))),
        wdT=_bf(np.transpose(np.asarray(ins["L0_ffn_w_down"]), (0, 2, 1))),
    )

    w_inproj = np.asarray(ins["L1_attn_in_proj_kernel"])
    shared.update(
        w_xbc=_bf(w_inproj[:, D_INNER:D_INNER + CONV_DIM]),
        w_dt=_bf(w_inproj[:, D_INNER + CONV_DIM:]),
        w_z=_bf(w_inproj[:, :D_INNER]),
        w_out=_bf(np.asarray(ins["L1_attn_out_proj_kernel"])),
        conv_w=_f32(np.asarray(ins["L1_attn_conv1d_weight"]).reshape(NCC, 128, 4).transpose(1, 0, 2).reshape(128, NCC * 4)),
        conv_b=_f32(np.asarray(ins["L1_attn_conv1d_bias"]).reshape(NCC, 128).T),
        dt_bias=_f32(np.asarray(ins["L1_attn_dt_bias"]).reshape(64, 1)),
        A_log=_f32(np.asarray(ins["L1_attn_A_log"]).reshape(64, 1)),
        D_row=_f32(np.asarray(ins["L1_attn_D"]).reshape(1, 64)),
        nw_row=_f32(np.asarray(ins["L1_attn_norm_weight"]).reshape(1, D_INNER)),
        factors=_f32(np.asarray(ins["L1_ffn_factors"])),
        l1_bias=_f32(np.asarray(ins["L1_ffn_bias"]).reshape(NFC, 128).T),
    )

    in_maps = []
    for c in range(NCORES):
        h0 = HPC * c
        g = h0 // 4
        wq = np.stack([_bf(w_aq[(h0 + j) // 4, (h0 + j) % 4]) for j in range(HPC)])
        m_pref = np.zeros((128, 7), dtype=np.float32)
        m_pref[:, :c] = 1.0
        w_halo = np.zeros((128, 8), dtype=np.float32)
        if c > 0:
            w_halo[:, c - 1] = 1.0
        im = dict(shared)
        im.update(
            x_fm=np.ascontiguousarray(x_fm[:, c * TC:(c + 1) * TC]),
            wq=wq, wk=_bf(w_ak[g]), wv=_bf(w_av[g]),
            m_pref=m_pref, w_halo=w_halo,
            tok_off=np.array([[c * TC]], dtype=np.int32),
        )
        in_maps.append(im)
    return in_maps



class SemC:
    def __init__(self, es, nc, name):
        self.h = es.enter_context(nc.semaphore(name))
        self.n = 0

    def inc(self, inst, k=1):
        inst.then_inc(self.h, k)
        self.n += k
        return self.n


class B:
    def __init__(self):
        self.nc = bass.Bass("TRN2", target_bir_lowering=False)
        self.es = ExitStack()
        self._uid = 0
        self._names = set()

    def uid(self):
        self._uid += 1
        return self._uid

    def _uniq(self, name):
        if name in self._names:
            name = f"{name}_{self.uid()}"
        self._names.add(name)
        return name

    def sem(self, name):
        return SemC(self.es, self.nc, self._uniq("sem_" + name))

    def sbuf(self, name, shape, dtype, es=None):
        return (es or self.es).enter_context(self.nc.sbuf_tensor(self._uniq(name), list(shape), dtype))

    def psum(self, name, shape, dtype=F32, es=None):
        return (es or self.es).enter_context(self.nc.psum_tensor(self._uniq(name), list(shape), dtype))

    def din(self, name, shape, dtype):
        return self.nc.dram_tensor(name, list(shape), dtype, kind="ExternalInput").ap()

    def dout(self, name, shape, dtype):
        return self.nc.dram_tensor(name, list(shape), dtype, kind="ExternalOutput").ap()

    def dram(self, name, shape, dtype):
        return self.nc.dram_tensor(name, list(shape), dtype)


class Sched:
    ENGS = ("tensor", "vector", "scalar", "gpsimd", "sync")

    def __init__(self, b, tag):
        self.b = b
        self.sems = {e: b.sem(f"{tag}_{e}") for e in self.ENGS}
        self.sems["coll"] = b.sem(f"{tag}_coll")
        self.steps = {e: [] for e in self.ENGS}

    def up(self, eng):
        return (self.sems[eng].h, self.sems[eng].n)

    def add(self, eng, fn, waits=(), amt=1):
        lane = "gpsimd" if eng == "coll" else eng
        s = self.sems[eng]
        s.n += amt
        self.steps[lane].append((fn, list(waits), amt, s))
        return (s.h, s.n)

    def emit(self, block):
        for eng in self.ENGS:
            if not self.steps[eng]:
                continue
            steps = self.steps[eng]

            def body(e, steps=steps):
                for fn, waits, amt, s in steps:
                    for (h, v) in waits:
                        if v > 0:
                            e.wait_ge(h, v)
                    inst = fn(e)
                    inst.then_inc(s.h, amt)
            getattr(block, eng)(body)


def fm_dram(ap_2d):
    return ap_2d.rearrange("(c p) t -> p c t", p=128)


def fm_view(sb_ap, nfc):
    return sb_ap.rearrange("p (c t) -> p c t", c=nfc)


def fm_sbuf_to_dram(sb_ap, nfc):
    return sb_ap.rearrange("p (c t) -> p c t", c=nfc)



def emit_ln_fm(b, block, *, xs, xn, tokens, scratch, stats,
               ones_col, ones_row, eps_ap, ready, done):
    nc = b.nc
    u = b.uid()
    ntt = tokens // 512
    ps = ExitStack()
    st_ps = ps.enter_context(nc.psum_tensor(f"ln{u}_st", [1, tokens], F32))
    sq_ps = ps.enter_context(nc.psum_tensor(f"ln{u}_sq", [1, tokens], F32))
    mb_ps = ps.enter_context(nc.psum_tensor(f"ln{u}_mb", [128, 512], F32))
    rb_ps = ps.enter_context(nc.psum_tensor(f"ln{u}_rb", [128, 512], F32))
    sq = b.sem(f"ln{u}_sqs")
    mm = b.sem(f"ln{u}_mm")
    sts = b.sem(f"ln{u}_st")
    bc = b.sem(f"ln{u}_bc")
    ap_s = b.sem(f"ln{u}_ap")

    @block.scalar
    def _(scalar):
        for rh, rv in ready:
            scalar.wait_ge(rh, rv)
        for fc in range(NFC):
            if fc >= 2:
                scalar.wait_ge(mm.h, fc - 1)
            sq.inc(scalar.activation(scratch[:, (fc % 2) * tokens:(fc % 2 + 1) * tokens],
                                     xs[:, fc * tokens:(fc + 1) * tokens], AF.Square))

    @block.tensor
    def _(tensor):
        for rh, rv in ready:
            tensor.wait_ge(rh, rv)
        for fc in range(NFC):
            tensor.wait_ge(sq.h, fc + 1)
            for tt in range(ntt):
                sl = slice(tt * 512, (tt + 1) * 512)
                tensor.matmul(st_ps[0:1, sl], ones_col[:, :],
                              xs[:, fc * tokens + tt * 512: fc * tokens + (tt + 1) * 512],
                              start=(fc == 0), stop=(fc == NFC - 1))
                last = tensor.matmul(sq_ps[0:1, sl], ones_col[:, :],
                                     scratch[:, (fc % 2) * tokens + tt * 512:(fc % 2) * tokens + (tt + 1) * 512],
                                     start=(fc == 0), stop=(fc == NFC - 1))
            mm.inc(last)

    mean = stats[0:1, 0:tokens]
    rstd = stats[0:1, tokens:2 * tokens]
    tmp1 = stats[0:1, 2 * tokens:3 * tokens]

    @block.vector
    def _(vector):
        vector.wait_ge(mm.h, NFC)
        vector.tensor_scalar(mean, st_ps[0:1, 0:tokens], 1.0 / D, None, ALU.mult)
        vector.tensor_scalar(tmp1, sq_ps[0:1, 0:tokens], 1.0 / D, None, ALU.mult)
        vector.tensor_tensor(rstd, mean, mean, ALU.mult)
        sts.inc(vector.tensor_tensor(tmp1, tmp1, rstd, ALU.subtract))

    @block.scalar
    def _(scalar):
        scalar.wait_ge(sts.h, 1)
        sts.inc(scalar.activation(tmp1, tmp1, AF.Sqrt, bias=eps_ap))

    @block.vector
    def _(vector):
        vector.wait_ge(sts.h, 2)
        sts.inc(vector.reciprocal(rstd, tmp1))

    @block.tensor
    def _(tensor):
        tensor.wait_ge(sts.h, 3)
        for tt in range(ntt):
            sl = slice(tt * 512, (tt + 1) * 512)
            if tt > 0:
                tensor.wait_ge(ap_s.h, tt)
            tensor.matmul(mb_ps[:, :], ones_row[:, :], mean[:, sl], start=True, stop=True)
            bc.inc(tensor.matmul(rb_ps[:, :], ones_row[:, :], rstd[:, sl], start=True, stop=True))

    @block.vector
    def _(vector):
        for tt in range(ntt):
            vector.wait_ge(bc.h, tt + 1)
            for fc in range(NFC):
                xsl = slice(fc * tokens + tt * 512, fc * tokens + (tt + 1) * 512)
                vector.tensor_tensor(scratch[:, 0:512], xs[:, xsl], mb_ps[:, :], ALU.subtract)
                last = vector.tensor_tensor(xn[:, xsl], scratch[:, 0:512], rb_ps[:, :], ALU.mult)
            ap_s.inc(last)
        done.inc(vector.tensor_copy(stats[0:1, 0:1], stats[0:1, 0:1]))
    ps.close()



def build_attn():
    b = B()
    nc = b.nc
    x_in = b.din("x_fm", [D, TC], F32)
    wq_in = b.din("wq", [HPC, D, 128], BF16)
    wk_in = b.din("wk", [D, 128], BF16)
    wv_in = b.din("wv", [D, 128], BF16)
    wo_in = b.din("wo", [NHEADS_Q * HD, D], BF16)
    cos_in = b.din("cos_t", [64, T], BF16)
    sin_in = b.din("sin_t", [64, T], BF16)
    masks_in = b.din("masks", [4, 128, 512], F32)
    ident_in = b.din("ident", [128, 128], F32)
    identb_in = b.din("ident_bf", [128, 128], BF16)

    xn_out = b.dout("xn_out", [D, TC], BF16)
    attn_out = b.dout("attn_out", [D, TC], F32)

    ag_in = b.dram("xn_ag_in", [D, TC], BF16)
    ag_out = b.dram("xn_ag_out", [NCORES * D, TC], BF16)
    oag_in = b.dram("o_ag_in", [HPC * HD, T], BF16)
    oag_out = b.dram("o_ag_out", [NHEADS_Q * HD, T], BF16)

    es = b.es
    with es:
        block = es.enter_context(nc.Block())
        ld = b.sem("ld")
        st = b.sem("st")
        cc = b.sem("cc")

        ones_col = b.sbuf("ones_col", [128, 1], F32)
        ones_row = b.sbuf("ones_row", [1, 128], F32)
        ones_bf = b.sbuf("ones_bf", [128, 128], BF16)
        cst = b.sbuf("cst", [128, 4], F32)
        ident_f = b.sbuf("ident_f", [128, 128], F32)
        ident_b = b.sbuf("ident_b", [128, 128], BF16)
        cset = b.sem("cset")

        @block.vector
        def _(vector):
            vector.memset(ones_col[:, :], 1.0)
            vector.memset(ones_row[:, :], 1.0)
            vector.memset(cst[:, 0:1], 1e-5)
            cset.inc(vector.memset(ones_bf[:, :], 1.0))

        sc1 = ExitStack()
        xs = b.sbuf("xs", [128, NFC * TC], F32, es=sc1)
        xn = b.sbuf("xn", [128, NFC * TC], BF16, es=sc1)
        scratch = b.sbuf("lnscr", [128, 2 * TC], F32, es=sc1)
        stats = b.sbuf("lnstats", [1, 3 * TC], F32, es=sc1)

        @block.sync
        def _(sync):
            ld.inc(sync.dma_start(out=fm_view(xs[:, :], NFC), in_=fm_dram(x_in)), 16)
            ld.inc(sync.dma_start(out=ident_f[:, :], in_=ident_in), 16)
            ld.inc(sync.dma_start(out=ident_b[:, :], in_=identb_in), 16)

        ln_done = b.sem("ln_done")
        emit_ln_fm(b, block, xs=xs, xn=xn, tokens=TC, scratch=scratch, stats=stats,
                   ones_col=ones_col, ones_row=ones_row, eps_ap=cst[0:1, 0:1],
                   ready=[(ld.h, 48), (cset.h, 1)], done=ln_done)

        @block.gpsimd
        def _(g):
            g.wait_ge(ln_done.h, 1)
            st.inc(g.dma_start(out=fm_dram(xn_out), in_=fm_sbuf_to_dram(xn[:, :], NFC)), 16)
            st.inc(g.dma_start(out=fm_dram(ag_in.ap()), in_=fm_sbuf_to_dram(xn[:, :], NFC)), 16)
            g.wait_ge(st.h, 32)
            cc.inc(g.collective_compute(
                "AllGather", ALU.bypass, replica_groups=[list(range(NCORES))],
                ins=[ag_in.ap().opt()], outs=[ag_out.ap().opt()]))

        sc1.close()

        q_s = b.sbuf("q_s", [128, HPC * T], BF16)
        k_s = b.sbuf("k_s", [128, T], BF16)
        v_tm = b.sbuf("v_tm", [128, T], BF16)
        sc2b = ExitStack()
        wq_s = b.sbuf("wq_s", [128, HPC * NFC * 128], BF16, es=sc2b)
        wk_s = b.sbuf("wk_s", [128, NFC * 128], BF16, es=sc2b)
        wv_s = b.sbuf("wv_s", [128, NFC * 128], BF16, es=sc2b)
        cos_s = b.sbuf("cos_s", [64, T], BF16, es=sc2b)
        sin_s = b.sbuf("sin_s", [64, T], BF16, es=sc2b)

        @block.sync
        def _(sync):
            sync.wait_ge(st.h, 32)
            ld.inc(sync.dma_start(out=wq_s[:, :].rearrange("p (j c d) -> p j c d", j=HPC, c=NFC), in_=wq_in.rearrange("j (c p) d -> p j c d", p=128)), 16)
            ld.inc(sync.dma_start(out=fm_view(wk_s[:, :], NFC), in_=fm_dram(wk_in)), 16)
            ld.inc(sync.dma_start(out=fm_view(wv_s[:, :], NFC), in_=fm_dram(wv_in)), 16)
            ld.inc(sync.dma_start(out=cos_s[:, :], in_=cos_in), 16)
            ld.inc(sync.dma_start(out=sin_s[:, :], in_=sin_in), 16)
        w_loaded = ld.n

        xt = b.sbuf("xt", [128, 2 * NFC * 512], BF16, es=sc2b)
        rsc = b.sbuf("rsc", [64, 2 * 512], F32, es=sc2b)
        vfm = b.sbuf("vfm", [128, 2 * 512], BF16, es=sc2b)

        q_ps0 = b.psum("q_ps0", [128, 512], F32, es=sc2b)
        q_ps1 = b.psum("q_ps1", [128, 512], F32, es=sc2b)
        k_ps = b.psum("k_ps", [128, 512], F32, es=sc2b)
        v_ps = b.psum("v_ps", [128, 512], F32, es=sc2b)
        t_ps = b.psum("t_ps", [128, 128], BF16, es=sc2b)

        ldq = b.sem("ldq")
        mmk = b.sem("mmk")
        rp = b.sem("rp")
        tvt = b.sem("tvt")
        tve = b.sem("tve")

        @block.sync
        def _(sync):
            sync.wait_ge(cc.h, 1)
            for tt in range(NQT):
                buf = tt % 2
                if tt >= 2:
                    sync.wait_ge(rp.h, 4 * (tt - 1))
                src = ag_out.ap()[D * (tt // 2):D * (tt // 2 + 1),
                                  (tt % 2) * 512:(tt % 2) * 512 + 512]
                ldq.inc(sync.dma_start(out=fm_view(xt[:, buf * NFC * 512:(buf + 1) * NFC * 512], NFC),
                                       in_=fm_dram(src)), 16)

        @block.tensor
        def _(tensor):
            tensor.wait_ge(ld.h, w_loaded)
            for tt in range(NQT):
                buf = tt % 2
                xsl = lambda fc: xt[:, (buf * NFC + fc) * 512:(buf * NFC + fc + 1) * 512]
                tensor.wait_ge(ldq.h, 16 * (tt + 1))
                if tt >= 1:
                    tensor.wait_ge(rp.h, 4 * tt)
                for jps, wsl in ((q_ps0, 0), (q_ps1, 1)):
                    for fc in range(NFC):
                        mm = tensor.matmul(jps[:, :],
                                           wq_s[:, (wsl * NFC + fc) * 128:(wsl * NFC + fc + 1) * 128],
                                           xsl(fc), start=(fc == 0), stop=(fc == NFC - 1))
                    mmk.inc(mm)
                for fc in range(NFC):
                    mm = tensor.matmul(k_ps[:, :], wk_s[:, fc * 128:(fc + 1) * 128],
                                       xsl(fc), start=(fc == 0), stop=(fc == NFC - 1))
                mmk.inc(mm)
                for fc in range(NFC):
                    mm = tensor.matmul(v_ps[:, :], wv_s[:, fc * 128:(fc + 1) * 128],
                                       xsl(fc), start=(fc == 0), stop=(fc == NFC - 1))
                mmk.inc(mm)
                tts = [tt - 1] if tt >= 1 else []
                if tt == NQT - 1:
                    tts.append(tt)
                for tp in tts:
                    tensor.wait_ge(rp.h, 4 * tp + 4)
                    for i in range(4):
                        idx = tp * 4 + i
                        if idx >= 1:
                            tensor.wait_ge(tve.h, idx)
                        tvt.inc(tensor.transpose(t_ps[:, :],
                                                 vfm[:, (tp % 2) * 512 + i * 128:(tp % 2) * 512 + (i + 1) * 128],
                                                 ident_b[:, :]))

        def rope_emit(vector, ps, dst, tt):
            c_sl = cos_s[:, tt * 512:(tt + 1) * 512]
            s_sl = sin_s[:, tt * 512:(tt + 1) * 512]
            e, o = ps[0:64, :], ps[64:128, :]
            t1 = rsc[:, 0 * 512:1 * 512]
            t2 = rsc[:, 1 * 512:2 * 512]
            vector.tensor_tensor(t1, e, c_sl, ALU.mult)
            vector.tensor_tensor(t2, o, s_sl, ALU.mult)
            vector.tensor_tensor(dst[0:64, :], t1, t2, ALU.subtract)
            vector.tensor_tensor(t1, e, s_sl, ALU.mult)
            vector.tensor_tensor(t2, o, c_sl, ALU.mult)
            return vector.tensor_tensor(dst[64:128, :], t1, t2, ALU.add)

        @block.vector
        def _(vector):
            for tt in range(NQT):
                buf = tt % 2
                base = 4 * tt
                vector.wait_ge(mmk.h, base + 1)
                rp.inc(rope_emit(vector, q_ps0, q_s[:, 0 * T + tt * 512:0 * T + tt * 512 + 512], tt))
                vector.wait_ge(mmk.h, base + 2)
                rp.inc(rope_emit(vector, q_ps1, q_s[:, 1 * T + tt * 512:1 * T + tt * 512 + 512], tt))
                vector.wait_ge(mmk.h, base + 3)
                rp.inc(rope_emit(vector, k_ps, k_s[:, tt * 512:(tt + 1) * 512], tt))
                vector.wait_ge(mmk.h, base + 4)
                if tt >= 2:
                    vector.wait_ge(tvt.h, 4 * (tt - 1))
                rp.inc(vector.tensor_copy(vfm[:, buf * 512:(buf + 1) * 512], v_ps[:, :]))
                tps = [tt - 1] if tt >= 1 else []
                if tt == NQT - 1:
                    tps.append(tt)
                for tp in tps:
                    for i in range(4):
                        idx = tp * 4 + i
                        vector.wait_ge(tvt.h, idx + 1)
                        tve.inc(vector.tensor_copy(v_tm[:, idx * 128:(idx + 1) * 128], t_ps[:, :]))

        sc2b.close()

        p_s = b.sbuf("p_s", [128, 2 * 512], BF16)
        mask_s = b.sbuf("mask_s", [128, 4 * 512], F32)
        o_stage = b.sbuf("o_stage", [128, 2 * 512], BF16)
        nrm_s = b.sbuf("nrm_s", [128, 512], F32)

        @block.sync
        def _(sync):
            sync.wait_ge(tve.h, 4 * NQT)
            ld.inc(sync.dma_start(out=mask_s[:, :].rearrange("p (d t) -> p d t", d=4), in_=masks_in.rearrange("d p t -> p d t")), 16)
        mask_ld = ld.n

        s_ps = [b.psum("s_ps0", [128, 512], F32), b.psum("s_ps1", [128, 512], F32)]
        o_ps = b.psum("o_ps", [128, 512], F32)
        n_ps = b.psum("n_ps", [128, 512], F32)

        sS = b.sem("sS")
        sM = b.sem("sM")
        sP = b.sem("sP")
        sC = b.sem("sC")
        sE = b.sem("sE")
        sO = b.sem("sO")

        jobs = []
        for j in range(HPC):
            for qt in range(NQT):
                jobs.append((j, qt, 4 * qt + 4, 4 * qt))
        iters = []
        for jb, (j, qt, nkc, diag0) in enumerate(jobs):
            for kc in range(nkc):
                iters.append(dict(jb=jb, j=j, qt=qt, kc=kc, first=(kc == 0),
                                  last=(kc == nkc - 1), diag=(kc >= diag0),
                                  delta=kc - diag0))
        nd_cum = np.cumsum([1 if it["diag"] else 0 for it in iters]).tolist()
        qkv_rp = 4 * NQT
        qkv_tve = 4 * NQT

        @block.tensor
        def _(tensor):
            tensor.wait_ge(rp.h, qkv_rp)
            tensor.wait_ge(tve.h, qkv_tve)
            tensor.wait_ge(ld.h, mask_ld)
            for i, it in enumerate(iters):
                q_sl = q_s[:, it["j"] * T + it["qt"] * 512: it["j"] * T + it["qt"] * 512 + 512]
                if i >= 2:
                    tensor.wait_ge(sP.h, i - 1)
                sS.inc(tensor.matmul(s_ps[i % 2][:, :],
                                     k_s[:, it["kc"] * 128:(it["kc"] + 1) * 128],
                                     q_sl, start=True, stop=True))
                if i >= 1:
                    pit = iters[i - 1]
                    if pit["first"]:
                        tensor.wait_ge(sE.h, pit["jb"])
                    tensor.wait_ge(sP.h, i)
                    tensor.matmul(o_ps[:, :], v_tm[:, pit["kc"] * 128:(pit["kc"] + 1) * 128],
                                  p_s[:, ((i - 1) % 2) * 512:((i - 1) % 2 + 1) * 512],
                                  start=pit["first"], stop=pit["last"])
                    sC.inc(tensor.matmul(n_ps[:, :], ones_bf[:, :],
                                         p_s[:, ((i - 1) % 2) * 512:((i - 1) % 2 + 1) * 512],
                                         start=pit["first"], stop=pit["last"]))
            i = len(iters)
            pit = iters[i - 1]
            if pit["first"]:
                tensor.wait_ge(sE.h, pit["jb"])
            tensor.wait_ge(sP.h, i)
            tensor.matmul(o_ps[:, :], v_tm[:, pit["kc"] * 128:(pit["kc"] + 1) * 128],
                          p_s[:, ((i - 1) % 2) * 512:((i - 1) % 2 + 1) * 512],
                          start=pit["first"], stop=pit["last"])
            sC.inc(tensor.matmul(n_ps[:, :], ones_bf[:, :],
                                 p_s[:, ((i - 1) % 2) * 512:((i - 1) % 2 + 1) * 512],
                                 start=pit["first"], stop=pit["last"]))

        @block.vector
        def _(vector):
            i = 0
            for jb, (j, qt, nkc, diag0) in enumerate(jobs):
                for kc in range(nkc):
                    it = iters[i]
                    if it["diag"]:
                        vector.wait_ge(sS.h, i + 1)
                        sM.inc(vector.tensor_tensor(
                            s_ps[i % 2][:, :], s_ps[i % 2][:, :],
                            mask_s[:, it["delta"] * 512:(it["delta"] + 1) * 512], ALU.add))
                    i += 1
                vector.wait_ge(sC.h, i)
                if jb >= 2:
                    vector.wait_ge(sO.h, 16 * (jb - 1))
                vector.reciprocal(nrm_s[:, :], n_ps[:, :])
                sE.inc(vector.tensor_tensor(o_stage[:, (jb % 2) * 512:(jb % 2 + 1) * 512],
                                            o_ps[:, :], nrm_s[:, :], ALU.mult))

        @block.scalar
        def _(scalar):
            for i, it in enumerate(iters):
                if it["diag"]:
                    scalar.wait_ge(sM.h, nd_cum[i])
                else:
                    scalar.wait_ge(sS.h, i + 1)
                if i >= 2:
                    scalar.wait_ge(sC.h, i - 1)
                sP.inc(scalar.activation(p_s[:, (i % 2) * 512:(i % 2 + 1) * 512],
                                         s_ps[i % 2][:, :], AF.Exp))

        @block.gpsimd
        def _(g):
            for jb, (j, qt, nkc, diag0) in enumerate(jobs):
                g.wait_ge(sE.h, jb + 1)
                sO.inc(g.dma_start(out=oag_in.ap()[j * 128:(j + 1) * 128, qt * 512:(qt + 1) * 512],
                                   in_=o_stage[:, (jb % 2) * 512:(jb % 2 + 1) * 512]), 16)
            g.wait_ge(sO.h, 16 * len(jobs))
            cc.inc(g.collective_compute(
                "AllGather", ALU.bypass, replica_groups=[list(range(NCORES))],
                ins=[oag_in.ap().opt()], outs=[oag_out.ap().opt()]))

        tok_in = b.din("tok_off", [1, 1], mybir.dt.int32)
        oa_s = b.sbuf("oa_s", [128, 2 * NFC * 512], BF16)
        wo_s = b.sbuf("wo_s", [128, 2 * NFC * 128], BF16)
        r_stage = b.sbuf("r_stage", [128, 2 * 512], F32)
        stok = b.sbuf("stok", [1, 1], mybir.dt.int32)

        r_ps = [b.psum("r_ps0", [128, 512], F32), b.psum("r_ps1", [128, 512], F32)]
        lo = b.sem("lo")
        lw = b.sem("lw")
        mo = b.sem("mo")
        re = b.sem("re")
        ro = b.sem("ro")

        @block.sync
        def _(sync):
            ld.inc(sync.dma_start(out=stok[:, :], in_=tok_in), 16)
            for oc in range(NFC):
                if oc >= 2:
                    sync.wait_ge(mo.h, 2 * (oc - 1))
                lw.inc(sync.dma_start(
                    out=fm_view(wo_s[:, (oc % 2) * NFC * 128:(oc % 2 + 1) * NFC * 128], NFC),
                    in_=fm_dram(wo_in[:, oc * 128:(oc + 1) * 128])), 16)
        tok_ld = ld.n

        @block.gpsimd
        def _(g):
            g.wait_ge(cc.h, 2)
            g.wait_ge(ld.h, tok_ld)
            with g.register("r_tok") as r:
                g.reg_load(r, stok[0:1, 0:1])
                for tt2 in range(2):
                    ofs = g.snap(r, min_val=0, max_val=T - 512)
                    src = oag_out.ap().rearrange("(h p) t -> p h t", p=128)[:, :, bass.ds(ofs, 512)]
                    dst = oa_s[:, tt2 * NFC * 512:(tt2 + 1) * NFC * 512].rearrange("p (h t) -> p h t", h=NFC)
                    lo.inc(g.dma_start(out=dst, in_=src), 16)
                    if tt2 == 0:
                        g.reg_add(r, r, 512)

        @block.tensor
        def _(tensor):
            tensor.wait_ge(lo.h, 32)
            for oc in range(NFC):
                tensor.wait_ge(lw.h, 16 * (oc + 1))
                for tt2 in range(2):
                    if oc >= 1:
                        tensor.wait_ge(re.h, 2 * (oc - 1) + tt2 + 1)
                    for hd in range(NFC):
                        mm = tensor.matmul(
                            r_ps[tt2][:, :],
                            wo_s[:, ((oc % 2) * NFC + hd) * 128:((oc % 2) * NFC + hd + 1) * 128],
                            oa_s[:, (tt2 * NFC + hd) * 512:(tt2 * NFC + hd + 1) * 512],
                            start=(hd == 0), stop=(hd == NFC - 1))
                    mo.inc(mm)

        @block.vector
        def _(vector):
            for oc in range(NFC):
                for tt2 in range(2):
                    vector.wait_ge(mo.h, 2 * oc + tt2 + 1)
                    if oc >= 1:
                        vector.wait_ge(ro.h, 16 * (2 * (oc - 1) + tt2 + 1))
                    re.inc(vector.tensor_copy(r_stage[:, tt2 * 512:(tt2 + 1) * 512],
                                              r_ps[tt2][:, :]))

        @block.gpsimd
        def _(g):
            for oc in range(NFC):
                for tt2 in range(2):
                    g.wait_ge(re.h, 2 * oc + tt2 + 1)
                    ro.inc(g.dma_start(
                        out=attn_out[oc * 128:(oc + 1) * 128, tt2 * 512:(tt2 + 1) * 512],
                        in_=r_stage[:, tt2 * 512:(tt2 + 1) * 512]), 16)
            g.wait_ge(ro.h, 16 * 2 * NFC)
            g.wait_ge(st.h, st.n)

    return b



def build_moe():
    b = B()
    nc = b.nc
    x_in = b.din("x_fm", [D, TC], F32)
    attn_in = b.din("attn_in", [D, TC], F32)
    xn_in = b.din("xn_in", [D, TC], BF16)
    rT_in = b.din("rT", [D, 8], BF16)
    sgT_in = b.din("sgT", [D, MOE_I], BF16)
    suT_in = b.din("suT", [D, MOE_I], BF16)
    sdT_in = b.din("sdT", [MOE_I, D], BF16)
    wgT_in = b.din("wgT", [8, D, MOE_I], BF16)
    wuT_in = b.din("wuT", [8, D, MOE_I], BF16)
    wdT_in = b.din("wdT", [8, MOE_I, D], BF16)
    sel8_in = b.din("sel8", [8, 8 * 128], BF16)
    ident_in = b.din("ident", [128, 128], F32)

    x1_out = b.dout("x1_out", [D, TC], F32)
    xn1_out = b.dout("xn1_out", [D, TC], BF16)

    es = b.es
    with es:
        block = es.enter_context(nc.Block())
        ld = b.sem("ld")
        gl = b.sem("gl")
        st = b.sem("st")

        ones_col = b.sbuf("ones_col", [128, 1], F32)
        ones_row = b.sbuf("ones_row", [1, 128], F32)
        cst = b.sbuf("cst", [128, 4], F32)
        ident_f = b.sbuf("ident_f", [128, 128], F32)
        cset = b.sem("cset")

        @block.vector
        def _(vector):
            vector.memset(ones_col[:, :], 1.0)
            vector.memset(ones_row[:, :], 1.0)
            vector.memset(cst[:, 0:1], 1e-5)
            cset.inc(vector.memset(cst[:, 1:2], 0.0))

        x1a = b.sbuf("x1a", [128, NFC * TC], F32)
        xn = b.sbuf("xn", [128, NFC * TC], BF16)

        @block.sync
        def _(sync):
            ld.inc(sync.dma_start(out=fm_view(xn[:, :], NFC), in_=fm_dram(xn_in)), 16)
            ld.inc(sync.dma_start(out=ident_f[:, :], in_=ident_in), 16)
        xn_ld = ld.n

        m1 = b.sem("m1")

        @block.gpsimd
        def _(g):
            gl.inc(g.dma_start(out=fm_view(x1a[:, :], NFC), in_=fm_dram(x_in)), 16)
            g.wait_ge(gl.h, 16)
            gl.inc(g.dma_start(out=fm_view(x1a[:, :], NFC), in_=fm_dram(attn_in),
                               accum_op=ALU.add), 16)

        @block.vector
        def _(vector):
            vector.wait_ge(gl.h, 32)
            vector.wait_ge(ld.h, xn_ld)
            for fc in range(NFC):
                last = vector.tensor_tensor(x1a[:, fc * TC:(fc + 1) * TC],
                                            x1a[:, fc * TC:(fc + 1) * TC],
                                            xn[:, fc * TC:(fc + 1) * TC], ALU.add)
            m1.inc(last)

        rT_s = b.sbuf("rT_s", [128, NFC * 8], BF16)
        sel_s = b.sbuf("sel_s", [8, 8 * 128], BF16)
        comb_tm = b.sbuf("comb_tm", [128, 8 * 8], F32)
        comb_fm = b.sbuf("comb_fm", [8, TC], BF16)
        rsc = b.sbuf("moe_rsc", [128, 8 * 8], F32)
        rcol = b.sbuf("rcol", [128, 8], F32)

        @block.sync
        def _(sync):
            ld.inc(sync.dma_start(out=rT_s[:, :].rearrange("p (c e) -> p c e", c=NFC),
                                  in_=rT_in.rearrange("(c p) e -> p c e", p=128)), 16)
            ld.inc(sync.dma_start(out=sel_s[:, :], in_=sel8_in), 16)
        r_ld = ld.n

        sc_r = ExitStack()
        r_ps = b.psum("r_ps", [128, 8], F32, es=sc_r)
        t_ps = b.psum("t_ps", [8, 128], F32, es=sc_r)

        rm = b.sem("rm")
        rsv = b.sem("rsv")
        rse = b.sem("rse")
        rt = b.sem("rt")
        rf = b.sem("rf")

        @block.tensor
        def _(tensor):
            tensor.wait_ge(ld.h, r_ld)
            for tc in range(8):
                if tc >= 1:
                    tensor.wait_ge(rsv.h, 2 * tc - 1)
                for fc in range(NFC):
                    mm = tensor.matmul(r_ps[:, :], xn[:, fc * TC + tc * 128:fc * TC + (tc + 1) * 128],
                                       rT_s[:, fc * 8:(fc + 1) * 8],
                                       start=(fc == 0), stop=(fc == NFC - 1))
                rm.inc(mm)

        vch = b.sem("vch")

        @block.vector
        def _(vector):
            def ch(inst):
                vch.inc(inst)
                vector.wait_ge(vch.h, vch.n)
            for tc in range(8):
                w = rsc[:, 0:8]
                ebuf = rsc[:, 8:16]
                tbuf = rsc[:, 16:24]
                vector.wait_ge(rm.h, tc + 1)
                ch(vector.reduce_max(rcol[:, 0:1], r_ps[:, :], axis=mybir.AxisListType.X))
                rsv.inc(vector.tensor_scalar(w, r_ps[:, :], rcol[:, 0:1], None, ALU.subtract))
                vector.wait_ge(rse.h, tc + 1)
                ch(vector.reciprocal(rcol[:, 1:2], rcol[:, 2:3]))
                ch(vector.tensor_scalar(w, ebuf, rcol[:, 1:2], None, ALU.mult))
                ch(vector.reduce_max(rcol[:, 3:4], w, axis=mybir.AxisListType.X))
                ch(vector.tensor_scalar(tbuf, w, rcol[:, 3:4], None, ALU.is_ge))
                ch(vector.scalar_tensor_tensor(tbuf, tbuf, -1e9, w, ALU.mult, ALU.add))
                ch(vector.reduce_max(rcol[:, 5:6], tbuf, axis=mybir.AxisListType.X))
                ch(vector.tensor_scalar(tbuf, w, rcol[:, 5:6], None, ALU.is_ge))
                rsv.inc(vector.tensor_tensor(comb_tm[:, tc * 8:(tc + 1) * 8], w, tbuf, ALU.mult))

        @block.scalar
        def _(scalar):
            for tc in range(8):
                scalar.wait_ge(rsv.h, 2 * tc + 1)
                if tc >= 1:
                    scalar.wait_ge(rsv.h, 2 * tc)
                rse.inc(scalar.activation(rsc[:, 8:16], rsc[:, 0:8], AF.Exp,
                                          accum_out=rcol[:, 2:3]))

        @block.tensor
        def _(tensor):
            for tc in range(8):
                tensor.wait_ge(rsv.h, 2 * (tc + 1))
                if tc >= 1:
                    tensor.wait_ge(rf.h, tc)
                rt.inc(tensor.transpose(t_ps[:, :], comb_tm[:, tc * 8:(tc + 1) * 8], ident_f[:, :]))

        @block.vector
        def _(vector):
            for tc in range(8):
                vector.wait_ge(rt.h, tc + 1)
                rf.inc(vector.tensor_copy(comb_fm[:, tc * 128:(tc + 1) * 128], t_ps[:, :]))

        sc_exp = ExitStack()
        he_buf = b.sbuf("he_buf", [128, NIC * TC], BF16, es=sc_exp)
        wg_buf = b.sbuf("wg_buf", [128, 2 * NFC * 128], BF16, es=sc_exp)
        wu_buf = b.sbuf("wu_buf", [128, 2 * NFC * 128], BF16, es=sc_exp)
        wd_buf = b.sbuf("wd_buf", [128, 2 * NIC * 128], BF16, es=sc_exp)
        sig_buf = b.sbuf("sig_buf", [128, 2 * 512], F32, es=sc_exp)
        tsc_buf = b.sbuf("tsc_buf", [128, 512], F32, es=sc_exp)
        comb_bc = b.sbuf("comb_bc", [128, TC], F32, es=sc_exp)

        sc_r.close()
        bc_ps = b.psum("bc_ps", [128, 1024], F32, es=sc_exp)
        g_ps = [b.psum("g_ps0", [128, 512], F32, es=sc_exp), b.psum("g_ps1", [128, 512], F32, es=sc_exp)]
        u_ps = [b.psum("u_ps0", [128, 512], F32, es=sc_exp), b.psum("u_ps1", [128, 512], F32, es=sc_exp)]
        d_ps = [b.psum("d_ps0", [128, 512], F32, es=sc_exp), b.psum("d_ps1", [128, 512], F32, es=sc_exp)]

        lwg = b.sem("lwg")
        lwd = b.sem("lwd")
        mmg = b.sem("mmg")
        sg = b.sem("sg")
        he = b.sem("he")
        cbc = b.sem("cbc")
        mmd = b.sem("mmd")
        dev_ = b.sem("dev")

        NEXP = 9

        def gu_src(e, kind, ic):
            if e == 8:
                base = sgT_in if kind == "g" else suT_in
                sl = base[:, ic * 128:(ic + 1) * 128]
            else:
                base = wgT_in if kind == "g" else wuT_in
                sl = base[e, :, ic * 128:(ic + 1) * 128]
            return sl.rearrange("(c p) d -> p c d", p=128)

        def wd_src(e, oc):
            if e == 8:
                sl = sdT_in[:, oc * 128:(oc + 1) * 128]
            else:
                sl = wdT_in[e, :, oc * 128:(oc + 1) * 128]
            return sl.rearrange("(c p) d -> p c d", p=128)

        @block.sync
        def _(sync):
            k = 0
            for e in range(NEXP):
                for ic in range(NIC):
                    if k >= 2:
                        sync.wait_ge(mmg.h, 4 * (k - 1))
                    buf = k % 2
                    lwg.inc(sync.dma_start(
                        out=wg_buf[:, buf * NFC * 128:(buf + 1) * NFC * 128].rearrange("p (c d) -> p c d", c=NFC),
                        in_=gu_src(e, "g", ic)), 16)
                    lwg.inc(sync.dma_start(
                        out=wu_buf[:, buf * NFC * 128:(buf + 1) * NFC * 128].rearrange("p (c d) -> p c d", c=NFC),
                        in_=gu_src(e, "u", ic)), 16)
                    k += 1

        @block.gpsimd
        def _(g):
            kd = 0
            for e in range(NEXP):
                for oc in range(NFC):
                    if kd >= 2:
                        g.wait_ge(mmd.h, 2 * (kd - 1))
                    buf = kd % 2
                    lwd.inc(g.dma_start(
                        out=wd_buf[:, buf * NIC * 128:(buf + 1) * NIC * 128].rearrange("p (c d) -> p c d", c=NIC),
                        in_=wd_src(e, oc)), 16)
                    kd += 1

        @block.tensor
        def _(tensor):
            k = 0
            kt = 0
            kd = 0
            kdt = 0
            for e in range(NEXP):
                if e < 8:
                    tensor.wait_ge(rf.h, 8)
                    if e >= 1:
                        tensor.wait_ge(he.h, 2 * NIC * e)
                    for th in range(2):
                        cbc.inc(tensor.matmul(bc_ps[:, th * 512:(th + 1) * 512],
                                              sel_s[:, e * 128:(e + 1) * 128],
                                              comb_fm[:, th * 512:(th + 1) * 512],
                                              start=True, stop=True))
                for ic in range(NIC):
                    tensor.wait_ge(lwg.h, 32 * (k + 1))
                    buf = k % 2
                    for th in range(2):
                        pb = kt % 2
                        if kt >= 2:
                            tensor.wait_ge(he.h, kt - 1)
                        for fc in range(NFC):
                            mm = tensor.matmul(g_ps[pb][:, :],
                                               wg_buf[:, (buf * NFC + fc) * 128:(buf * NFC + fc + 1) * 128],
                                               xn[:, fc * TC + th * 512:fc * TC + th * 512 + 512],
                                               start=(fc == 0), stop=(fc == NFC - 1))
                        mmg.inc(mm)
                        for fc in range(NFC):
                            mm = tensor.matmul(u_ps[pb][:, :],
                                               wu_buf[:, (buf * NFC + fc) * 128:(buf * NFC + fc + 1) * 128],
                                               xn[:, fc * TC + th * 512:fc * TC + th * 512 + 512],
                                               start=(fc == 0), stop=(fc == NFC - 1))
                        mmg.inc(mm)
                        kt += 1
                    k += 1
                tensor.wait_ge(he.h, 2 * NIC * (e + 1))
                for oc in range(NFC):
                    tensor.wait_ge(lwd.h, 16 * (kd + 1))
                    bufd = kd % 2
                    for th in range(2):
                        pb = kdt % 2
                        if kdt >= 2:
                            tensor.wait_ge(dev_.h, kdt - 1)
                        for ic in range(NIC):
                            mm = tensor.matmul(d_ps[pb][:, :],
                                               wd_buf[:, (bufd * NIC + ic) * 128:(bufd * NIC + ic + 1) * 128],
                                               he_buf[:, ic * TC + th * 512:ic * TC + th * 512 + 512],
                                               start=(ic == 0), stop=(ic == NIC - 1))
                        mmd.inc(mm)
                        kdt += 1
                    kd += 1

        @block.scalar
        def _(scalar):
            kt = 0
            for e in range(NEXP):
                for ic in range(NIC):
                    for th in range(2):
                        scalar.wait_ge(mmg.h, 2 * kt + 1)
                        if kt >= 2:
                            scalar.wait_ge(he.h, kt - 1)
                        sg.inc(scalar.activation(sig_buf[:, (kt % 2) * 512:(kt % 2 + 1) * 512],
                                                 g_ps[kt % 2][:, :], AF.Sigmoid))
                        kt += 1

        @block.vector
        def _(vector):
            kt = 0
            kdt = 0
            for e in range(NEXP):
                if e < 8:
                    for th in range(2):
                        vector.wait_ge(cbc.h, 2 * e + th + 1)
                        vector.tensor_copy(comb_bc[:, th * 512:(th + 1) * 512],
                                           bc_ps[:, th * 512:(th + 1) * 512])
                for ic in range(NIC):
                    for th in range(2):
                        pb = kt % 2
                        vector.wait_ge(sg.h, kt + 1)
                        vector.wait_ge(mmg.h, 2 * kt + 2)
                        vector.tensor_tensor(tsc_buf[:, :], sig_buf[:, pb * 512:(pb + 1) * 512],
                                             u_ps[pb][:, :], ALU.mult)
                        dst = he_buf[:, ic * TC + th * 512:ic * TC + th * 512 + 512]
                        if e < 8:
                            he.inc(vector.tensor_tensor(dst, tsc_buf[:, :],
                                                        comb_bc[:, th * 512:(th + 1) * 512], ALU.mult))
                        else:
                            he.inc(vector.tensor_copy(dst, tsc_buf[:, :]))
                        kt += 1
                for oc in range(NFC):
                    for th in range(2):
                        pb = kdt % 2
                        vector.wait_ge(mmd.h, kdt + 1)
                        vector.wait_ge(m1.h, 1)
                        dev_.inc(vector.tensor_tensor(
                            x1a[:, oc * TC + th * 512:oc * TC + th * 512 + 512],
                            x1a[:, oc * TC + th * 512:oc * TC + th * 512 + 512],
                            d_ps[pb][:, :], ALU.add))
                        kdt += 1

        sc_exp.close()
        sc_ln = ExitStack()
        scratch1 = b.sbuf("lnscr1", [128, 2 * TC], F32, es=sc_ln)
        stats1 = b.sbuf("lnstats1", [1, 3 * TC], F32, es=sc_ln)
        xn1 = b.sbuf("xn1", [128, NFC * TC], BF16, es=sc_ln)
        ln1_done = b.sem("ln1_done")
        emit_ln_fm(b, block, xs=x1a, xn=xn1, tokens=TC, scratch=scratch1, stats=stats1,
                   ones_col=ones_col, ones_row=ones_row, eps_ap=cst[0:1, 0:1],
                   ready=[(dev_.h, 2 * NFC * NEXP), (cset.h, 1)], done=ln1_done)

        @block.gpsimd
        def _(g):
            g.wait_ge(ln1_done.h, 1)
            st.inc(g.dma_start(out=fm_dram(x1_out), in_=fm_sbuf_to_dram(x1a[:, :], NFC)), 16)
            st.inc(g.dma_start(out=fm_dram(xn1_out), in_=fm_sbuf_to_dram(xn1[:, :], NFC)), 16)
            g.wait_ge(st.h, 32)
        sc_ln.close()
    return b



XBW = 1027

def build_mamba():
    b = B()
    nc = b.nc
    xn1_in = b.din("xn1_in", [D, TC], BF16)
    wxbc_in = b.din("w_xbc", [D, CONV_DIM], BF16)
    wdt_in = b.din("w_dt", [D, 64], BF16)
    wz_in = b.din("w_z", [D, D_INNER], BF16)
    wout_in = b.din("w_out", [D_INNER, D], BF16)
    convw_in = b.din("conv_w", [128, NCC * 4], F32)
    convb_in = b.din("conv_b", [128, NCC], F32)
    dtb_in = b.din("dt_bias", [64, 1], F32)
    alog_in = b.din("A_log", [64, 1], F32)
    drow_in = b.din("D_row", [1, 64], F32)
    nw_in = b.din("nw_row", [1, D_INNER], F32)
    ident_in = b.din("ident", [128, 128], F32)
    identb_in = b.din("ident_bf", [128, 128], BF16)
    tril_in = b.din("tril", [128, 128], F32)
    mpref_in = b.din("m_pref", [128, 7], F32)
    whalo_in = b.din("w_halo", [128, 8], F32)

    mamba_out = b.dout("mamba_out", [D, TC], F32)

    halo_in_d = b.dram("halo_in", [128, NCC * 3], BF16)
    halo_out_d = b.dram("halo_out", [NCORES * 128, NCC * 3], BF16)
    carry_in_d = b.dram("carry_in", [128, D_INNER + 64], F32)
    carry_out_d = b.dram("carry_out", [NCORES * 128, D_INNER + 64], F32)
    y_scr_d = b.dram("y_scr", [128, NCHUNK * D_INNER], BF16)

    es = b.es
    with es:
        block = es.enter_context(nc.Block())
        ld = b.sem("ld")
        gl = b.sem("gl")
        st = b.sem("st")
        cc = b.sem("cc")

        ones_col = b.sbuf("ones_col", [128, 1], F32)
        ones_row = b.sbuf("ones_row", [1, 128], F32)
        ones64 = b.sbuf("ones64", [64, 128], F32)
        i64 = b.sbuf("i64", [64, 64], F32)
        cst = b.sbuf("cst", [128, 8], F32)
        ident_f = b.sbuf("ident_f", [128, 128], F32)
        ident_b = b.sbuf("ident_b", [128, 128], BF16)
        tril_s = b.sbuf("tril_s", [128, 128], F32)
        mpref_s = b.sbuf("mpref_s", [128, 7], F32)
        whalo_s = b.sbuf("whalo_s", [128, 8], F32)
        dtb_s = b.sbuf("dtb_s", [64, 1], F32)
        alog_s = b.sbuf("alog_s", [64, 1], F32)
        drow_s = b.sbuf("drow_s", [1, 64], F32)
        cset = b.sem("cset")

        @block.vector
        def _(vector):
            vector.memset(ones_col[:, :], 1.0)
            vector.memset(ones_row[:, :], 1.0)
            vector.memset(ones64[:, :], 1.0)
            vector.memset(cst[:, 0:1], 1e-5)
            vector.memset(cst[:, 1:2], 0.0)
            vector.memset(cst[:, 2:3], -1.0)
            cset.inc(vector.memset(cst[:, 3:4], 1.0))

        @block.sync
        def _(sync):
            ld.inc(sync.dma_start(out=ident_f[:, :], in_=ident_in), 16)
            ld.inc(sync.dma_start(out=ident_b[:, :], in_=identb_in), 16)
            ld.inc(sync.dma_start(out=tril_s[:, :], in_=tril_in), 16)
            ld.inc(sync.dma_start(out=mpref_s[:, :], in_=mpref_in), 16)
            ld.inc(sync.dma_start(out=whalo_s[:, :], in_=whalo_in), 16)
            ld.inc(sync.dma_start(out=dtb_s[:, :], in_=dtb_in), 16)
            ld.inc(sync.dma_start(out=alog_s[:, :], in_=alog_in), 16)
            ld.inc(sync.dma_start(out=drow_s[:, :], in_=drow_in), 16)
            ld.inc(sync.dma_start(out=i64[:, :], in_=ident_in[0:64, 0:64]), 16)
        const_ld = ld.n

        xn1 = b.sbuf("xn1", [128, NFC * TC], BF16)
        sdect_all = b.sbuf("sdect_all", [128, NCHUNK * 64], F32)
        dbc_all = b.sbuf("dbc_all", [128, NCHUNK * 64], F32)
        c_keep = b.sbuf("c_keep", [128, TC], BF16)
        sc_body = ExitStack()
        xbc = b.sbuf("xbc", [128, NCC * XBW], BF16, es=sc_body)
        convw_s = b.sbuf("convw_s", [128, NCC * 4], F32, es=sc_body)
        convb_s = b.sbuf("convb_s", [128, NCC], F32, es=sc_body)
        dt_raw = b.sbuf("dt_raw", [64, TC], F32, es=sc_body)
        dt_s = b.sbuf("dt_s", [64, TC], F32, es=sc_body)
        dsc = b.sbuf("dsc", [64, TC], F32, es=sc_body)
        a_cs = b.sbuf("a_cs", [64, TC], F32, es=sc_body)
        dec_s = b.sbuf("dec_s", [64, TC], F32, es=sc_body)
        aexp_s = b.sbuf("aexp_s", [64, 2], F32, es=sc_body)
        acst_all = b.sbuf("acst_all", [128, NCHUNK * 64], F32, es=sc_body)
        dtt_all = b.sbuf("dtt_all", [128, NCHUNK * 64], F32, es=sc_body)
        dect_all = b.sbuf("dect_all", [128, NCHUNK * 64], F32, es=sc_body)
        prod_d = b.sbuf("prod_d", [128, 64], F32, es=sc_body)
        rowsc = b.sbuf("rowsc", [1, 64], F32, es=sc_body)
        s_run = b.sbuf("s_run", [128, D_INNER], F32, es=sc_body)
        zsc = b.sbuf("zsc", [64, 128], F32, es=sc_body)
        d_b = b.sbuf("d_b", [128, 64], F32, es=sc_body)

        @block.sync
        def _(sync):
            ld.inc(sync.dma_start(out=fm_view(xn1[:, :], NFC), in_=fm_dram(xn1_in)), 16)
            ld.inc(sync.dma_start(out=convw_s[:, :], in_=convw_in), 16)
            ld.inc(sync.dma_start(out=convb_s[:, :], in_=convb_in), 16)
        xn1_ld = ld.n

        sc_pa = ExitStack()
        wx_buf = b.sbuf("wx_buf", [128, 2 * NFC * 128], BF16, es=sc_pa)
        sc_ip = ExitStack()
        ip_ps = [b.psum("ip_ps0", [128, 512], F32, es=sc_ip), b.psum("ip_ps1", [128, 512], F32, es=sc_ip)]
        lwx = b.sem("lwx")
        mip = b.sem("mip")
        eip = b.sem("eip")

        @block.sync
        def _(sync):
            for oc in range(NCC + 1):
                if oc >= 2:
                    sync.wait_ge(mip.h, 2 * (oc - 1))
                src = wxbc_in[:, oc * 128:(oc + 1) * 128] if oc < NCC else wdt_in
                w = 128 if oc < NCC else 64
                lwx.inc(sync.dma_start(
                    out=wx_buf[:, (oc % 2) * NFC * 128:(oc % 2) * NFC * 128 + NFC * w].rearrange("p (c d) -> p c d", c=NFC),
                    in_=src.rearrange("(c p) d -> p c d", p=128)), 16)

        @block.tensor
        def _(tensor):
            tensor.wait_ge(ld.h, xn1_ld)
            for oc in range(NCC + 1):
                tensor.wait_ge(lwx.h, 16 * (oc + 1))
                w = 128 if oc < NCC else 64
                for th in range(2):
                    k = 2 * oc + th
                    if k >= 2:
                        tensor.wait_ge(eip.h, k - 1)
                    out_ap = ip_ps[k % 2][:, :] if w == 128 else ip_ps[k % 2][0:64, :]
                    for fc in range(NFC):
                        mm = tensor.matmul(out_ap,
                                           wx_buf[:, (oc % 2) * NFC * 128 + fc * w:(oc % 2) * NFC * 128 + fc * w + w],
                                           xn1[:, fc * TC + th * 512:fc * TC + th * 512 + 512],
                                           start=(fc == 0), stop=(fc == NFC - 1))
                    mip.inc(mm)

        @block.vector
        def _(vector):
            for oc in range(NCC + 1):
                for th in range(2):
                    k = 2 * oc + th
                    vector.wait_ge(mip.h, k + 1)
                    if oc < NCC:
                        dst = xbc[:, oc * XBW + 3 + th * 512:oc * XBW + 3 + th * 512 + 512]
                        eip.inc(vector.tensor_copy(dst, ip_ps[k % 2][:, :]))
                    else:
                        eip.inc(vector.tensor_copy(dt_raw[:, th * 512:(th + 1) * 512],
                                                   ip_ps[k % 2][0:64, :]))
        ip_done = eip.n

        halo_sb = b.sbuf("halo_sb", [128, 8 * NCC * 3], BF16, es=sc_pa)
        halo_acc = b.sbuf("halo_acc", [128, NCC * 3], F32, es=sc_pa)
        hcmb = b.sem("hcmb")

        @block.gpsimd
        def _(g):
            g.wait_ge(eip.h, ip_done)
            gl.inc(g.dma_start(
                out=halo_in_d.ap().rearrange("p (c t) -> p c t", c=NCC),
                in_=xbc[:, :].rearrange("p (c t) -> p c t", c=NCC)[:, :, 1024:1027]), 16)
            g.wait_ge(gl.h, gl.n)
            cc.inc(g.collective_compute(
                "AllGather", ALU.bypass, replica_groups=[list(range(NCORES))],
                ins=[halo_in_d.ap().opt()], outs=[halo_out_d.ap().opt()]))
            g.wait_ge(cc.h, 1)
            gl.inc(g.dma_start(
                out=halo_sb[:, :].rearrange("p (j t) -> p j t", j=8),
                in_=halo_out_d.ap().rearrange("(j p) t -> p j t", p=128)), 16)
        halo_gl = gl.n

        @block.vector
        def _(vector):
            vector.wait_ge(gl.h, halo_gl)
            vector.wait_ge(cset.h, 1)
            vector.memset(halo_acc[:, :], 0.0)
            for j in range(8):
                hs = halo_sb[:, j * NCC * 3:(j + 1) * NCC * 3]
                inst = vector.scalar_tensor_tensor(halo_acc[:, :], hs, whalo_s[:, j:j + 1],
                                                   halo_acc[:, :], ALU.mult, ALU.add)
                hcmb.inc(inst)
                vector.wait_ge(hcmb.h, hcmb.n)
            hcmb.inc(vector.tensor_copy(
                xbc[:, :].rearrange("p (c t) -> p c t", c=NCC)[:, :, 0:3],
                halo_acc[:, :].rearrange("p (c t) -> p c t", c=NCC)))
        halo_done = hcmb.n

        cacc = b.sbuf("cacc", [128, 2 * TC], F32, es=sc_pa)
        cvt = b.sem("cvt")
        cvs = b.sem("cvs")

        @block.vector
        def _(vector):
            vector.wait_ge(hcmb.h, halo_done)
            for cci in range(NCC):
                buf = cci % 2
                acc = cacc[:, buf * TC:(buf + 1) * TC]
                base = cci * XBW
                if cci >= 2:
                    vector.wait_ge(cvs.h, cci - 1)
                vector.tensor_scalar(acc, xbc[:, base + 0:base + TC],
                                     convw_s[:, cci * 4 + 0:cci * 4 + 1],
                                     convb_s[:, cci:cci + 1], ALU.mult, ALU.add)
                for kk in range(1, 3):
                    vector.scalar_tensor_tensor(acc, xbc[:, base + kk:base + kk + TC],
                                                convw_s[:, cci * 4 + kk:cci * 4 + kk + 1],
                                                acc, ALU.mult, ALU.add)
                cvt.inc(vector.scalar_tensor_tensor(acc, xbc[:, base + 3:base + 3 + TC],
                                                    convw_s[:, cci * 4 + 3:cci * 4 + 4],
                                                    acc, ALU.mult, ALU.add))

        @block.scalar
        def _(scalar):
            for cci in range(NCC):
                buf = cci % 2
                scalar.wait_ge(cvt.h, cci + 1)
                cvs.inc(scalar.activation(xbc[:, cci * XBW + 3:cci * XBW + 3 + TC],
                                          cacc[:, buf * TC:(buf + 1) * TC], AF.Silu))
        conv_done = cvs.n

        S = Sched(b, "mb")
        AX = mybir.AxisListType.X

        @block.vector
        def _(vector):
            vector.memset(zsc[:, :], 0.0)

        ec = cst[0:64, :]

        S.add("vector", lambda e: e.tensor_scalar(dsc[:, :], dt_raw[:, :], dtb_s[:, :], None, ALU.add),
              waits=[(eip.h, ip_done), (cset.h, 1)])
        S.add("scalar", lambda e: e.activation(dec_s[:, :], dsc[:, :], AF.Abs), waits=[S.up("vector")])
        S.add("scalar", lambda e: e.activation(dec_s[:, :], dec_s[:, :], AF.Exp, scale=ec[:, 2:3]))
        S.add("scalar", lambda e: e.activation(dec_s[:, :], dec_s[:, :], AF.Ln, bias=ec[:, 3:4]))
        S.add("vector", lambda e: e.tensor_scalar(dt_s[:, :], dsc[:, :], 0.0, None, ALU.max))
        S.add("vector", lambda e: e.tensor_tensor(dt_s[:, :], dt_s[:, :], dec_s[:, :], ALU.add),
              waits=[S.up("scalar")])
        S.add("scalar", lambda e: e.activation(aexp_s[:, 0:1], alog_s[:, :], AF.Exp))
        S.add("vector", lambda e: e.tensor_scalar(dsc[:, :], dt_s[:, :], aexp_s[:, 0:1], ec[:, 2:3], ALU.mult, ALU.mult),
              waits=[S.up("scalar")])
        for ci in range(NCHUNK):
            sl = slice(ci * MCH, (ci + 1) * MCH)
            S.add("vector", lambda e, sl=sl: e.tensor_tensor_scan(a_cs[:, sl], dsc[:, sl], zsc[:, :], 0.0, ALU.add, ALU.add))
        for ci in range(NCHUNK):
            sl = slice(ci * MCH, (ci + 1) * MCH)
            lc = a_cs[:, ci * MCH + MCH - 1:ci * MCH + MCH]
            S.add("vector", lambda e, sl=sl, lc=lc: e.tensor_scalar(dec_s[:, sl], a_cs[:, sl], lc, ec[:, 2:3], ALU.subtract, ALU.mult))
        dec_ts_done = S.up("vector")
        for ci in range(NCHUNK):
            sl = slice(ci * MCH, (ci + 1) * MCH)
            S.add("scalar", lambda e, sl=sl: e.activation(dec_s[:, sl], dec_s[:, sl], AF.Exp),
                  waits=[dec_ts_done])

        sc_ip.close()
        sc_pa.close()
        tp_f = b.psum("tp_f", [128, 128], F32)
        tp_b = b.psum("tp_b", [128, 128], BF16)
        bcA = b.psum("bcA", [128, 128], F32)
        bcB = b.psum("bcB", [128, 128], F32)
        sh4 = b.psum("sh4", [128, 2048], F32)

        S.add("tensor", lambda e: e.matmul(tp_f[:, 0:64], ones_row[:, :], drow_s[:, :], start=True, stop=True),
              waits=[(ld.h, const_ld), (cset.h, 1)])
        S.add("vector", lambda e: e.tensor_copy(d_b[:, :], tp_f[:, 0:64]), waits=[S.up("tensor")])
        S.add("vector", lambda e: e.memset(prod_d[:, :], 1.0))

        id64 = ident_f[0:64, 0:64]
        for ci in range(NCHUNK):
            sl = slice(ci * MCH, (ci + 1) * MCH)
            s64 = slice(ci * 64, (ci + 1) * 64)
            S.add("tensor", lambda e, sl=sl: e.transpose(tp_f[:, 0:64], a_cs[:, sl], id64),
                  waits=[S.up("vector"), S.up("scalar")])
            S.add("vector", lambda e, s64=s64: e.tensor_copy(acst_all[:, s64], tp_f[:, 0:64]),
                  waits=[S.up("tensor")])
            S.add("scalar", lambda e, s64=s64: e.activation(sdect_all[:, s64], acst_all[:, s64], AF.Exp),
                  waits=[S.up("vector")])
            S.add("tensor", lambda e, sl=sl: e.transpose(tp_f[:, 0:64], dt_s[:, sl], id64),
                  waits=[S.up("vector")])
            S.add("vector", lambda e, s64=s64: e.tensor_copy(dtt_all[:, s64], tp_f[:, 0:64]),
                  waits=[S.up("tensor")])
            S.add("tensor", lambda e, sl=sl: e.transpose(tp_f[:, 0:64], dec_s[:, sl], id64),
                  waits=[S.up("vector")])
            S.add("vector", lambda e, s64=s64: e.tensor_copy(dect_all[:, s64], tp_f[:, 0:64]),
                  waits=[S.up("tensor")])
            S.add("scalar", lambda e, ci=ci: e.activation(aexp_s[:, 1:2],
                  a_cs[:, ci * MCH + MCH - 1:ci * MCH + MCH], AF.Exp), waits=[S.up("vector")])
            S.add("tensor", lambda e: e.transpose(tp_f[0:1, 0:64], aexp_s[:, 1:2], id64),
                  waits=[S.up("scalar"), S.up("vector")])
            S.add("vector", lambda e: e.tensor_copy(rowsc[:, :], tp_f[0:1, 0:64]),
                  waits=[S.up("tensor")])
            S.add("tensor", lambda e: e.matmul(tp_f[:, 0:64], ones_row[:, :], rowsc[:, :], start=True, stop=True),
                  waits=[S.up("vector")])
            S.add("vector", lambda e, s64=s64: e.tensor_copy(dbc_all[:, s64], tp_f[:, 0:64]),
                  waits=[S.up("tensor")])
            S.add("vector", lambda e, s64=s64: e.tensor_tensor(prod_d[:, :], prod_d[:, :], dbc_all[:, s64], ALU.mult),
                  waits=[S.up("vector")])

        x_tm = b.sbuf("x_tm", [128, 2048], BF16, es=sc_body)
        xdt_h = b.sbuf("xdt_h", [128, 2048], BF16, es=sc_body)
        xdtd_h = b.sbuf("xdtd_h", [128, 2048], BF16, es=sc_body)
        m_all = b.sbuf("m_all", [128, 32 * 128], BF16, es=sc_body)
        msk_s = b.sbuf("msk_s", [64, 2 * 128], F32, es=sc_body)
        seg_s = b.sbuf("seg_s", [128, 2 * 128], F32, es=sc_body)
        exps_s = b.sbuf("exps_s", [128, 2 * 128], F32, es=sc_body)
        g_tm = b.sbuf("g_tm", [128, 128], F32, es=sc_body)
        b_tm = b.sbuf("b_tm", [128, 128], BF16, es=sc_body)
        y_sc = b.sbuf("y_sc", [128, 2048], F32, es=sc_body)
        y_sc2 = b.sbuf("y_sc2", [128, 2048], F32, es=sc_body)
        s_bf = b.sbuf("s_bf", [128, 2048], BF16, es=sc_body)
        y_stage = b.sbuf("y_stage", [128, 2 * 2048], BF16, es=sc_body)

        def exp_ap(t, col0, nh):
            pstep = t[:, :].ap[0][0]
            return bass.AP(t, col0, [[pstep, 128], [1, nh], [0, 64]])

        S.add("vector", lambda e: e.memset(s_run[:, :], 0.0))

        conv_w_ = (cvs.h, conv_done)
        for ci in range(NCHUNK):
            s64c = ci * 64
            btile = xbc[:, 32 * XBW + 3 + ci * MCH:32 * XBW + 3 + ci * MCH + MCH]
            ctile = xbc[:, 33 * XBW + 3 + ci * MCH:33 * XBW + 3 + ci * MCH + MCH]
            S.add("tensor", lambda e, btile=btile: e.transpose(tp_b[:, :], btile, ident_b[:, :]),
                  waits=[conv_w_, S.up("vector")])
            S.add("vector", lambda e: e.tensor_copy(b_tm[:, :], tp_b[:, :]), waits=[S.up("tensor")])
            S.add("tensor", lambda e, btile=btile, ctile=ctile: e.matmul(bcA[:, :], btile, ctile, start=True, stop=True),
                  waits=[S.up("vector")])
            S.add("vector", lambda e: e.tensor_tensor(g_tm[:, :], bcA[:, :], tril_s[:, :], ALU.mult),
                  waits=[S.up("tensor")])
            for hf in range(2):
                h0 = 32 * hf
                for fc in range(NFC):
                    cci = hf * NFC + fc
                    xt_ = xbc[:, cci * XBW + 3 + ci * MCH:cci * XBW + 3 + ci * MCH + MCH]
                    S.add("tensor", lambda e, xt_=xt_: e.transpose(tp_b[:, :], xt_, ident_b[:, :]),
                          waits=[S.up("vector")])
                    S.add("vector", lambda e, fc=fc: e.tensor_copy(x_tm[:, fc * 128:(fc + 1) * 128], tp_b[:, :]),
                          waits=[S.up("tensor")])
                S.add("vector", lambda e, s64c=s64c, h0=h0: e.tensor_tensor(
                    xdt_h[:, :], x_tm[:, :], exp_ap(dtt_all, s64c + h0, 32), ALU.mult))
                S.add("vector", lambda e, s64c=s64c, h0=h0: e.tensor_tensor(
                    xdtd_h[:, :], xdt_h[:, :], exp_ap(dect_all, s64c + h0, 32), ALU.mult))
                if ci >= 1:
                    S.add("vector", lambda e, hf=hf: e.tensor_copy(s_bf[:, :], s_run[:, hf * 2048:(hf + 1) * 2048]))
                    def yoff(e, ctile=ctile):
                        for q in range(4):
                            mm = e.matmul(sh4[:, q * 512:(q + 1) * 512], ctile, s_bf[:, q * 512:(q + 1) * 512],
                                          start=True, stop=True)
                        return mm
                    S.add("tensor", yoff, waits=[S.up("vector")])
                    S.add("vector", lambda e, s64c=s64c, h0=h0: e.tensor_tensor(
                        y_sc[:, :], sh4[:, :], exp_ap(sdect_all, s64c + h0, 32), ALU.mult),
                        waits=[S.up("tensor")])
                else:
                    S.add("vector", lambda e: e.memset(y_sc[:, :], 0.0))
                for hp in range(32):
                    h = h0 + hp
                    pb = hp % 2
                    S.add("vector", lambda e, pb=pb, ci=ci, h=h: e.tensor_scalar(
                        msk_s[:, pb * 128:(pb + 1) * 128], a_cs[:, ci * MCH:(ci + 1) * MCH],
                        i64[:, h:h + 1], None, ALU.mult),
                        waits=[S.up("tensor")])
                    S.add("tensor", lambda e, pb=pb: e.matmul(
                        bcB[:, :] if pb else bcA[:, :], ones64[:, :],
                        msk_s[:, pb * 128:(pb + 1) * 128], start=True, stop=True),
                        waits=[S.up("vector")])
                    S.add("vector", lambda e, pb=pb, col=s64c + h: e.tensor_scalar(
                        seg_s[:, pb * 128:(pb + 1) * 128], bcB[:, :] if pb else bcA[:, :],
                        acst_all[:, col:col + 1], cst[:, 1:2], ALU.subtract, ALU.min),
                        waits=[S.up("tensor"), S.up("scalar")])
                    S.add("scalar", lambda e, pb=pb: e.activation(
                        exps_s[:, pb * 128:(pb + 1) * 128], seg_s[:, pb * 128:(pb + 1) * 128], AF.Exp),
                        waits=[S.up("vector")])
                    S.add("vector", lambda e, pb=pb, hp=hp: e.tensor_tensor(
                        m_all[:, hp * 128:(hp + 1) * 128], exps_s[:, pb * 128:(pb + 1) * 128],
                        g_tm[:, :], ALU.mult),
                        waits=[S.up("scalar")])
                def ydiag(e):
                    for hp in range(32):
                        mm = e.matmul(sh4[:, hp * 64:(hp + 1) * 64], m_all[:, hp * 128:(hp + 1) * 128],
                                      xdt_h[:, hp * 64:(hp + 1) * 64], start=True, stop=True)
                    return mm
                S.add("tensor", ydiag, waits=[S.up("vector")])
                S.add("vector", lambda e, h0=h0: e.tensor_tensor(
                    y_sc2[:, :], x_tm[:, :], exp_ap(d_b, h0, 32), ALU.mult))
                S.add("vector", lambda e: e.tensor_tensor(y_sc[:, :], y_sc[:, :], y_sc2[:, :], ALU.add))
                kst = (2 * ci + hf) % 2
                S.add("vector", lambda e, kst=kst: e.tensor_tensor(
                    y_stage[:, kst * 2048:(kst + 1) * 2048],
                    y_sc[:, :], sh4[:, :], ALU.add),
                    waits=[S.up("tensor"), S.up("gpsimd")])
                S.add("gpsimd", lambda e, ci=ci, hf=hf, kst=kst: e.dma_start(
                    out=y_scr_d.ap()[:, ci * D_INNER + hf * 2048:ci * D_INNER + (hf + 1) * 2048],
                    in_=y_stage[:, kst * 2048:(kst + 1) * 2048]),
                    waits=[S.up("vector")], amt=16)
                def stmm(e):
                    for q in range(4):
                        mm = e.matmul(sh4[:, q * 512:(q + 1) * 512], b_tm[:, :],
                                      xdtd_h[:, q * 512:(q + 1) * 512], start=True, stop=True)
                    return mm
                S.add("tensor", stmm, waits=[S.up("vector")])
                S.add("vector", lambda e, hf=hf, col=s64c + h0: e.tensor_tensor(
                    y_sc2[:, :], s_run[:, hf * 2048:(hf + 1) * 2048],
                    exp_ap(dbc_all, col, 32), ALU.mult),
                    waits=[S.up("tensor")])
                S.add("vector", lambda e, hf=hf: e.tensor_tensor(
                    s_run[:, hf * 2048:(hf + 1) * 2048], y_sc2[:, :], sh4[:, :], ALU.add))

        S.add("vector", lambda e: e.tensor_copy(
            c_keep[:, :], bass.AP(xbc, 33 * XBW + 3, [[xbc[:, :].ap[0][0], 128], [1, TC]])))

        S.add("gpsimd", lambda e: e.dma_start(out=carry_in_d.ap()[:, 0:D_INNER], in_=s_run[:, :]),
              waits=[S.up("vector")], amt=16)
        S.add("gpsimd", lambda e: e.dma_start(out=carry_in_d.ap()[:, D_INNER:D_INNER + 64], in_=prod_d[:, :]),
              amt=16)
        S.add("coll", lambda e: e.collective_compute(
            "AllGather", ALU.bypass, replica_groups=[list(range(NCORES))],
            ins=[carry_in_d.ap().opt()], outs=[carry_out_d.ap().opt()]),
            waits=[S.up("gpsimd")])
        S.emit(block)
        pb_g = (S.sems["gpsimd"].h, S.sems["gpsimd"].n)
        pb_c = (S.sems["coll"].h, S.sems["coll"].n)
        pb_v = (S.sems["vector"].h, S.sems["vector"].n)
        sc_body.close()

        P = Sched(b, "pc")
        sc_pc = ExitStack()
        y_sb = b.sbuf("y_sb", [128, NCHUNK * D_INNER], BF16, es=sc_pc)
        nw_b = b.sbuf("nw_b", [128, D_INNER], F32, es=sc_pc)
        ysc_pc = b.sbuf("ysc_pc", [128, 2048], F32, es=sc_pc)
        sumsq = b.sbuf("sumsq", [128, 8], F32, es=sc_pc)
        rstd_a = b.sbuf("rstd_a", [128, 8], F32, es=sc_pc)
        dsc2 = b.sbuf("dsc2", [128, 64], F32, es=sc_pc)
        zs_s = b.sbuf("zs_s", [128, 2 * 512], F32, es=sc_pc)
        sq_scr = b.sbuf("sq_scr", [128, 512], BF16, es=sc_pc)
        sqtmp = b.sbuf("sqtmp", [128, 2], F32, es=sc_pc)
        ostage = b.sbuf("ostage", [128, 2 * 512], F32, es=sc_pc)
        nwst = b.sbuf("nwst", [1, 512], F32, es=sc_pc)

        if _DBG_SPILL:
            P.add("gpsimd", lambda e: e.dma_start(out=dbg_y, in_=y_scr_d.ap()),
                  waits=[pb_g, pb_c], amt=16)
        P.add("gpsimd", lambda e: e.dma_start(out=y_sb[:, :], in_=y_scr_d.ap()),
              waits=[pb_g, pb_c], amt=16)
        for zg in range(8):
            P.add("gpsimd", lambda e, zg=zg: e.dma_start(out=nwst[:, :], in_=nw_in[0:1, zg * 512:(zg + 1) * 512]),
                  waits=[P.up("tensor")], amt=16)
            P.add("tensor", lambda e, zg=zg: e.matmul(sh4[:, (zg % 4) * 512:(zg % 4) * 512 + 512],
                                                      ones_row[:, :], nwst[:, :],
                                                      start=True, stop=True),
                  waits=[pb_g, pb_v, P.up("vector"), P.up("gpsimd")])
            P.add("vector", lambda e, zg=zg: e.tensor_copy(nw_b[:, zg * 512:(zg + 1) * 512],
                                                           sh4[:, (zg % 4) * 512:(zg % 4) * 512 + 512]),
                  waits=[P.up("tensor")])
        P.add("vector", lambda e: e.memset(sumsq[:, :], 0.0))
        P.add("vector", lambda e: e.memset(t_buf[:, :], 0.0))

        sc_mid = ExitStack()
        t_buf = b.sbuf("t_buf", [128, D_INNER], F32, es=sc_mid)
        t_bf = b.sbuf("t_bf", [128, D_INNER], BF16, es=sc_mid)
        sc_slab = ExitStack()
        slab = b.sbuf("slab", [128, D_INNER + 64], F32, es=sc_slab)
        for j in range(7):
            P.add("gpsimd", lambda e, j=j: e.dma_start(
                out=slab[:, :], in_=carry_out_d.ap()[j * 128:(j + 1) * 128, :]),
                waits=[P.up("vector")], amt=16)
            P.add("vector", lambda e: e.tensor_scalar(dsc2[:, :], slab[:, D_INNER:D_INNER + 64], 1.0, None, ALU.subtract),
                  waits=[P.up("gpsimd")])
            P.add("vector", lambda e, j=j: e.tensor_scalar(dsc2[:, :], dsc2[:, :], mpref_s[:, j:j + 1],
                                                           cst[:, 3:4], ALU.mult, ALU.add),
                  waits=[P.up("vector")])
            P.add("vector", lambda e: e.tensor_tensor(t_buf[:, :], t_buf[:, :], exp_ap(dsc2, 0, 64), ALU.mult),
                  waits=[P.up("vector")])
            P.add("vector", lambda e, j=j: e.scalar_tensor_tensor(t_buf[:, :], slab[:, 0:D_INNER],
                                                                  mpref_s[:, j:j + 1], t_buf[:, :],
                                                                  ALU.mult, ALU.add))
        sc_slab.close()

        for ci in range(NCHUNK):
            P.add("vector", lambda e: e.tensor_copy(t_bf[:, :], t_buf[:, :]),
                  waits=[P.up("tensor")])
            for hf in range(2):
                def ycar(e, ci=ci, hf=hf):
                    ct = c_keep[:, ci * MCH:(ci + 1) * MCH]
                    for q in range(4):
                        mm = e.matmul(sh4[:, q * 512:(q + 1) * 512], ct,
                                      t_bf[:, hf * 2048 + q * 512:hf * 2048 + (q + 1) * 512],
                                      start=True, stop=True)
                    return mm
                P.add("tensor", ycar, waits=[P.up("vector")])
                P.add("vector", lambda e, ci=ci, hf=hf: e.tensor_tensor(
                    ysc_pc[:, :], sh4[:, :], exp_ap(sdect_all, ci * 64 + 32 * hf, 32), ALU.mult),
                    waits=[P.up("tensor")])
                ysl = slice(ci * D_INNER + hf * 2048, ci * D_INNER + (hf + 1) * 2048)
                P.add("vector", lambda e, ysl=ysl: e.tensor_tensor(
                    y_sb[:, ysl], y_sb[:, ysl], ysc_pc[:, :], ALU.add))
            P.add("vector", lambda e, ci=ci: e.tensor_tensor(
                t_buf[:, :], t_buf[:, :], exp_ap(dbc_all, ci * 64, 64), ALU.mult))

        sc_mid.close()
        sc_wz = ExitStack()
        wz_buf = b.sbuf("wz_buf", [128, 2 * NFC * 512], BF16, es=sc_wz)
        kz = 0
        for zg in range(8):
            P.add("gpsimd", lambda e, zg=zg: e.dma_start(
                out=wz_buf[:, (zg % 2) * NFC * 512:(zg % 2 + 1) * NFC * 512].rearrange("p (c d) -> p c d", c=NFC),
                in_=wz_in[:, zg * 512:(zg + 1) * 512].rearrange("(c p) d -> p c d", p=128)),
                waits=[P.up("tensor"), P.up("vector")], amt=16)
            wz_tok = P.up("gpsimd")
            for ch in range(NCHUNK):
                def zmm(e, zg=zg, ch=ch, kz=kz):
                    for fc in range(NFC):
                        mm = e.matmul(sh4[:, (kz % 2) * 512:(kz % 2) * 512 + 512],
                                      xn1[:, fc * TC + ch * 128:fc * TC + (ch + 1) * 128],
                                      wz_buf[:, ((zg % 2) * NFC + fc) * 512:((zg % 2) * NFC + fc + 1) * 512],
                                      start=(fc == 0), stop=(fc == NFC - 1))
                    return mm
                P.add("tensor", zmm, waits=[wz_tok, P.up("scalar"), P.up("vector")])
                P.add("scalar", lambda e, kz=kz: e.activation(
                    zs_s[:, (kz % 2) * 512:(kz % 2) * 512 + 512],
                    sh4[:, (kz % 2) * 512:(kz % 2) * 512 + 512], AF.Silu),
                    waits=[P.up("tensor")])
                ysl = slice(ch * D_INNER + zg * 512, ch * D_INNER + (zg + 1) * 512)
                P.add("vector", lambda e, ysl=ysl, kz=kz: e.tensor_tensor(
                    y_sb[:, ysl], y_sb[:, ysl], zs_s[:, (kz % 2) * 512:(kz % 2) * 512 + 512], ALU.mult),
                    waits=[P.up("scalar")])
                P.add("scalar", lambda e, ysl=ysl: e.activation(
                    sq_scr[:, :], y_sb[:, ysl], AF.Square, accum_out=sqtmp[:, 0:1]),
                    waits=[P.up("vector")])
                P.add("vector", lambda e, ch=ch: e.tensor_tensor(
                    sumsq[:, ch:ch + 1], sumsq[:, ch:ch + 1], sqtmp[:, 0:1], ALU.add),
                    waits=[P.up("scalar"), P.up("vector")])
                kz += 1
        sc_wz.close()

        for ch in range(NCHUNK):
            P.add("vector", lambda e, ch=ch: e.tensor_scalar(rstd_a[:, ch:ch + 1], sumsq[:, ch:ch + 1],
                                                             1.0 / D_INNER, None, ALU.mult),
                  waits=[P.up("vector")])
            P.add("scalar", lambda e, ch=ch: e.activation(rstd_a[:, ch:ch + 1], rstd_a[:, ch:ch + 1],
                                                          AF.Sqrt, bias=cst[:, 0:1]),
                  waits=[P.up("vector")])
            P.add("vector", lambda e, ch=ch: e.reciprocal(rstd_a[:, ch:ch + 1], rstd_a[:, ch:ch + 1]),
                  waits=[P.up("scalar")])
            ysl = slice(ch * D_INNER, (ch + 1) * D_INNER)
            P.add("vector", lambda e, ysl=ysl, ch=ch: e.tensor_scalar(
                y_sb[:, ysl], y_sb[:, ysl], rstd_a[:, ch:ch + 1], None, ALU.mult),
                waits=[P.up("vector")])
            P.add("vector", lambda e, ysl=ysl: e.tensor_tensor(
                y_sb[:, ysl], y_sb[:, ysl], nw_b[:, :], ALU.mult))

        sc_fm = ExitStack()
        y_fm = b.sbuf("y_fm", [128, 32 * 512], BF16, es=sc_fm)
        wo_buf = b.sbuf("wo_buf", [128, 2 * 32 * 128], BF16, es=sc_fm)
        ko = 0
        for hf2 in range(2):
            for ch in range(4 * hf2, 4 * hf2 + 4):
                for dc in range(32):
                    ysl = y_sb[:, ch * D_INNER + dc * 128:ch * D_INNER + (dc + 1) * 128]
                    P.add("tensor", lambda e, ysl=ysl: e.transpose(tp_b[:, :], ysl, ident_b[:, :]),
                          waits=[P.up("vector")])
                    P.add("vector", lambda e, dc=dc, ch=ch, hf2=hf2: e.tensor_copy(
                        y_fm[:, dc * 512 + (ch - 4 * hf2) * 128:dc * 512 + (ch - 4 * hf2) * 128 + 128],
                        tp_b[:, :]),
                        waits=[P.up("tensor")])
            for oc in range(NFC):
                P.add("gpsimd", lambda e, oc=oc, ko=ko: e.dma_start(
                    out=wo_buf[:, (ko % 2) * 32 * 128:(ko % 2 + 1) * 32 * 128].rearrange("p (c d) -> p c d", c=32),
                    in_=wout_in[:, oc * 128:(oc + 1) * 128].rearrange("(c p) d -> p c d", p=128)),
                    waits=[P.up("tensor")], amt=16)
                wo_tok = P.up("gpsimd")
                def opmm(e, ko=ko):
                    for dc in range(32):
                        mm = e.matmul(sh4[:, (ko % 2) * 512:(ko % 2) * 512 + 512],
                                      wo_buf[:, ((ko % 2) * 32 + dc) * 128:((ko % 2) * 32 + dc + 1) * 128],
                                      y_fm[:, dc * 512:(dc + 1) * 512],
                                      start=(dc == 0), stop=(dc == 31))
                    return mm
                P.add("tensor", opmm, waits=[wo_tok, P.up("vector")])
                P.add("vector", lambda e, ko=ko: e.tensor_copy(ostage[:, (ko % 2) * 512:(ko % 2) * 512 + 512],
                                                               sh4[:, (ko % 2) * 512:(ko % 2) * 512 + 512]),
                      waits=[P.up("tensor"), P.up("gpsimd")])
                P.add("gpsimd", lambda e, oc=oc, hf2=hf2, ko=ko: e.dma_start(
                    out=mamba_out[oc * 128:(oc + 1) * 128, hf2 * 512:(hf2 + 1) * 512],
                    in_=ostage[:, (ko % 2) * 512:(ko % 2) * 512 + 512]),
                    waits=[P.up("vector")], amt=16)
                ko += 1
        sc_fm.close()
        P.emit(block)

        @block.gpsimd
        def _(g):
            g.wait_ge(P.sems["gpsimd"].h, P.sems["gpsimd"].n)
        sc_pc.close()
    return b



def build_final():
    b = B()
    nc = b.nc
    x1_in = b.din("x1_in", [D, TC], F32)
    mamba_in = b.din("mamba_in", [D, TC], F32)
    xn1_in = b.din("xn1_in", [D, TC], BF16)
    fac_in = b.din("factors", [8, D, D], F32)
    bias_in = b.din("l1_bias", [128, NFC], F32)
    out = b.dout("out", [D, TC], F32)

    es = b.es
    with es:
        block = es.enter_context(nc.Block())
        ld = b.sem("ld")
        gl = b.sem("gl")
        acd = b.sem("acd")
        mmf = b.sem("mmf")
        evf = b.sem("evf")
        stf = b.sem("stf")
        cvt = b.sem("cvt")

        x1a = b.sbuf("x1a", [128, NFC * TC], F32)
        xn1 = b.sbuf("xn1", [128, NFC * TC], BF16)
        bias_s = b.sbuf("bias_s", [128, NFC], F32)
        wf_buf = b.sbuf("wf_buf", [128, 2 * NFC * 128], F32)
        wsum_bf = b.sbuf("wsum_bf", [128, 2 * NFC * 128], BF16)
        ostage = b.sbuf("ostage", [128, 2 * 512], F32)
        f_ps = [b.psum("f_ps0", [128, 512], F32), b.psum("f_ps1", [128, 512], F32)]

        @block.sync
        def _(sync):
            ld.inc(sync.dma_start(out=fm_view(xn1[:, :], NFC), in_=fm_dram(xn1_in)), 16)
            ld.inc(sync.dma_start(out=bias_s[:, :], in_=bias_in), 16)
        xn1_ld = ld.n

        m1 = b.sem("m1")

        @block.gpsimd
        def _(g):
            gl.inc(g.dma_start(out=fm_view(x1a[:, :], NFC), in_=fm_dram(x1_in)), 16)
            g.wait_ge(gl.h, gl.n)
            gl.inc(g.dma_start(out=fm_view(x1a[:, :], NFC), in_=fm_dram(mamba_in),
                               accum_op=ALU.add), 16)
            g.wait_ge(gl.h, gl.n)
            for oc in range(NFC):
                buf = oc % 2
                if oc >= 2:
                    g.wait_ge(cvt.h, oc - 1)
                dst = fm_view(wf_buf[:, buf * NFC * 128:(buf + 1) * NFC * 128], NFC)
                for r in range(8):
                    acd.inc(g.dma_start(out=dst,
                                        in_=fac_in[r, :, oc * 128:(oc + 1) * 128].rearrange("(c p) d -> p c d", p=128),
                                        accum_op=(ALU.add if r > 0 else ALU.bypass)), 16)
                    g.wait_ge(acd.h, acd.n)

        @block.vector
        def _(vector):
            def evac(oc, tt2):
                k = 2 * oc + tt2
                vector.wait_ge(mmf.h, k + 1)
                vector.wait_ge(gl.h, 32)
                if k >= 2:
                    vector.wait_ge(stf.h, 16 * (k - 1))
                evf.inc(vector.scalar_tensor_tensor(
                    ostage[:, (k % 2) * 512:(k % 2 + 1) * 512],
                    f_ps[k % 2][:, :], bias_s[:, oc:oc + 1],
                    x1a[:, oc * TC + tt2 * 512:oc * TC + tt2 * 512 + 512],
                    ALU.add, ALU.add))
            for oc in range(NFC):
                buf = oc % 2
                vector.wait_ge(acd.h, 16 * 8 * (oc + 1))
                if oc >= 2:
                    vector.wait_ge(mmf.h, 2 * (oc - 1))
                cvt.inc(vector.tensor_scalar(wsum_bf[:, buf * NFC * 128:(buf + 1) * NFC * 128],
                                             wf_buf[:, buf * NFC * 128:(buf + 1) * NFC * 128],
                                             1.0 / math.sqrt(8.0), None, ALU.mult))
                if oc >= 1:
                    evac(oc - 1, 0)
                    evac(oc - 1, 1)
            evac(NFC - 1, 0)
            evac(NFC - 1, 1)

        @block.tensor
        def _(tensor):
            tensor.wait_ge(ld.h, xn1_ld)
            for oc in range(NFC):
                buf = oc % 2
                tensor.wait_ge(cvt.h, oc + 1)
                for tt2 in range(2):
                    k = 2 * oc + tt2
                    if k >= 2:
                        tensor.wait_ge(evf.h, k - 1)
                    for fc in range(NFC):
                        mm = tensor.matmul(f_ps[k % 2][:, :],
                                           wsum_bf[:, (buf * NFC + fc) * 128:(buf * NFC + fc + 1) * 128],
                                           xn1[:, fc * TC + tt2 * 512:fc * TC + tt2 * 512 + 512],
                                           start=(fc == 0), stop=(fc == NFC - 1))
                    mmf.inc(mm)

        @block.sync
        def _(sync):
            for oc in range(NFC):
                for tt2 in range(2):
                    k = 2 * oc + tt2
                    sync.wait_ge(evf.h, k + 1)
                    stf.inc(sync.dma_start(out=out[oc * 128:(oc + 1) * 128, tt2 * 512:(tt2 + 1) * 512],
                                           in_=ostage[:, (k % 2) * 512:(k % 2 + 1) * 512]), 16)
            sync.wait_ge(stf.h, 16 * 2 * NFC)
    return b




def build_fused():
    b = B()
    nc = b.nc
    x1_spill = b.dram("x1_spill", [D, TC], F32)
    dbg_spill = b.dout("spill_dbg", [D, TC], F32) if _DBG_SPILL else None
    dbg_y = b.dout("y_dbg", [128, NCHUNK * D_INNER], BF16) if _DBG_SPILL else None
    dbg_y2 = b.dout("y2_dbg", [128, NCHUNK * D_INNER], BF16) if _DBG_SPILL else None
    attn_scr = b.dram("attn_scr", [D, TC], F32)
    out_ap = b.dout("out", [D, TC], F32)

    es = b.es
    with es:
        block = es.enter_context(nc.Block())
        xn = b.sbuf("xn_shared", [128, NFC * TC], BF16)
        sc_l0 = ExitStack()
        sc_at = ExitStack()
        spl = b.sem("spl")
        x_in = b.din("x_fm", [D, TC], F32)
        wq_in = b.din("wq", [HPC, D, 128], BF16)
        wk_in = b.din("wk", [D, 128], BF16)
        wv_in = b.din("wv", [D, 128], BF16)
        wo_in = b.din("wo", [NHEADS_Q * HD, D], BF16)
        cos_in = b.din("cos_t", [64, T], BF16)
        sin_in = b.din("sin_t", [64, T], BF16)
        masks_in = b.din("masks", [4, 128, 512], F32)
        ident_in = b.din("ident", [128, 128], F32)
        identb_in = b.din("ident_bf", [128, 128], BF16)


        ag_in = b.dram("xn_ag_in", [D, TC], BF16)
        ag_out = b.dram("xn_ag_out", [NCORES * D, TC], BF16)
        oag_in = b.dram("o_ag_in", [HPC * HD, T], BF16)
        oag_out = b.dram("o_ag_out", [NHEADS_Q * HD, T], BF16)

        ld = b.sem("ld")
        st = b.sem("st")
        cc = b.sem("cc")
        gx1 = b.sem("gx1")

        ones_col = b.sbuf("ones_col", [128, 1], F32)
        ones_row = b.sbuf("ones_row", [1, 128], F32)
        ones_bf = b.sbuf("ones_bf", [128, 128], BF16)
        cst = b.sbuf("cst", [128, 4], F32)
        ident_f = b.sbuf("ident_f", [128, 128], F32)
        ident_b = b.sbuf("ident_b", [128, 128], BF16)
        cset = b.sem("cset")

        @block.vector
        def _(vector):
            vector.memset(ones_col[:, :], 1.0)
            vector.memset(ones_row[:, :], 1.0)
            vector.memset(cst[:, 0:1], 1e-5)
            cset.inc(vector.memset(ones_bf[:, :], 1.0))

        sc1 = ExitStack()
        xs = b.sbuf("xs", [128, NFC * TC], F32, es=sc1)
        scratch = b.sbuf("lnscr", [128, 2 * TC], F32, es=sc1)
        stats = b.sbuf("lnstats", [1, 3 * TC], F32, es=sc1)

        @block.sync
        def _(sync):
            ld.inc(sync.dma_start(out=fm_view(xs[:, :], NFC), in_=fm_dram(x_in)), 16)
            ld.inc(sync.dma_start(out=ident_f[:, :], in_=ident_in), 16)
            ld.inc(sync.dma_start(out=ident_b[:, :], in_=identb_in), 16)

        ln_done = b.sem("ln_done")
        emit_ln_fm(b, block, xs=xs, xn=xn, tokens=TC, scratch=scratch, stats=stats,
                   ones_col=ones_col, ones_row=ones_row, eps_ap=cst[0:1, 0:1],
                   ready=[(ld.h, 48), (cset.h, 1)], done=ln_done)

        @block.gpsimd
        def _(g):
            g.wait_ge(ln_done.h, 1)
            st.inc(g.dma_start(out=fm_dram(ag_in.ap()), in_=fm_sbuf_to_dram(xn[:, :], NFC)), 16)
            g.wait_ge(st.h, 16)
            cc.inc(g.collective_compute(
                "AllGather", ALU.bypass, replica_groups=[list(range(NCORES))],
                ins=[ag_in.ap().opt()], outs=[ag_out.ap().opt()]))

        sc1.close()

        q_s = b.sbuf("q_s", [128, HPC * T], BF16, es=sc_at)
        k_s = b.sbuf("k_s", [128, T], BF16, es=sc_at)
        v_tm = b.sbuf("v_tm", [128, T], BF16, es=sc_at)
        sc2b = ExitStack()
        wq_s = b.sbuf("wq_s", [128, HPC * NFC * 128], BF16, es=sc2b)
        wk_s = b.sbuf("wk_s", [128, NFC * 128], BF16, es=sc2b)
        wv_s = b.sbuf("wv_s", [128, NFC * 128], BF16, es=sc2b)
        cos_s = b.sbuf("cos_s", [64, T], BF16, es=sc2b)
        sin_s = b.sbuf("sin_s", [64, T], BF16, es=sc2b)

        @block.sync
        def _(sync):
            sync.wait_ge(st.h, 16)
            ld.inc(sync.dma_start(out=wq_s[:, :].rearrange("p (j c d) -> p j c d", j=HPC, c=NFC), in_=wq_in.rearrange("j (c p) d -> p j c d", p=128)), 16)
            ld.inc(sync.dma_start(out=fm_view(wk_s[:, :], NFC), in_=fm_dram(wk_in)), 16)
            ld.inc(sync.dma_start(out=fm_view(wv_s[:, :], NFC), in_=fm_dram(wv_in)), 16)
            ld.inc(sync.dma_start(out=cos_s[:, :], in_=cos_in), 16)
            ld.inc(sync.dma_start(out=sin_s[:, :], in_=sin_in), 16)
        w_loaded = ld.n

        xt = b.sbuf("xt", [128, 2 * NFC * 512], BF16, es=sc2b)
        rsc = b.sbuf("rsc", [64, 2 * 512], F32, es=sc2b)
        vfm = b.sbuf("vfm", [128, 2 * 512], BF16, es=sc2b)

        q_ps0 = b.psum("q_ps0", [128, 512], F32, es=sc2b)
        q_ps1 = b.psum("q_ps1", [128, 512], F32, es=sc2b)
        k_ps = b.psum("k_ps", [128, 512], F32, es=sc2b)
        v_ps = b.psum("v_ps", [128, 512], F32, es=sc2b)
        t_ps = b.psum("t_ps", [128, 128], BF16, es=sc2b)

        ldq = b.sem("ldq")
        mmk = b.sem("mmk")
        rp = b.sem("rp")
        tvt = b.sem("tvt")
        tve = b.sem("tve")

        @block.sync
        def _(sync):
            sync.wait_ge(cc.h, 1)
            for tt in range(NQT):
                buf = tt % 2
                if tt >= 2:
                    sync.wait_ge(rp.h, 4 * (tt - 1))
                src = ag_out.ap()[D * (tt // 2):D * (tt // 2 + 1),
                                  (tt % 2) * 512:(tt % 2) * 512 + 512]
                ldq.inc(sync.dma_start(out=fm_view(xt[:, buf * NFC * 512:(buf + 1) * NFC * 512], NFC),
                                       in_=fm_dram(src)), 16)

        @block.tensor
        def _(tensor):
            tensor.wait_ge(ld.h, w_loaded)
            for tt in range(NQT):
                buf = tt % 2
                xsl = lambda fc: xt[:, (buf * NFC + fc) * 512:(buf * NFC + fc + 1) * 512]
                tensor.wait_ge(ldq.h, 16 * (tt + 1))
                if tt >= 1:
                    tensor.wait_ge(rp.h, 4 * tt)
                for jps, wsl in ((q_ps0, 0), (q_ps1, 1)):
                    for fc in range(NFC):
                        mm = tensor.matmul(jps[:, :],
                                           wq_s[:, (wsl * NFC + fc) * 128:(wsl * NFC + fc + 1) * 128],
                                           xsl(fc), start=(fc == 0), stop=(fc == NFC - 1))
                    mmk.inc(mm)
                for fc in range(NFC):
                    mm = tensor.matmul(k_ps[:, :], wk_s[:, fc * 128:(fc + 1) * 128],
                                       xsl(fc), start=(fc == 0), stop=(fc == NFC - 1))
                mmk.inc(mm)
                for fc in range(NFC):
                    mm = tensor.matmul(v_ps[:, :], wv_s[:, fc * 128:(fc + 1) * 128],
                                       xsl(fc), start=(fc == 0), stop=(fc == NFC - 1))
                mmk.inc(mm)
                tts = [tt - 1] if tt >= 1 else []
                if tt == NQT - 1:
                    tts.append(tt)
                for tp in tts:
                    tensor.wait_ge(rp.h, 4 * tp + 4)
                    for i in range(4):
                        idx = tp * 4 + i
                        if idx >= 1:
                            tensor.wait_ge(tve.h, idx)
                        tvt.inc(tensor.transpose(t_ps[:, :],
                                                 vfm[:, (tp % 2) * 512 + i * 128:(tp % 2) * 512 + (i + 1) * 128],
                                                 ident_b[:, :]))

        def rope_emit(vector, ps, dst, tt):
            c_sl = cos_s[:, tt * 512:(tt + 1) * 512]
            s_sl = sin_s[:, tt * 512:(tt + 1) * 512]
            e, o = ps[0:64, :], ps[64:128, :]
            t1 = rsc[:, 0 * 512:1 * 512]
            t2 = rsc[:, 1 * 512:2 * 512]
            vector.tensor_tensor(t1, e, c_sl, ALU.mult)
            vector.tensor_tensor(t2, o, s_sl, ALU.mult)
            vector.tensor_tensor(dst[0:64, :], t1, t2, ALU.subtract)
            vector.tensor_tensor(t1, e, s_sl, ALU.mult)
            vector.tensor_tensor(t2, o, c_sl, ALU.mult)
            return vector.tensor_tensor(dst[64:128, :], t1, t2, ALU.add)

        @block.vector
        def _(vector):
            for tt in range(NQT):
                buf = tt % 2
                base = 4 * tt
                vector.wait_ge(mmk.h, base + 1)
                rp.inc(rope_emit(vector, q_ps0, q_s[:, 0 * T + tt * 512:0 * T + tt * 512 + 512], tt))
                vector.wait_ge(mmk.h, base + 2)
                rp.inc(rope_emit(vector, q_ps1, q_s[:, 1 * T + tt * 512:1 * T + tt * 512 + 512], tt))
                vector.wait_ge(mmk.h, base + 3)
                rp.inc(rope_emit(vector, k_ps, k_s[:, tt * 512:(tt + 1) * 512], tt))
                vector.wait_ge(mmk.h, base + 4)
                if tt >= 2:
                    vector.wait_ge(tvt.h, 4 * (tt - 1))
                rp.inc(vector.tensor_copy(vfm[:, buf * 512:(buf + 1) * 512], v_ps[:, :]))
                tps = [tt - 1] if tt >= 1 else []
                if tt == NQT - 1:
                    tps.append(tt)
                for tp in tps:
                    for i in range(4):
                        idx = tp * 4 + i
                        vector.wait_ge(tvt.h, idx + 1)
                        tve.inc(vector.tensor_copy(v_tm[:, idx * 128:(idx + 1) * 128], t_ps[:, :]))

        sc2b.close()

        p_s = b.sbuf("p_s", [128, 2 * 512], BF16, es=sc_at)
        mask_s = b.sbuf("mask_s", [128, 4 * 512], F32, es=sc_at)
        o_stage = b.sbuf("o_stage", [128, 2 * 512], BF16, es=sc_at)
        nrm_s = b.sbuf("nrm_s", [128, 512], F32, es=sc_at)

        @block.sync
        def _(sync):
            sync.wait_ge(tve.h, 4 * NQT)
            ld.inc(sync.dma_start(out=mask_s[:, :].rearrange("p (d t) -> p d t", d=4), in_=masks_in.rearrange("d p t -> p d t")), 16)
        mask_ld = ld.n

        s_ps = [b.psum("s_ps0", [128, 512], F32, es=sc_at), b.psum("s_ps1", [128, 512], F32, es=sc_at)]
        o_ps = b.psum("o_ps", [128, 512], F32, es=sc_at)
        n_ps = b.psum("n_ps", [128, 512], F32, es=sc_at)

        sS = b.sem("sS")
        sM = b.sem("sM")
        sP = b.sem("sP")
        sC = b.sem("sC")
        sE = b.sem("sE")
        sO = b.sem("sO")

        jobs = []
        for j in range(HPC):
            for qt in range(NQT):
                jobs.append((j, qt, 4 * qt + 4, 4 * qt))
        iters = []
        for jb, (j, qt, nkc, diag0) in enumerate(jobs):
            for kc in range(nkc):
                iters.append(dict(jb=jb, j=j, qt=qt, kc=kc, first=(kc == 0),
                                  last=(kc == nkc - 1), diag=(kc >= diag0),
                                  delta=kc - diag0))
        nd_cum = np.cumsum([1 if it["diag"] else 0 for it in iters]).tolist()
        qkv_rp = 4 * NQT
        qkv_tve = 4 * NQT

        @block.tensor
        def _(tensor):
            tensor.wait_ge(rp.h, qkv_rp)
            tensor.wait_ge(tve.h, qkv_tve)
            tensor.wait_ge(ld.h, mask_ld)
            for i, it in enumerate(iters):
                q_sl = q_s[:, it["j"] * T + it["qt"] * 512: it["j"] * T + it["qt"] * 512 + 512]
                if i >= 2:
                    tensor.wait_ge(sP.h, i - 1)
                sS.inc(tensor.matmul(s_ps[i % 2][:, :],
                                     k_s[:, it["kc"] * 128:(it["kc"] + 1) * 128],
                                     q_sl, start=True, stop=True))
                if i >= 1:
                    pit = iters[i - 1]
                    if pit["first"]:
                        tensor.wait_ge(sE.h, pit["jb"])
                    tensor.wait_ge(sP.h, i)
                    tensor.matmul(o_ps[:, :], v_tm[:, pit["kc"] * 128:(pit["kc"] + 1) * 128],
                                  p_s[:, ((i - 1) % 2) * 512:((i - 1) % 2 + 1) * 512],
                                  start=pit["first"], stop=pit["last"])
                    sC.inc(tensor.matmul(n_ps[:, :], ones_bf[:, :],
                                         p_s[:, ((i - 1) % 2) * 512:((i - 1) % 2 + 1) * 512],
                                         start=pit["first"], stop=pit["last"]))
            i = len(iters)
            pit = iters[i - 1]
            if pit["first"]:
                tensor.wait_ge(sE.h, pit["jb"])
            tensor.wait_ge(sP.h, i)
            tensor.matmul(o_ps[:, :], v_tm[:, pit["kc"] * 128:(pit["kc"] + 1) * 128],
                          p_s[:, ((i - 1) % 2) * 512:((i - 1) % 2 + 1) * 512],
                          start=pit["first"], stop=pit["last"])
            sC.inc(tensor.matmul(n_ps[:, :], ones_bf[:, :],
                                 p_s[:, ((i - 1) % 2) * 512:((i - 1) % 2 + 1) * 512],
                                 start=pit["first"], stop=pit["last"]))

        @block.vector
        def _(vector):
            i = 0
            for jb, (j, qt, nkc, diag0) in enumerate(jobs):
                for kc in range(nkc):
                    it = iters[i]
                    if it["diag"]:
                        vector.wait_ge(sS.h, i + 1)
                        sM.inc(vector.tensor_tensor(
                            s_ps[i % 2][:, :], s_ps[i % 2][:, :],
                            mask_s[:, it["delta"] * 512:(it["delta"] + 1) * 512], ALU.add))
                    i += 1
                vector.wait_ge(sC.h, i)
                if jb >= 2:
                    vector.wait_ge(sO.h, 16 * (jb - 1))
                vector.reciprocal(nrm_s[:, :], n_ps[:, :])
                sE.inc(vector.tensor_tensor(o_stage[:, (jb % 2) * 512:(jb % 2 + 1) * 512],
                                            o_ps[:, :], nrm_s[:, :], ALU.mult))

        @block.scalar
        def _(scalar):
            for i, it in enumerate(iters):
                if it["diag"]:
                    scalar.wait_ge(sM.h, nd_cum[i])
                else:
                    scalar.wait_ge(sS.h, i + 1)
                if i >= 2:
                    scalar.wait_ge(sC.h, i - 1)
                sP.inc(scalar.activation(p_s[:, (i % 2) * 512:(i % 2 + 1) * 512],
                                         s_ps[i % 2][:, :], AF.Exp))

        @block.gpsimd
        def _(g):
            for jb, (j, qt, nkc, diag0) in enumerate(jobs):
                g.wait_ge(sE.h, jb + 1)
                sO.inc(g.dma_start(out=oag_in.ap()[j * 128:(j + 1) * 128, qt * 512:(qt + 1) * 512],
                                   in_=o_stage[:, (jb % 2) * 512:(jb % 2 + 1) * 512]), 16)
            g.wait_ge(sO.h, 16 * len(jobs))
            cc.inc(g.collective_compute(
                "AllGather", ALU.bypass, replica_groups=[list(range(NCORES))],
                ins=[oag_in.ap().opt()], outs=[oag_out.ap().opt()]))

        tok_in = b.din("tok_off", [1, 1], mybir.dt.int32)
        oa_s = b.sbuf("oa_s", [128, 2 * NFC * 512], BF16, es=sc_at)
        wo_s = b.sbuf("wo_s", [128, 2 * NFC * 128], BF16, es=sc_at)
        r_stage = b.sbuf("r_stage", [128, 2 * 512], F32, es=sc_at)
        stok = b.sbuf("stok", [1, 1], mybir.dt.int32, es=sc_at)

        r_ps = [b.psum("r_ps0", [128, 512], F32, es=sc_at), b.psum("r_ps1", [128, 512], F32, es=sc_at)]
        lo = b.sem("lo")
        lw = b.sem("lw")
        mo = b.sem("mo")
        re = b.sem("re")
        ro = b.sem("ro")

        @block.sync
        def _(sync):
            ld.inc(sync.dma_start(out=stok[:, :], in_=tok_in), 16)
            for oc in range(NFC):
                if oc >= 2:
                    sync.wait_ge(mo.h, 2 * (oc - 1))
                lw.inc(sync.dma_start(
                    out=fm_view(wo_s[:, (oc % 2) * NFC * 128:(oc % 2 + 1) * NFC * 128], NFC),
                    in_=fm_dram(wo_in[:, oc * 128:(oc + 1) * 128])), 16)
        tok_ld = ld.n

        @block.gpsimd
        def _(g):
            g.wait_ge(cc.h, 2)
            g.wait_ge(ld.h, tok_ld)
            with g.register("r_tok") as r:
                g.reg_load(r, stok[0:1, 0:1])
                for tt2 in range(2):
                    ofs = g.snap(r, min_val=0, max_val=T - 512)
                    src = oag_out.ap().rearrange("(h p) t -> p h t", p=128)[:, :, bass.ds(ofs, 512)]
                    dst = oa_s[:, tt2 * NFC * 512:(tt2 + 1) * NFC * 512].rearrange("p (h t) -> p h t", h=NFC)
                    lo.inc(g.dma_start(out=dst, in_=src), 16)
                    if tt2 == 0:
                        g.reg_add(r, r, 512)

        @block.tensor
        def _(tensor):
            tensor.wait_ge(lo.h, 32)
            for oc in range(NFC):
                tensor.wait_ge(lw.h, 16 * (oc + 1))
                for tt2 in range(2):
                    if oc >= 1:
                        tensor.wait_ge(re.h, 2 * (oc - 1) + tt2 + 1)
                    for hd in range(NFC):
                        mm = tensor.matmul(
                            r_ps[tt2][:, :],
                            wo_s[:, ((oc % 2) * NFC + hd) * 128:((oc % 2) * NFC + hd + 1) * 128],
                            oa_s[:, (tt2 * NFC + hd) * 512:(tt2 * NFC + hd + 1) * 512],
                            start=(hd == 0), stop=(hd == NFC - 1))
                    mo.inc(mm)

        @block.vector
        def _(vector):
            for oc in range(NFC):
                for tt2 in range(2):
                    vector.wait_ge(mo.h, 2 * oc + tt2 + 1)
                    if oc >= 1:
                        vector.wait_ge(ro.h, 16 * (2 * (oc - 1) + tt2 + 1))
                    re.inc(vector.tensor_copy(r_stage[:, tt2 * 512:(tt2 + 1) * 512],
                                              r_ps[tt2][:, :]))

        @block.gpsimd
        def _(g):
            for oc in range(NFC):
                for tt2 in range(2):
                    g.wait_ge(re.h, 2 * oc + tt2 + 1)
                    ro.inc(g.dma_start(
                        out=attn_scr.ap()[oc * 128:(oc + 1) * 128, tt2 * 512:(tt2 + 1) * 512],
                        in_=r_stage[:, tt2 * 512:(tt2 + 1) * 512]), 16)

        attn_done_h, attn_done_v = ro.h, ro.n
        sc_at.close()
        x1a = b.sbuf("x1a", [128, NFC * TC], F32, es=sc_l0)
        rT_in = b.din("rT", [D, 8], BF16)
        sgT_in = b.din("sgT", [D, MOE_I], BF16)
        suT_in = b.din("suT", [D, MOE_I], BF16)
        sdT_in = b.din("sdT", [MOE_I, D], BF16)
        wgT_in = b.din("wgT", [8, D, MOE_I], BF16)
        wuT_in = b.din("wuT", [8, D, MOE_I], BF16)
        wdT_in = b.din("wdT", [8, MOE_I, D], BF16)
        sel8_in = b.din("sel8", [8, 8 * 128], BF16)
    

        ld = b.sem("ld")
        gl = b.sem("gl")
        st = b.sem("st")

        cset = b.sem("cset")

        @block.vector
        def _(vector):
            cset.inc(vector.memset(cst[:, 1:2], 0.0))

        m1 = b.sem("m1")

        @block.gpsimd
        def _(g):
            g.wait_ge(attn_done_h, attn_done_v)
            gl.inc(g.dma_start(out=fm_view(x1a[:, :], NFC), in_=fm_dram(x_in)), 16)
            g.wait_ge(gl.h, gl.n)
            gl.inc(g.dma_start(out=fm_view(x1a[:, :], NFC), in_=fm_dram(attn_scr.ap()),
                               accum_op=ALU.add), 16)
        moe_gl = gl.n

        @block.vector
        def _(vector):
            vector.wait_ge(gl.h, moe_gl)
            for fc in range(NFC):
                last = vector.tensor_tensor(x1a[:, fc * TC:(fc + 1) * TC],
                                            x1a[:, fc * TC:(fc + 1) * TC],
                                            xn[:, fc * TC:(fc + 1) * TC], ALU.add)
            m1.inc(last)

        rT_s = b.sbuf("rT_s", [128, NFC * 8], BF16, es=sc_l0)
        sel_s = b.sbuf("sel_s", [8, 8 * 128], BF16, es=sc_l0)
        comb_tm = b.sbuf("comb_tm", [128, 8 * 8], F32, es=sc_l0)
        comb_fm = b.sbuf("comb_fm", [8, TC], BF16, es=sc_l0)
        rsc = b.sbuf("moe_rsc", [128, 8 * 8], F32, es=sc_l0)
        rcol = b.sbuf("rcol", [128, 8], F32, es=sc_l0)

        @block.sync
        def _(sync):
            sync.wait_ge(attn_done_h, attn_done_v)
            ld.inc(sync.dma_start(out=rT_s[:, :].rearrange("p (c e) -> p c e", c=NFC),
                                  in_=rT_in.rearrange("(c p) e -> p c e", p=128)), 16)
            ld.inc(sync.dma_start(out=sel_s[:, :], in_=sel8_in), 16)
        r_ld = ld.n

        sc_r = ExitStack()
        r_ps = b.psum("r_ps", [128, 8], F32, es=sc_r)
        t_ps = b.psum("t_ps", [8, 128], F32, es=sc_r)

        rm = b.sem("rm")
        rsv = b.sem("rsv")
        rse = b.sem("rse")
        rt = b.sem("rt")
        rf = b.sem("rf")

        @block.tensor
        def _(tensor):
            tensor.wait_ge(ld.h, r_ld)
            for tc in range(8):
                if tc >= 1:
                    tensor.wait_ge(rsv.h, 2 * tc - 1)
                for fc in range(NFC):
                    mm = tensor.matmul(r_ps[:, :], xn[:, fc * TC + tc * 128:fc * TC + (tc + 1) * 128],
                                       rT_s[:, fc * 8:(fc + 1) * 8],
                                       start=(fc == 0), stop=(fc == NFC - 1))
                rm.inc(mm)

        vch = b.sem("vch")

        @block.vector
        def _(vector):
            def ch(inst):
                vch.inc(inst)
                vector.wait_ge(vch.h, vch.n)
            for tc in range(8):
                w = rsc[:, 0:8]
                ebuf = rsc[:, 8:16]
                tbuf = rsc[:, 16:24]
                vector.wait_ge(rm.h, tc + 1)
                ch(vector.reduce_max(rcol[:, 0:1], r_ps[:, :], axis=mybir.AxisListType.X))
                rsv.inc(vector.tensor_scalar(w, r_ps[:, :], rcol[:, 0:1], None, ALU.subtract))
                vector.wait_ge(rse.h, tc + 1)
                ch(vector.reciprocal(rcol[:, 1:2], rcol[:, 2:3]))
                ch(vector.tensor_scalar(w, ebuf, rcol[:, 1:2], None, ALU.mult))
                ch(vector.reduce_max(rcol[:, 3:4], w, axis=mybir.AxisListType.X))
                ch(vector.tensor_scalar(tbuf, w, rcol[:, 3:4], None, ALU.is_ge))
                ch(vector.scalar_tensor_tensor(tbuf, tbuf, -1e9, w, ALU.mult, ALU.add))
                ch(vector.reduce_max(rcol[:, 5:6], tbuf, axis=mybir.AxisListType.X))
                ch(vector.tensor_scalar(tbuf, w, rcol[:, 5:6], None, ALU.is_ge))
                rsv.inc(vector.tensor_tensor(comb_tm[:, tc * 8:(tc + 1) * 8], w, tbuf, ALU.mult))

        @block.scalar
        def _(scalar):
            for tc in range(8):
                scalar.wait_ge(rsv.h, 2 * tc + 1)
                if tc >= 1:
                    scalar.wait_ge(rsv.h, 2 * tc)
                rse.inc(scalar.activation(rsc[:, 8:16], rsc[:, 0:8], AF.Exp,
                                          accum_out=rcol[:, 2:3]))

        @block.tensor
        def _(tensor):
            for tc in range(8):
                tensor.wait_ge(rsv.h, 2 * (tc + 1))
                if tc >= 1:
                    tensor.wait_ge(rf.h, tc)
                rt.inc(tensor.transpose(t_ps[:, :], comb_tm[:, tc * 8:(tc + 1) * 8], ident_f[:, :]))

        @block.vector
        def _(vector):
            for tc in range(8):
                vector.wait_ge(rt.h, tc + 1)
                rf.inc(vector.tensor_copy(comb_fm[:, tc * 128:(tc + 1) * 128], t_ps[:, :]))

        sc_exp = ExitStack()
        he_buf = b.sbuf("he_buf", [128, NIC * TC], BF16, es=sc_exp)
        wg_buf = b.sbuf("wg_buf", [128, 2 * NFC * 128], BF16, es=sc_exp)
        wu_buf = b.sbuf("wu_buf", [128, 2 * NFC * 128], BF16, es=sc_exp)
        wd_buf = b.sbuf("wd_buf", [128, 2 * NIC * 128], BF16, es=sc_exp)
        sig_buf = b.sbuf("sig_buf", [128, 2 * 512], F32, es=sc_exp)
        tsc_buf = b.sbuf("tsc_buf", [128, 512], F32, es=sc_exp)
        comb_bc = b.sbuf("comb_bc", [128, TC], F32, es=sc_exp)

        sc_r.close()
        bc_ps = b.psum("bc_ps", [128, 1024], F32, es=sc_exp)
        g_ps = [b.psum("g_ps0", [128, 512], F32, es=sc_exp), b.psum("g_ps1", [128, 512], F32, es=sc_exp)]
        u_ps = [b.psum("u_ps0", [128, 512], F32, es=sc_exp), b.psum("u_ps1", [128, 512], F32, es=sc_exp)]
        d_ps = [b.psum("d_ps0", [128, 512], F32, es=sc_exp), b.psum("d_ps1", [128, 512], F32, es=sc_exp)]

        lwg = b.sem("lwg")
        lwd = b.sem("lwd")
        mmg = b.sem("mmg")
        sg = b.sem("sg")
        he = b.sem("he")
        cbc = b.sem("cbc")
        mmd = b.sem("mmd")
        dev_ = b.sem("dev")

        NEXP = 9

        def gu_src(e, kind, ic):
            if e == 8:
                base = sgT_in if kind == "g" else suT_in
                sl = base[:, ic * 128:(ic + 1) * 128]
            else:
                base = wgT_in if kind == "g" else wuT_in
                sl = base[e, :, ic * 128:(ic + 1) * 128]
            return sl.rearrange("(c p) d -> p c d", p=128)

        def wd_src(e, oc):
            if e == 8:
                sl = sdT_in[:, oc * 128:(oc + 1) * 128]
            else:
                sl = wdT_in[e, :, oc * 128:(oc + 1) * 128]
            return sl.rearrange("(c p) d -> p c d", p=128)

        @block.sync
        def _(sync):
            k = 0
            for e in range(NEXP):
                for ic in range(NIC):
                    if k >= 2:
                        sync.wait_ge(mmg.h, 4 * (k - 1))
                    buf = k % 2
                    lwg.inc(sync.dma_start(
                        out=wg_buf[:, buf * NFC * 128:(buf + 1) * NFC * 128].rearrange("p (c d) -> p c d", c=NFC),
                        in_=gu_src(e, "g", ic)), 16)
                    lwg.inc(sync.dma_start(
                        out=wu_buf[:, buf * NFC * 128:(buf + 1) * NFC * 128].rearrange("p (c d) -> p c d", c=NFC),
                        in_=gu_src(e, "u", ic)), 16)
                    k += 1

        @block.gpsimd
        def _(g):
            kd = 0
            for e in range(NEXP):
                for oc in range(NFC):
                    if kd >= 2:
                        g.wait_ge(mmd.h, 2 * (kd - 1))
                    buf = kd % 2
                    lwd.inc(g.dma_start(
                        out=wd_buf[:, buf * NIC * 128:(buf + 1) * NIC * 128].rearrange("p (c d) -> p c d", c=NIC),
                        in_=wd_src(e, oc)), 16)
                    kd += 1

        @block.tensor
        def _(tensor):
            k = 0
            kt = 0
            kd = 0
            kdt = 0
            for e in range(NEXP):
                if e < 8:
                    tensor.wait_ge(rf.h, 8)
                    if e >= 1:
                        tensor.wait_ge(he.h, 2 * NIC * e)
                    for th in range(2):
                        cbc.inc(tensor.matmul(bc_ps[:, th * 512:(th + 1) * 512],
                                              sel_s[:, e * 128:(e + 1) * 128],
                                              comb_fm[:, th * 512:(th + 1) * 512],
                                              start=True, stop=True))
                for ic in range(NIC):
                    tensor.wait_ge(lwg.h, 32 * (k + 1))
                    buf = k % 2
                    for th in range(2):
                        pb = kt % 2
                        if kt >= 2:
                            tensor.wait_ge(he.h, kt - 1)
                        for fc in range(NFC):
                            mm = tensor.matmul(g_ps[pb][:, :],
                                               wg_buf[:, (buf * NFC + fc) * 128:(buf * NFC + fc + 1) * 128],
                                               xn[:, fc * TC + th * 512:fc * TC + th * 512 + 512],
                                               start=(fc == 0), stop=(fc == NFC - 1))
                        mmg.inc(mm)
                        for fc in range(NFC):
                            mm = tensor.matmul(u_ps[pb][:, :],
                                               wu_buf[:, (buf * NFC + fc) * 128:(buf * NFC + fc + 1) * 128],
                                               xn[:, fc * TC + th * 512:fc * TC + th * 512 + 512],
                                               start=(fc == 0), stop=(fc == NFC - 1))
                        mmg.inc(mm)
                        kt += 1
                    k += 1
                tensor.wait_ge(he.h, 2 * NIC * (e + 1))
                for oc in range(NFC):
                    tensor.wait_ge(lwd.h, 16 * (kd + 1))
                    bufd = kd % 2
                    for th in range(2):
                        pb = kdt % 2
                        if kdt >= 2:
                            tensor.wait_ge(dev_.h, kdt - 1)
                        for ic in range(NIC):
                            mm = tensor.matmul(d_ps[pb][:, :],
                                               wd_buf[:, (bufd * NIC + ic) * 128:(bufd * NIC + ic + 1) * 128],
                                               he_buf[:, ic * TC + th * 512:ic * TC + th * 512 + 512],
                                               start=(ic == 0), stop=(ic == NIC - 1))
                        mmd.inc(mm)
                        kdt += 1
                    kd += 1

        @block.scalar
        def _(scalar):
            kt = 0
            for e in range(NEXP):
                for ic in range(NIC):
                    for th in range(2):
                        scalar.wait_ge(mmg.h, 2 * kt + 1)
                        if kt >= 2:
                            scalar.wait_ge(he.h, kt - 1)
                        sg.inc(scalar.activation(sig_buf[:, (kt % 2) * 512:(kt % 2 + 1) * 512],
                                                 g_ps[kt % 2][:, :], AF.Sigmoid))
                        kt += 1

        @block.vector
        def _(vector):
            kt = 0
            kdt = 0
            for e in range(NEXP):
                if e < 8:
                    for th in range(2):
                        vector.wait_ge(cbc.h, 2 * e + th + 1)
                        vector.tensor_copy(comb_bc[:, th * 512:(th + 1) * 512],
                                           bc_ps[:, th * 512:(th + 1) * 512])
                for ic in range(NIC):
                    for th in range(2):
                        pb = kt % 2
                        vector.wait_ge(sg.h, kt + 1)
                        vector.wait_ge(mmg.h, 2 * kt + 2)
                        vector.tensor_tensor(tsc_buf[:, :], sig_buf[:, pb * 512:(pb + 1) * 512],
                                             u_ps[pb][:, :], ALU.mult)
                        dst = he_buf[:, ic * TC + th * 512:ic * TC + th * 512 + 512]
                        if e < 8:
                            he.inc(vector.tensor_tensor(dst, tsc_buf[:, :],
                                                        comb_bc[:, th * 512:(th + 1) * 512], ALU.mult))
                        else:
                            he.inc(vector.tensor_copy(dst, tsc_buf[:, :]))
                        kt += 1
                for oc in range(NFC):
                    for th in range(2):
                        pb = kdt % 2
                        vector.wait_ge(mmd.h, kdt + 1)
                        vector.wait_ge(m1.h, 1)
                        dev_.inc(vector.tensor_tensor(
                            x1a[:, oc * TC + th * 512:oc * TC + th * 512 + 512],
                            x1a[:, oc * TC + th * 512:oc * TC + th * 512 + 512],
                            d_ps[pb][:, :], ALU.add))
                        kdt += 1

        sc_exp.close()
        sc_ln = ExitStack()
        scratch1 = b.sbuf("lnscr1", [128, 2 * TC], F32, es=sc_ln)
        stats1 = b.sbuf("lnstats1", [1, 3 * TC], F32, es=sc_ln)
        ln1_done = b.sem("ln1_done")
        emit_ln_fm(b, block, xs=x1a, xn=xn, tokens=TC, scratch=scratch1, stats=stats1,
                   ones_col=ones_col, ones_row=ones_row, eps_ap=cst[0:1, 0:1],
                   ready=[(dev_.h, 2 * NFC * NEXP), (cset.h, 1)], done=ln1_done)
        sc_ln.close()


        fac_in = b.din("factors", [8, D, D], F32)
        bias_in = b.din("l1_bias", [128, NFC], F32)

        ld = b.sem("ld")
        gl = b.sem("gl")
        acd = b.sem("acd")
        mmf = b.sem("mmf")
        evf = b.sem("evf")
        stf = b.sem("stf")
        cvt = b.sem("cvt")

        bias_s = b.sbuf("bias_s", [128, NFC], F32, es=sc_l0)
        wf_buf = b.sbuf("wf_buf", [128, 2 * NFC * 128], F32, es=sc_l0)
        wsum_bf = b.sbuf("wsum_bf", [128, 2 * NFC * 128], BF16, es=sc_l0)
        f_ps = [b.psum("f_ps0", [128, 512], F32, es=sc_l0), b.psum("f_ps1", [128, 512], F32, es=sc_l0)]

        @block.sync
        def _(sync):
            sync.wait_ge(ln1_done.h, 1)
            ld.inc(sync.dma_start(out=bias_s[:, :], in_=bias_in), 16)
        xn1_ld = ld.n

        @block.gpsimd
        def _(g):
            g.wait_ge(ln1_done.h, 1)
            for oc in range(NFC):
                buf = oc % 2
                if oc >= 2:
                    g.wait_ge(cvt.h, oc - 1)
                dst = fm_view(wf_buf[:, buf * NFC * 128:(buf + 1) * NFC * 128], NFC)
                for r in range(8):
                    acd.inc(g.dma_start(out=dst,
                                        in_=fac_in[r, :, oc * 128:(oc + 1) * 128].rearrange("(c p) d -> p c d", p=128),
                                        accum_op=(ALU.add if r > 0 else ALU.bypass)), 16)
                    g.wait_ge(acd.h, acd.n)

        @block.vector
        def _(vector):
            def evac(oc, tt2):
                k = 2 * oc + tt2
                vector.wait_ge(mmf.h, k + 1)
                sl = slice(oc * TC + tt2 * 512, oc * TC + tt2 * 512 + 512)
                evf.inc(vector.scalar_tensor_tensor(
                    x1a[:, sl], f_ps[k % 2][:, :], bias_s[:, oc:oc + 1],
                    x1a[:, sl], ALU.add, ALU.add))
            for oc in range(NFC):
                buf = oc % 2
                vector.wait_ge(acd.h, 16 * 8 * (oc + 1))
                if oc >= 2:
                    vector.wait_ge(mmf.h, 2 * (oc - 1))
                cvt.inc(vector.tensor_scalar(wsum_bf[:, buf * NFC * 128:(buf + 1) * NFC * 128],
                                             wf_buf[:, buf * NFC * 128:(buf + 1) * NFC * 128],
                                             1.0 / math.sqrt(8.0), None, ALU.mult))
                if oc >= 1:
                    evac(oc - 1, 0)
                    evac(oc - 1, 1)
            evac(NFC - 1, 0)
            evac(NFC - 1, 1)

        @block.tensor
        def _(tensor):
            tensor.wait_ge(ld.h, xn1_ld)
            for oc in range(NFC):
                buf = oc % 2
                tensor.wait_ge(cvt.h, oc + 1)
                for tt2 in range(2):
                    k = 2 * oc + tt2
                    if k >= 2:
                        tensor.wait_ge(evf.h, k - 1)
                    for fc in range(NFC):
                        mm = tensor.matmul(f_ps[k % 2][:, :],
                                           wsum_bf[:, (buf * NFC + fc) * 128:(buf * NFC + fc + 1) * 128],
                                           xn[:, fc * TC + tt2 * 512:fc * TC + tt2 * 512 + 512],
                                           start=(fc == 0), stop=(fc == NFC - 1))
                    mmf.inc(mm)

        @block.gpsimd
        def _(g):
            g.wait_ge(evf.h, 2 * NFC)
            spl.inc(g.dma_start(out=fm_dram(x1_spill.ap()), in_=fm_sbuf_to_dram(x1a[:, :], NFC)), 16)



        sc_l0.close()
        wxbc_in = b.din("w_xbc", [D, CONV_DIM], BF16)
        wdt_in = b.din("w_dt", [D, 64], BF16)
        wz_in = b.din("w_z", [D, D_INNER], BF16)
        wout_in = b.din("w_out", [D_INNER, D], BF16)
        convw_in = b.din("conv_w", [128, NCC * 4], F32)
        convb_in = b.din("conv_b", [128, NCC], F32)
        dtb_in = b.din("dt_bias", [64, 1], F32)
        alog_in = b.din("A_log", [64, 1], F32)
        drow_in = b.din("D_row", [1, 64], F32)
        nw_in = b.din("nw_row", [1, D_INNER], F32)
        tril_in = b.din("tril", [128, 128], F32)
        mpref_in = b.din("m_pref", [128, 7], F32)
        whalo_in = b.din("w_halo", [128, 8], F32)


        halo_in_d = b.dram("halo_in", [128, NCC * 3], BF16)
        halo_out_d = b.dram("halo_out", [NCORES * 128, NCC * 3], BF16)
        carry_in_d = b.dram("carry_in", [128, D_INNER + 64], F32)
        carry_out_d = b.dram("carry_out", [NCORES * 128, D_INNER + 64], F32)
        y_scr_d = b.dram("y_scr", [128, NCHUNK * D_INNER], BF16)

        ld = b.sem("ld")
        gl = b.sem("gl")
        st = b.sem("st")
        cc = b.sem("cc")

        ones64 = b.sbuf("ones64", [64, 128], F32)
        i64 = b.sbuf("i64", [64, 64], F32)
        tril_s = b.sbuf("tril_s", [128, 128], F32)
        mpref_s = b.sbuf("mpref_s", [128, 7], F32)
        whalo_s = b.sbuf("whalo_s", [128, 8], F32)
        dtb_s = b.sbuf("dtb_s", [64, 1], F32)
        alog_s = b.sbuf("alog_s", [64, 1], F32)
        drow_s = b.sbuf("drow_s", [1, 64], F32)
        cset = b.sem("cset")

        @block.vector
        def _(vector):
            vector.wait_ge(spl.h, 16)
            vector.memset(ones64[:, :], 1.0)
            vector.memset(cst[:, 2:3], -1.0)
            cset.inc(vector.memset(cst[:, 3:4], 1.0))

        @block.sync
        def _(sync):
            sync.wait_ge(spl.h, 16)
            ld.inc(sync.dma_start(out=tril_s[:, :], in_=tril_in), 16)
            ld.inc(sync.dma_start(out=mpref_s[:, :], in_=mpref_in), 16)
            ld.inc(sync.dma_start(out=whalo_s[:, :], in_=whalo_in), 16)
            ld.inc(sync.dma_start(out=dtb_s[:, :], in_=dtb_in), 16)
            ld.inc(sync.dma_start(out=alog_s[:, :], in_=alog_in), 16)
            ld.inc(sync.dma_start(out=drow_s[:, :], in_=drow_in), 16)
            ld.inc(sync.dma_start(out=i64[:, :], in_=ident_in[0:64, 0:64]), 16)
        const_ld = ld.n

        xn1 = xn
        sdect_all = b.sbuf("sdect_all", [128, NCHUNK * 64], F32)
        dbc_all = b.sbuf("dbc_all", [128, NCHUNK * 64], F32)
        c_keep = b.sbuf("c_keep", [128, TC], BF16)
        sc_body = ExitStack()
        xbc = b.sbuf("xbc", [128, NCC * XBW], BF16, es=sc_body)
        convw_s = b.sbuf("convw_s", [128, NCC * 4], F32, es=sc_body)
        convb_s = b.sbuf("convb_s", [128, NCC], F32, es=sc_body)
        dt_raw = b.sbuf("dt_raw", [64, TC], F32, es=sc_body)
        dt_s = b.sbuf("dt_s", [64, TC], F32, es=sc_body)
        dsc = b.sbuf("dsc", [64, TC], F32, es=sc_body)
        a_cs = b.sbuf("a_cs", [64, TC], F32, es=sc_body)
        dec_s = b.sbuf("dec_s", [64, TC], F32, es=sc_body)
        aexp_s = b.sbuf("aexp_s", [64, 2], F32, es=sc_body)
        acst_all = b.sbuf("acst_all", [128, NCHUNK * 64], F32, es=sc_body)
        dtt_all = b.sbuf("dtt_all", [128, NCHUNK * 64], F32, es=sc_body)
        dect_all = b.sbuf("dect_all", [128, NCHUNK * 64], F32, es=sc_body)
        prod_d = b.sbuf("prod_d", [128, 64], F32, es=sc_body)
        rowsc = b.sbuf("rowsc", [1, 64], F32, es=sc_body)
        s_run = b.sbuf("s_run", [128, D_INNER], F32, es=sc_body)
        zsc = b.sbuf("zsc", [64, 128], F32, es=sc_body)
        d_b = b.sbuf("d_b", [128, 64], F32, es=sc_body)

        @block.sync
        def _(sync):
            ld.inc(sync.dma_start(out=convw_s[:, :], in_=convw_in), 16)
            ld.inc(sync.dma_start(out=convb_s[:, :], in_=convb_in), 16)
        xn1_ld = ld.n

        sc_pa = ExitStack()
        wx_buf = b.sbuf("wx_buf", [128, 2 * NFC * 128], BF16, es=sc_pa)
        sc_ip = ExitStack()
        ip_ps = [b.psum("ip_ps0", [128, 512], F32, es=sc_ip), b.psum("ip_ps1", [128, 512], F32, es=sc_ip)]
        lwx = b.sem("lwx")
        mip = b.sem("mip")
        eip = b.sem("eip")

        @block.sync
        def _(sync):
            for oc in range(NCC + 1):
                if oc >= 2:
                    sync.wait_ge(mip.h, 2 * (oc - 1))
                src = wxbc_in[:, oc * 128:(oc + 1) * 128] if oc < NCC else wdt_in
                w = 128 if oc < NCC else 64
                lwx.inc(sync.dma_start(
                    out=wx_buf[:, (oc % 2) * NFC * 128:(oc % 2) * NFC * 128 + NFC * w].rearrange("p (c d) -> p c d", c=NFC),
                    in_=src.rearrange("(c p) d -> p c d", p=128)), 16)

        @block.tensor
        def _(tensor):
            tensor.wait_ge(ld.h, xn1_ld)
            for oc in range(NCC + 1):
                tensor.wait_ge(lwx.h, 16 * (oc + 1))
                w = 128 if oc < NCC else 64
                for th in range(2):
                    k = 2 * oc + th
                    if k >= 2:
                        tensor.wait_ge(eip.h, k - 1)
                    out_ap = ip_ps[k % 2][:, :] if w == 128 else ip_ps[k % 2][0:64, :]
                    for fc in range(NFC):
                        mm = tensor.matmul(out_ap,
                                           wx_buf[:, (oc % 2) * NFC * 128 + fc * w:(oc % 2) * NFC * 128 + fc * w + w],
                                           xn1[:, fc * TC + th * 512:fc * TC + th * 512 + 512],
                                           start=(fc == 0), stop=(fc == NFC - 1))
                    mip.inc(mm)

        @block.vector
        def _(vector):
            for oc in range(NCC + 1):
                for th in range(2):
                    k = 2 * oc + th
                    vector.wait_ge(mip.h, k + 1)
                    if oc < NCC:
                        dst = xbc[:, oc * XBW + 3 + th * 512:oc * XBW + 3 + th * 512 + 512]
                        eip.inc(vector.tensor_copy(dst, ip_ps[k % 2][:, :]))
                    else:
                        eip.inc(vector.tensor_copy(dt_raw[:, th * 512:(th + 1) * 512],
                                                   ip_ps[k % 2][0:64, :]))
        ip_done = eip.n

        halo_sb = b.sbuf("halo_sb", [128, 8 * NCC * 3], BF16, es=sc_pa)
        halo_acc = b.sbuf("halo_acc", [128, NCC * 3], F32, es=sc_pa)
        hcmb = b.sem("hcmb")

        @block.gpsimd
        def _(g):
            g.wait_ge(eip.h, ip_done)
            gl.inc(g.dma_start(
                out=halo_in_d.ap().rearrange("p (c t) -> p c t", c=NCC),
                in_=xbc[:, :].rearrange("p (c t) -> p c t", c=NCC)[:, :, 1024:1027]), 16)
            g.wait_ge(gl.h, gl.n)
            cc.inc(g.collective_compute(
                "AllGather", ALU.bypass, replica_groups=[list(range(NCORES))],
                ins=[halo_in_d.ap().opt()], outs=[halo_out_d.ap().opt()]))
            g.wait_ge(cc.h, 1)
            gl.inc(g.dma_start(
                out=halo_sb[:, :].rearrange("p (j t) -> p j t", j=8),
                in_=halo_out_d.ap().rearrange("(j p) t -> p j t", p=128)), 16)
        halo_gl = gl.n

        @block.vector
        def _(vector):
            vector.wait_ge(gl.h, halo_gl)
            vector.wait_ge(cset.h, 1)
            vector.memset(halo_acc[:, :], 0.0)
            for j in range(8):
                hs = halo_sb[:, j * NCC * 3:(j + 1) * NCC * 3]
                inst = vector.scalar_tensor_tensor(halo_acc[:, :], hs, whalo_s[:, j:j + 1],
                                                   halo_acc[:, :], ALU.mult, ALU.add)
                hcmb.inc(inst)
                vector.wait_ge(hcmb.h, hcmb.n)
            hcmb.inc(vector.tensor_copy(
                xbc[:, :].rearrange("p (c t) -> p c t", c=NCC)[:, :, 0:3],
                halo_acc[:, :].rearrange("p (c t) -> p c t", c=NCC)))
        halo_done = hcmb.n

        cacc = b.sbuf("cacc", [128, 2 * TC], F32, es=sc_pa)
        cvt = b.sem("cvt")
        cvs = b.sem("cvs")

        @block.vector
        def _(vector):
            vector.wait_ge(hcmb.h, halo_done)
            for cci in range(NCC):
                buf = cci % 2
                acc = cacc[:, buf * TC:(buf + 1) * TC]
                base = cci * XBW
                if cci >= 2:
                    vector.wait_ge(cvs.h, cci - 1)
                vector.tensor_scalar(acc, xbc[:, base + 0:base + TC],
                                     convw_s[:, cci * 4 + 0:cci * 4 + 1],
                                     convb_s[:, cci:cci + 1], ALU.mult, ALU.add)
                for kk in range(1, 3):
                    vector.scalar_tensor_tensor(acc, xbc[:, base + kk:base + kk + TC],
                                                convw_s[:, cci * 4 + kk:cci * 4 + kk + 1],
                                                acc, ALU.mult, ALU.add)
                cvt.inc(vector.scalar_tensor_tensor(acc, xbc[:, base + 3:base + 3 + TC],
                                                    convw_s[:, cci * 4 + 3:cci * 4 + 4],
                                                    acc, ALU.mult, ALU.add))

        @block.scalar
        def _(scalar):
            for cci in range(NCC):
                buf = cci % 2
                scalar.wait_ge(cvt.h, cci + 1)
                cvs.inc(scalar.activation(xbc[:, cci * XBW + 3:cci * XBW + 3 + TC],
                                          cacc[:, buf * TC:(buf + 1) * TC], AF.Silu))
        conv_done = cvs.n

        S = Sched(b, "mb")
        AX = mybir.AxisListType.X

        @block.vector
        def _(vector):
            vector.wait_ge(spl.h, 16)
            vector.memset(zsc[:, :], 0.0)

        ec = cst[0:64, :]

        S.add("vector", lambda e: e.tensor_scalar(dsc[:, :], dt_raw[:, :], dtb_s[:, :], None, ALU.add),
              waits=[(eip.h, ip_done), (cset.h, 1)])
        S.add("scalar", lambda e: e.activation(dec_s[:, :], dsc[:, :], AF.Abs), waits=[S.up("vector")])
        S.add("scalar", lambda e: e.activation(dec_s[:, :], dec_s[:, :], AF.Exp, scale=ec[:, 2:3]))
        S.add("scalar", lambda e: e.activation(dec_s[:, :], dec_s[:, :], AF.Ln, bias=ec[:, 3:4]))
        S.add("vector", lambda e: e.tensor_scalar(dt_s[:, :], dsc[:, :], 0.0, None, ALU.max))
        S.add("vector", lambda e: e.tensor_tensor(dt_s[:, :], dt_s[:, :], dec_s[:, :], ALU.add),
              waits=[S.up("scalar")])
        S.add("scalar", lambda e: e.activation(aexp_s[:, 0:1], alog_s[:, :], AF.Exp))
        S.add("vector", lambda e: e.tensor_scalar(dsc[:, :], dt_s[:, :], aexp_s[:, 0:1], ec[:, 2:3], ALU.mult, ALU.mult),
              waits=[S.up("scalar")])
        for ci in range(NCHUNK):
            sl = slice(ci * MCH, (ci + 1) * MCH)
            S.add("vector", lambda e, sl=sl: e.tensor_tensor_scan(a_cs[:, sl], dsc[:, sl], zsc[:, :], 0.0, ALU.add, ALU.add))
        for ci in range(NCHUNK):
            sl = slice(ci * MCH, (ci + 1) * MCH)
            lc = a_cs[:, ci * MCH + MCH - 1:ci * MCH + MCH]
            S.add("vector", lambda e, sl=sl, lc=lc: e.tensor_scalar(dec_s[:, sl], a_cs[:, sl], lc, ec[:, 2:3], ALU.subtract, ALU.mult))
        dec_ts_done = S.up("vector")
        for ci in range(NCHUNK):
            sl = slice(ci * MCH, (ci + 1) * MCH)
            S.add("scalar", lambda e, sl=sl: e.activation(dec_s[:, sl], dec_s[:, sl], AF.Exp),
                  waits=[dec_ts_done])

        sc_ip.close()
        sc_pa.close()
        tp_f = b.psum("tp_f", [128, 128], F32)
        tp_b2 = b.psum("tp_b2", [128, 256], BF16)
        tp_c = b.psum("tp_c", [128, 128], BF16)
        bcA = b.psum("bcA", [128, 128], F32)
        sh4 = b.psum("sh4", [128, 2048], F32)

        S.add("tensor", lambda e: e.matmul(tp_f[:, 0:64], ones_row[:, :], drow_s[:, :], start=True, stop=True),
              waits=[(ld.h, const_ld), (cset.h, 1)])
        S.add("vector", lambda e: e.tensor_copy(d_b[:, :], tp_f[:, 0:64]), waits=[S.up("tensor")])
        S.add("vector", lambda e: e.memset(prod_d[:, :], 1.0))

        id64 = ident_f[0:64, 0:64]
        for ci in range(NCHUNK):
            sl = slice(ci * MCH, (ci + 1) * MCH)
            s64 = slice(ci * 64, (ci + 1) * 64)
            S.add("tensor", lambda e, sl=sl: e.transpose(tp_f[:, 0:64], a_cs[:, sl], id64),
                  waits=[S.up("vector"), S.up("scalar")])
            S.add("vector", lambda e, s64=s64: e.tensor_copy(acst_all[:, s64], tp_f[:, 0:64]),
                  waits=[S.up("tensor")])
            S.add("scalar", lambda e, s64=s64: e.activation(sdect_all[:, s64], acst_all[:, s64], AF.Exp),
                  waits=[S.up("vector")])
            S.add("tensor", lambda e, sl=sl: e.transpose(tp_f[:, 0:64], dt_s[:, sl], id64),
                  waits=[S.up("vector")])
            S.add("vector", lambda e, s64=s64: e.tensor_copy(dtt_all[:, s64], tp_f[:, 0:64]),
                  waits=[S.up("tensor")])
            S.add("tensor", lambda e, sl=sl: e.transpose(tp_f[:, 0:64], dec_s[:, sl], id64),
                  waits=[S.up("vector")])
            S.add("vector", lambda e, s64=s64: e.tensor_copy(dect_all[:, s64], tp_f[:, 0:64]),
                  waits=[S.up("tensor")])
            S.add("scalar", lambda e, ci=ci: e.activation(aexp_s[:, 1:2],
                  a_cs[:, ci * MCH + MCH - 1:ci * MCH + MCH], AF.Exp), waits=[S.up("vector")])
            S.add("tensor", lambda e: e.transpose(tp_f[0:1, 0:64], aexp_s[:, 1:2], id64),
                  waits=[S.up("scalar"), S.up("vector")])
            S.add("vector", lambda e: e.tensor_copy(rowsc[:, :], tp_f[0:1, 0:64]),
                  waits=[S.up("tensor")])
            S.add("tensor", lambda e: e.matmul(tp_f[:, 0:64], ones_row[:, :], rowsc[:, :], start=True, stop=True),
                  waits=[S.up("vector")])
            S.add("vector", lambda e, s64=s64: e.tensor_copy(dbc_all[:, s64], tp_f[:, 0:64]),
                  waits=[S.up("tensor")])
            S.add("vector", lambda e, s64=s64: e.tensor_tensor(prod_d[:, :], prod_d[:, :], dbc_all[:, s64], ALU.mult),
                  waits=[S.up("vector")])

        x_tm = b.sbuf("x_tm", [128, 2048], BF16, es=sc_body)
        xdt_h = b.sbuf("xdt_h", [128, 2048], BF16, es=sc_body)
        xdtd_h = b.sbuf("xdtd_h", [128, 2048], BF16, es=sc_body)
        m_all = b.sbuf("m_all", [128, 32 * 128], BF16, es=sc_body)
        wide = b.sbuf("wide", [128, 2048], F32, es=sc_body)
        g_tm = b.sbuf("g_tm", [128, 128], F32, es=sc_body)
        b_tm = b.sbuf("b_tm", [128, 128], BF16, es=sc_body)
        y_sc = b.sbuf("y_sc", [128, 2048], F32, es=sc_body)
        y_sc2 = b.sbuf("y_sc2", [128, 2048], F32, es=sc_body)
        s_bf = b.sbuf("s_bf", [128, 2048], BF16, es=sc_body)
        y_stage = b.sbuf("y_stage", [128, 2048], BF16, es=sc_body)

        def exp_ap(t, col0, nh):
            pstep = t[:, :].ap[0][0]
            return bass.AP(t, col0, [[pstep, 128], [1, nh], [0, 64]])

        S.add("vector", lambda e: e.memset(s_run[:, :], 0.0))

        conv_w_ = (cvs.h, conv_done)
        tp_toks = [None, None]
        cp_toks = [None, None]
        for ci in range(NCHUNK):
            s64c = ci * 64
            btile = xbc[:, 32 * XBW + 3 + ci * MCH:32 * XBW + 3 + ci * MCH + MCH]
            ctile = xbc[:, 33 * XBW + 3 + ci * MCH:33 * XBW + 3 + ci * MCH + MCH]
            tp_toks[0] = S.add("tensor", lambda e, btile=btile: e.transpose(tp_b2[:, 0:128], btile, ident_b[:, :]),
                  waits=[conv_w_] + ([cp_toks[0]] if cp_toks[0] else [S.up("vector")]))
            cp_toks[0] = S.add("vector", lambda e: e.tensor_copy(b_tm[:, :], tp_b2[:, 0:128]), waits=[tp_toks[0]])
            S.add("tensor", lambda e, btile=btile, ctile=ctile: e.matmul(bcA[:, :], btile, ctile, start=True, stop=True),
                  waits=[S.up("vector")])
            S.add("vector", lambda e: e.tensor_tensor(g_tm[:, :], bcA[:, :], tril_s[:, :], ALU.mult),
                  waits=[S.up("tensor")])
            for hf in range(2):
                h0 = 32 * hf

                for fc in range(NFC):
                    cci = hf * NFC + fc
                    xt_ = xbc[:, cci * XBW + 3 + ci * MCH:cci * XBW + 3 + ci * MCH + MCH]
                    pbf = fc % 2
                    dst = tp_b2[:, 0:128] if pbf == 0 else tp_c[:, :]
                    tp_toks[pbf] = S.add("tensor", lambda e, xt_=xt_, dst=dst: e.transpose(dst, xt_, ident_b[:, :]),
                          waits=([cp_toks[pbf]] if cp_toks[pbf] else [S.up("vector")]))
                    cp_toks[pbf] = S.add("vector", lambda e, fc=fc, dst=dst: e.tensor_copy(
                        x_tm[:, fc * 128:(fc + 1) * 128], dst),
                        waits=[tp_toks[pbf]])
                S.add("vector", lambda e, s64c=s64c, h0=h0: e.tensor_tensor(
                    xdt_h[:, :], x_tm[:, :], exp_ap(dtt_all, s64c + h0, 32), ALU.mult))
                S.add("vector", lambda e, s64c=s64c, h0=h0: e.tensor_tensor(
                    xdtd_h[:, :], xdt_h[:, :], exp_ap(dect_all, s64c + h0, 32), ALU.mult))
                if ci >= 1:
                    S.add("vector", lambda e, hf=hf: e.tensor_copy(s_bf[:, :], s_run[:, hf * 2048:(hf + 1) * 2048]))
                    def yoff(e, ctile=ctile):
                        for q in range(4):
                            mm = e.matmul(sh4[:, q * 512:(q + 1) * 512], ctile, s_bf[:, q * 512:(q + 1) * 512],
                                          start=True, stop=True)
                        return mm
                    S.add("tensor", yoff, waits=[S.up("vector")])
                    S.add("vector", lambda e, s64c=s64c, h0=h0: e.tensor_tensor(
                        y_sc[:, :], sh4[:, :], exp_ap(sdect_all, s64c + h0, 32), ALU.mult),
                        waits=[S.up("tensor")])
                else:
                    S.add("vector", lambda e: e.memset(y_sc[:, :], 0.0))
                def pstep(t):
                    return t[:, :].ap[0][0]
                for sub in range(2):
                    for hl in range(16):
                        h = h0 + 16 * sub + hl
                        S.add("vector", lambda e, hl=hl, ci=ci, h=h: e.tensor_scalar(
                            wide[0:64, hl * 128:(hl + 1) * 128], a_cs[:, ci * MCH:(ci + 1) * MCH],
                            i64[:, h:h + 1], None, ALU.mult),
                            waits=([S.up("tensor"), S.up("scalar")] if hl == 0 else []))
                    def bcmm(e):
                        for q in range(4):
                            mm = e.matmul(sh4[:, q * 512:(q + 1) * 512], ones64[:, :],
                                          wide[0:64, q * 512:(q + 1) * 512],
                                          start=True, stop=True)
                        return mm
                    S.add("tensor", bcmm, waits=[S.up("vector")])
                    for q in range(4):
                        S.add("vector", lambda e, q=q, col0=s64c + h0 + 16 * sub + 4 * q: e.tensor_tensor(
                            wide[:, q * 512:(q + 1) * 512], sh4[:, q * 512:(q + 1) * 512],
                            bass.AP(acst_all, col0, [[pstep(acst_all), 128], [1, 4], [0, MCH]]),
                            ALU.subtract),
                            waits=([S.up("tensor")] if q == 0 else []))
                    S.add("vector", lambda e: e.tensor_scalar(
                        wide[:, :], wide[:, :], 0.0, None, ALU.min))
                    S.add("scalar", lambda e: e.activation(y_sc2[:, :], wide[:, :], AF.Exp),
                        waits=[S.up("vector")])
                    for hl in range(16):
                        S.add("vector", lambda e, hp=sub * 16 + hl, hl=hl: e.tensor_tensor(
                            m_all[:, hp * 128:(hp + 1) * 128], y_sc2[:, hl * 128:(hl + 1) * 128],
                            g_tm[:, :], ALU.mult),
                            waits=([S.up("scalar")] if hl == 0 else []))
                def ydiag(e):
                    for hp in range(32):
                        mm = e.matmul(sh4[:, hp * 64:(hp + 1) * 64], m_all[:, hp * 128:(hp + 1) * 128],
                                      xdt_h[:, hp * 64:(hp + 1) * 64], start=True, stop=True)
                    return mm
                S.add("tensor", ydiag, waits=[S.up("vector")])
                S.add("vector", lambda e, h0=h0: e.tensor_tensor(
                    y_sc2[:, :], x_tm[:, :], exp_ap(d_b, h0, 32), ALU.mult))
                S.add("vector", lambda e: e.tensor_tensor(y_sc[:, :], y_sc[:, :], y_sc2[:, :], ALU.add))
                S.add("vector", lambda e: e.tensor_tensor(
                    y_stage[:, :], y_sc[:, :], sh4[:, :], ALU.add),
                    waits=[S.up("tensor"), S.up("gpsimd")])
                S.add("gpsimd", lambda e, ci=ci, hf=hf: e.dma_start(
                    out=y_scr_d.ap()[:, ci * D_INNER + hf * 2048:ci * D_INNER + (hf + 1) * 2048],
                    in_=y_stage[:, :]),
                    waits=[S.up("vector")], amt=16)
                def stmm(e):
                    for q in range(4):
                        mm = e.matmul(sh4[:, q * 512:(q + 1) * 512], b_tm[:, :],
                                      xdtd_h[:, q * 512:(q + 1) * 512], start=True, stop=True)
                    return mm
                S.add("tensor", stmm, waits=[S.up("vector")])
                S.add("vector", lambda e, hf=hf, col=s64c + h0: e.tensor_tensor(
                    y_sc2[:, :], s_run[:, hf * 2048:(hf + 1) * 2048],
                    exp_ap(dbc_all, col, 32), ALU.mult),
                    waits=[S.up("tensor")])
                S.add("vector", lambda e, hf=hf: e.tensor_tensor(
                    s_run[:, hf * 2048:(hf + 1) * 2048], y_sc2[:, :], sh4[:, :], ALU.add))

        S.add("vector", lambda e: e.tensor_copy(
            c_keep[:, :], bass.AP(xbc, 33 * XBW + 3, [[xbc[:, :].ap[0][0], 128], [1, TC]])))

        S.add("gpsimd", lambda e: e.dma_start(out=carry_in_d.ap()[:, 0:D_INNER], in_=s_run[:, :]),
              waits=[S.up("vector")], amt=16)
        S.add("gpsimd", lambda e: e.dma_start(out=carry_in_d.ap()[:, D_INNER:D_INNER + 64], in_=prod_d[:, :]),
              amt=16)
        S.add("coll", lambda e: e.collective_compute(
            "AllGather", ALU.bypass, replica_groups=[list(range(NCORES))],
            ins=[carry_in_d.ap().opt()], outs=[carry_out_d.ap().opt()]),
            waits=[S.up("gpsimd")])
        S.emit(block)
        pb_g = (S.sems["gpsimd"].h, S.sems["gpsimd"].n)
        pb_c = (S.sems["coll"].h, S.sems["coll"].n)
        pb_v = (S.sems["vector"].h, S.sems["vector"].n)
        sc_body.close()

        P = Sched(b, "pc")
        sc_pc = ExitStack()
        y_sb = b.sbuf("y_sb", [128, NCHUNK * D_INNER], BF16, es=sc_pc)
        nw_b = b.sbuf("nw_b", [128, D_INNER], F32, es=sc_pc)
        ysc_pc = b.sbuf("ysc_pc", [128, 2048], F32, es=sc_pc)
        sumsq = b.sbuf("sumsq", [128, 8], F32, es=sc_pc)
        rstd_a = b.sbuf("rstd_a", [128, 8], F32, es=sc_pc)
        dsc2 = b.sbuf("dsc2", [128, 64], F32, es=sc_pc)
        zs_s = b.sbuf("zs_s", [128, 2 * 512], F32, es=sc_pc)
        sq_scr = b.sbuf("sq_scr", [128, 512], BF16, es=sc_pc)
        sqtmp = b.sbuf("sqtmp", [128, 2], F32, es=sc_pc)
        ostage = b.sbuf("ostage", [128, 2 * 512], F32, es=sc_pc)
        spsl = b.sbuf("spsl", [128, 2 * 512], F32, es=sc_pc)
        nwst = b.sbuf("nwst", [1, 512], F32, es=sc_pc)

        if _DBG_SPILL:
            P.add("gpsimd", lambda e: e.dma_start(out=dbg_y, in_=y_scr_d.ap()),
                  waits=[pb_g, pb_c], amt=16)
        P.add("gpsimd", lambda e: e.dma_start(out=y_sb[:, :], in_=y_scr_d.ap()),
              waits=[pb_g, pb_c], amt=16)
        for zg in range(8):
            P.add("gpsimd", lambda e, zg=zg: e.dma_start(out=nwst[:, :], in_=nw_in[0:1, zg * 512:(zg + 1) * 512]),
                  waits=[P.up("tensor")], amt=16)
            P.add("tensor", lambda e, zg=zg: e.matmul(sh4[:, (zg % 4) * 512:(zg % 4) * 512 + 512],
                                                      ones_row[:, :], nwst[:, :],
                                                      start=True, stop=True),
                  waits=[pb_g, pb_v, P.up("vector"), P.up("gpsimd")])
            P.add("vector", lambda e, zg=zg: e.tensor_copy(nw_b[:, zg * 512:(zg + 1) * 512],
                                                           sh4[:, (zg % 4) * 512:(zg % 4) * 512 + 512]),
                  waits=[P.up("tensor")])
        P.add("vector", lambda e: e.memset(sumsq[:, :], 0.0))
        P.add("vector", lambda e: e.memset(t_buf[:, :], 0.0))

        sc_mid = ExitStack()
        t_buf = b.sbuf("t_buf", [128, D_INNER], F32, es=sc_mid)
        t_bf = b.sbuf("t_bf", [128, D_INNER], BF16, es=sc_mid)
        sc_slab = ExitStack()
        slab = b.sbuf("slab", [128, D_INNER + 64], F32, es=sc_slab)
        for j in range(7):
            P.add("gpsimd", lambda e, j=j: e.dma_start(
                out=slab[:, :], in_=carry_out_d.ap()[j * 128:(j + 1) * 128, :]),
                waits=[P.up("vector")], amt=16)
            P.add("vector", lambda e: e.tensor_scalar(dsc2[:, :], slab[:, D_INNER:D_INNER + 64], 1.0, None, ALU.subtract),
                  waits=[P.up("gpsimd")])
            P.add("vector", lambda e, j=j: e.tensor_scalar(dsc2[:, :], dsc2[:, :], mpref_s[:, j:j + 1],
                                                           cst[:, 3:4], ALU.mult, ALU.add),
                  waits=[P.up("vector")])
            P.add("vector", lambda e: e.tensor_tensor(t_buf[:, :], t_buf[:, :], exp_ap(dsc2, 0, 64), ALU.mult),
                  waits=[P.up("vector")])
            P.add("vector", lambda e, j=j: e.scalar_tensor_tensor(t_buf[:, :], slab[:, 0:D_INNER],
                                                                  mpref_s[:, j:j + 1], t_buf[:, :],
                                                                  ALU.mult, ALU.add))
        sc_slab.close()

        for ci in range(NCHUNK):
            P.add("vector", lambda e: e.tensor_copy(t_bf[:, :], t_buf[:, :]),
                  waits=[P.up("tensor"), P.up("gpsimd"), P.up("scalar")])
            for hf in range(2):
                def ycar(e, ci=ci, hf=hf):
                    ct = c_keep[:, ci * MCH:(ci + 1) * MCH]
                    for q in range(4):
                        mm = e.matmul(sh4[:, q * 512:(q + 1) * 512], ct,
                                      t_bf[:, hf * 2048 + q * 512:hf * 2048 + (q + 1) * 512],
                                      start=True, stop=True)
                    return mm
                P.add("tensor", ycar, waits=[P.up("vector"), P.up("gpsimd"), P.up("scalar")])
                P.add("vector", lambda e, ci=ci, hf=hf: e.tensor_tensor(
                    ysc_pc[:, :], sh4[:, :], exp_ap(sdect_all, ci * 64 + 32 * hf, 32), ALU.mult),
                    waits=[P.up("tensor")])
                ysl = slice(ci * D_INNER + hf * 2048, ci * D_INNER + (hf + 1) * 2048)
                P.add("vector", lambda e, ysl=ysl: e.tensor_tensor(
                    y_sb[:, ysl], y_sb[:, ysl], ysc_pc[:, :], ALU.add))
            P.add("vector", lambda e, ci=ci: e.tensor_tensor(
                t_buf[:, :], t_buf[:, :], exp_ap(dbc_all, ci * 64, 64), ALU.mult))

        sc_mid.close()
        sc_wz = ExitStack()
        wz_buf = b.sbuf("wz_buf", [128, 2 * NFC * 512], BF16, es=sc_wz)
        kz = 0
        for zg in range(8):
            P.add("gpsimd", lambda e, zg=zg: e.dma_start(
                out=wz_buf[:, (zg % 2) * NFC * 512:(zg % 2 + 1) * NFC * 512].rearrange("p (c d) -> p c d", c=NFC),
                in_=wz_in[:, zg * 512:(zg + 1) * 512].rearrange("(c p) d -> p c d", p=128)),
                waits=[P.up("tensor"), P.up("vector")], amt=16)
            wz_tok = P.up("gpsimd")
            for ch in range(NCHUNK):
                def zmm(e, zg=zg, ch=ch, kz=kz):
                    for fc in range(NFC):
                        mm = e.matmul(sh4[:, (kz % 2) * 512:(kz % 2) * 512 + 512],
                                      xn1[:, fc * TC + ch * 128:fc * TC + (ch + 1) * 128],
                                      wz_buf[:, ((zg % 2) * NFC + fc) * 512:((zg % 2) * NFC + fc + 1) * 512],
                                      start=(fc == 0), stop=(fc == NFC - 1))
                    return mm
                P.add("tensor", zmm, waits=[wz_tok, P.up("scalar"), P.up("vector"), P.up("gpsimd")])
                P.add("scalar", lambda e, kz=kz: e.activation(
                    zs_s[:, (kz % 2) * 512:(kz % 2) * 512 + 512],
                    sh4[:, (kz % 2) * 512:(kz % 2) * 512 + 512], AF.Silu),
                    waits=[P.up("tensor"), P.up("vector")])
                ysl = slice(ch * D_INNER + zg * 512, ch * D_INNER + (zg + 1) * 512)
                P.add("vector", lambda e, ysl=ysl, kz=kz: e.tensor_tensor(
                    y_sb[:, ysl], y_sb[:, ysl], zs_s[:, (kz % 2) * 512:(kz % 2) * 512 + 512], ALU.mult),
                    waits=[P.up("scalar")])
                P.add("scalar", lambda e, ysl=ysl: e.activation(
                    sq_scr[:, :], y_sb[:, ysl], AF.Square, accum_out=sqtmp[:, 0:1]),
                    waits=[P.up("vector")])
                P.add("vector", lambda e, ch=ch: e.tensor_tensor(
                    sumsq[:, ch:ch + 1], sumsq[:, ch:ch + 1], sqtmp[:, 0:1], ALU.add),
                    waits=[P.up("scalar"), P.up("vector")])
                kz += 1
        sc_wz.close()

        for ch in range(NCHUNK):
            P.add("vector", lambda e, ch=ch: e.tensor_scalar(rstd_a[:, ch:ch + 1], sumsq[:, ch:ch + 1],
                                                             1.0 / D_INNER, None, ALU.mult),
                  waits=[P.up("vector")])
            P.add("scalar", lambda e, ch=ch: e.activation(rstd_a[:, ch:ch + 1], rstd_a[:, ch:ch + 1],
                                                          AF.Sqrt, bias=cst[:, 0:1]),
                  waits=[P.up("vector")])
            P.add("vector", lambda e, ch=ch: e.reciprocal(rstd_a[:, ch:ch + 1], rstd_a[:, ch:ch + 1]),
                  waits=[P.up("scalar")])
            ysl = slice(ch * D_INNER, (ch + 1) * D_INNER)
            P.add("vector", lambda e, ysl=ysl, ch=ch: e.tensor_scalar(
                y_sb[:, ysl], y_sb[:, ysl], rstd_a[:, ch:ch + 1], None, ALU.mult),
                waits=[P.up("vector"), P.up("scalar")])
            P.add("vector", lambda e, ysl=ysl: e.tensor_tensor(
                y_sb[:, ysl], y_sb[:, ysl], nw_b[:, :], ALU.mult))

        if _DBG_SPILL:
            P.add("gpsimd", lambda e: e.dma_start(out=dbg_y2, in_=y_sb[:, :]),
                  waits=[P.up("vector"), P.up("scalar")], amt=16)
        sc_fm = ExitStack()
        y_fm = b.sbuf("y_fm", [128, 32 * 512], BF16, es=sc_fm)
        wo_buf = b.sbuf("wo_buf", [128, 2 * 32 * 128], BF16, es=sc_fm)
        ko = 0
        for hf2 in range(2):
            for ch in range(4 * hf2, 4 * hf2 + 4):
                for dc in range(32):
                    ysl = y_sb[:, ch * D_INNER + dc * 128:ch * D_INNER + (dc + 1) * 128]
                    P.add("tensor", lambda e, ysl=ysl: e.transpose(tp_b2[:, 0:128], ysl, ident_b[:, :]),
                          waits=[P.up("vector")])
                    P.add("vector", lambda e, dc=dc, ch=ch, hf2=hf2: e.tensor_copy(
                        y_fm[:, dc * 512 + (ch - 4 * hf2) * 128:dc * 512 + (ch - 4 * hf2) * 128 + 128],
                        tp_b2[:, 0:128]),
                        waits=[P.up("tensor")])
            for oc in range(NFC):
                P.add("gpsimd", lambda e, oc=oc, ko=ko: e.dma_start(
                    out=wo_buf[:, (ko % 2) * 32 * 128:(ko % 2 + 1) * 32 * 128].rearrange("p (c d) -> p c d", c=32),
                    in_=wout_in[:, oc * 128:(oc + 1) * 128].rearrange("(c p) d -> p c d", p=128)),
                    waits=[P.up("tensor")], amt=16)
                wo_tok = P.up("gpsimd")
                def opmm(e, ko=ko):
                    for dc in range(32):
                        mm = e.matmul(sh4[:, (ko % 2) * 512:(ko % 2) * 512 + 512],
                                      wo_buf[:, ((ko % 2) * 32 + dc) * 128:((ko % 2) * 32 + dc + 1) * 128],
                                      y_fm[:, dc * 512:(dc + 1) * 512],
                                      start=(dc == 0), stop=(dc == 31))
                    return mm
                P.add("tensor", opmm, waits=[wo_tok, P.up("vector"), P.up("scalar"), P.up("gpsimd")])
                P.add("gpsimd", lambda e, oc=oc, hf2=hf2, ko=ko: e.dma_start(
                    out=spsl[:, (ko % 2) * 512:(ko % 2 + 1) * 512],
                    in_=x1_spill.ap()[oc * 128:(oc + 1) * 128, hf2 * 512:(hf2 + 1) * 512]),
                    waits=[P.up("vector"), (spl.h, 16)], amt=16)
                P.add("vector", lambda e, ko=ko: e.tensor_tensor(
                    ostage[:, (ko % 2) * 512:(ko % 2) * 512 + 512],
                    spsl[:, (ko % 2) * 512:(ko % 2) * 512 + 512],
                    sh4[:, (ko % 2) * 512:(ko % 2) * 512 + 512], ALU.add),
                    waits=[P.up("tensor"), P.up("gpsimd"), P.up("scalar")])
                P.add("gpsimd", lambda e, oc=oc, hf2=hf2, ko=ko: e.dma_start(
                    out=out_ap[oc * 128:(oc + 1) * 128, hf2 * 512:(hf2 + 1) * 512],
                    in_=ostage[:, (ko % 2) * 512:(ko % 2) * 512 + 512]),
                    waits=[P.up("vector")], amt=16)
                ko += 1
        sc_fm.close()
        P.emit(block)

        @block.gpsimd
        def _(g):
            g.wait_ge(P.sems["gpsimd"].h, P.sems["gpsimd"].n)
        sc_pc.close()


    return b


_BUILD_CACHE = {}


def _get(name, fn):
    if name not in _BUILD_CACHE:
        _BUILD_CACHE[name] = fn()
    return _BUILD_CACHE[name]


def kernel(**ins):
    maps = prep_host(ins)
    keys = ("x_fm", "wq", "wk", "wv", "wo", "cos_t", "sin_t", "masks", "ident", "ident_bf",
            "tok_off", "rT", "sgT", "suT", "sdT", "wgT", "wuT", "wdT", "sel8",
            "w_xbc", "w_dt", "w_z", "w_out", "conv_w", "conv_b", "dt_bias", "A_log",
            "D_row", "nw_row", "tril", "m_pref", "w_halo", "factors", "l1_bias")
    b = _get("fused", build_fused)
    res = run_bass_kernel_spmd(b.nc, [{k: m[k] for k in keys} for m in maps],
                               core_ids=list(range(NCORES)))
    out = np.empty((1, T, D), dtype=np.float32)
    for c in range(NCORES):
        out[0, c * TC:(c + 1) * TC, :] = res.results[c]["out"].T
    return out
